# version 08
# baseline (speedup 1.0000x reference)
import numpy as np
import ml_dtypes
from contextlib import ExitStack
import concourse.bass as bass
import concourse.mybir as mybir
from concourse.bass_utils import run_bass_kernel_spmd

F32 = mybir.dt.float32
BF16 = mybir.dt.bfloat16
AF = mybir.ActivationFunctionType
ALU = mybir.AluOpType
AX = mybir.AxisListType

D = 1024
NSEQ = 4096
NT_ALL = 32
NT_OWN = 16
EPS = 1e-5
NEG = -3.0e38
MAGIC = 12582912.0
ENGS = ("sync", "gpsimd", "tensor", "scalar", "vector")

PHASES = ["prep", "p0a", "p0b", "p1", "p2w", "p2", "p3w", "p3a", "p3x", "p3e1", "p3e2", "p3e3"]
DBG = None
STRIP = 0
SERIAL = False


class Sched:
    def __init__(self, counts, last_wait):
        self.ev = {e: [] for e in ENGS}
        self.C = counts
        self.lastw = last_wait

    def op(self, eng, fn, waits=(), inc=None, n_dma=1):
        if SERIAL:
            waits = list(waits) + [(s, v) for s, v in self.C.items()
                                   if s not in ("sPH",) and v > 0]
        w = []
        for s, v in waits:
            if v > 0 and self.lastw.get((eng, s), -1) < v:
                w.append((s, v))
                self.lastw[(eng, s)] = v
        dma = inc in ("sIN", "sGP")
        self.ev[eng].append((fn, w, inc, dma))
        if inc is not None:
            self.C[inc] += 16 * n_dma if dma else 1
        return self.C[inc] if inc is not None else None


def emit_block(nc, sched, sems):
    with nc.Block() as blk:
        def mk(eng_name):
            def closure(eng):
                for fn, waits, inc, dma in sched.ev[eng_name]:
                    for s, v in waits:
                        eng.wait_ge(sems[s], v)
                    r = fn(eng)
                    if inc is None:
                        continue
                    instrs = r if isinstance(r, (list, tuple)) else [r]
                    if dma:
                        for i in instrs:
                            i.then_inc(sems[inc], 16)
                    else:
                        instrs[-1].then_inc(sems[inc], 1)
            return closure
        blk.sync(mk("sync"))
        blk.gpsimd(mk("gpsimd"))
        blk.tensor(mk("tensor"))
        blk.scalar(mk("scalar"))
        blk.vector(mk("vector"))


class Builder:
    def __init__(self, phases, dbg=None):
        self.phases = phases
        self.dbg = dbg
        nc = bass.Bass("TRN2", target_bir_lowering=False)
        self.nc = nc
        self.C = {k: 0 for k in ("sIN", "sGP", "sMM", "sSC", "sVE", "sPH", "sCC", "sRD", "sGD")}
        self.lastw = {}
        self.stack = ExitStack()
        self.sems = {k: self.stack.enter_context(nc.semaphore(k)) for k in self.C}
        self.phase_idx = 0
        self.dram = {}
        self.sb = {}
        self._cms = {}
        self._seq = {}
        self._declare_dram()
        self._alloc_persistent()

    def _declare_dram(self):
        nc = self.nc
        inp = lambda name, shape, dt=F32: self.dram.__setitem__(
            name, nc.dram_tensor(name, shape, dt, kind="ExternalInput"))
        inp("x0", [NSEQ, D])
        inp("x0own", [NT_OWN * 128, D])
        inp("p1mask", [8, 128, 1024])
        inp("wqT", [D, D]); inp("wkT", [D, D]); inp("wvT", [D, D])
        inp("costab", [128, 4 * NSEQ]); inp("sintab", [128, 4 * NSEQ])
        inp("costab_own", [128, 4 * 2048]); inp("sintab_own", [128, 4 * 2048])
        inp("mxw", [D, D]); inp("mxb", [1, D])
        inp("bwq", [D, D]); inp("bwk", [D, D]); inp("bwv", [D, D])
        inp("bwo", [D, D]); inp("bwd", [D, D])
        inp("bff1", [D, 4 * D]); inp("bff2", [4 * D, D])
        inp("g1c", [128, 8]); inp("b1c", [128, 8]); inp("g2c", [128, 8]); inp("b2c", [128, 8])
        inp("bo_r", [1, D]); inp("bf1_r", [1, 4 * D]); inp("bf2_r", [1, D]); inp("bd_r", [1, D])
        inp("ident", [128, 128]); inp("ones_r", [1, 1024]); inp("ones_c", [128, 1])
        self.dram["out"] = nc.dram_tensor("out", [NT_OWN * 128, D], F32, kind="ExternalOutput")
        d = self.dbg
        scr = lambda name, shape, dt=F32, dbg_out=False: self.dram.__setitem__(
            name, nc.dram_tensor(name, shape, dt, kind="ExternalOutput" if dbg_out else "Internal"))
        scr("x1s", [NT_OWN * 128, D], F32, d in ("p1", "all"))
        scr("x2s", [NT_OWN * 128, D], F32, d in ("p2", "all"))
        scr("xxs", [NT_OWN * 128, D], F32, d in ("p3e1", "all"))
        scr("xx2s", [NT_OWN * 128, D], F32, d in ("p3e2", "all"))
        scr("qfm_s", [128, 8 * 2048], BF16, d == "p0")
        scr("qefm_s", [128, 8 * 2048], BF16, d == "p3a")
        scr("cc_in", [128, 2056], F32)
        scr("cc_out", [128, 2056], F32)
        scr("wdq_s", [128, 8 * D], BF16, d == "all")
        scr("b1row_s", [1, 4 * D], BF16, d == "all")
        if d == "all":
            scr("dbg_wff1", [128, 4 * D], BF16, True)
            scr("dbg_u", [128, 8 * D], BF16, True)
        if d == "p0":
            scr("dbg_kfm", [128, 8 * NSEQ], BF16, True)
            scr("dbg_vtm", [128, NT_ALL * D], BF16, True)
            scr("dbg_negm", [1, 256], F32, True)
            scr("dbg_tiny", [128, 16], F32, True)

            scr("dbg_kn", [128, NT_ALL], F32, True)
            scr("dbg_qn", [128, NT_OWN], F32, True)

    def _alloc_persistent(self):
        nc, st = self.nc, self.stack
        T = lambda n, s, dt: st.enter_context(nc.sbuf_tensor(n, s, dt))
        self.sb["ident"] = T("c_ident", [128, 128], BF16)
        self.sb["ones_rb"] = T("c_ones_rb", [1, 1024], BF16)
        self.sb["ones_rf"] = T("c_ones_rf", [1, 1024], F32)
        self.sb["ones_cb"] = T("c_ones_cb", [128, 1], BF16)
        self.sb["negm_row"] = T("negm_row", [1, 256], BF16)
        self.sb["sq"] = T("sq_scratch", [128, 1024], F32)
        self.sb["rsc"] = T("rope_scratch", [128, 1024], BF16)
        self.sb["tiny"] = T("tiny", [128, 80], F32)
        self.sb["epsc"] = T("epsc", [128, 1], F32)
        self.sb["nepsc"] = T("nepsc", [128, 1], F32)
        self.sb["knorm"] = T("knorm", [128, NT_ALL], F32)
        self.sb["qnorm"] = T("qnorm", [128, NT_OWN], F32)
        self.sb["ln1_r"] = T("ln1_r", [128, NT_OWN], F32)
        self.sb["ln1_b"] = T("ln1_b", [128, NT_OWN], F32)
        self.sb["ln2_r"] = T("ln2_r", [128, NT_OWN], F32)
        self.sb["ln2_b"] = T("ln2_b", [128, NT_OWN], F32)
        self.sb["rq"] = T("rq", [128, NT_OWN * 4], F32)
        self.sb["qt"] = T("qt", [128, 384], F32)


    def barrier(self):
        S = self.S
        for e in ENGS:
            w = [("sIN", self.C["sIN"])] if e == "sync" else (
                [("sGP", self.C["sGP"])] if e == "gpsimd" else [])
            S.op(e, lambda eng: eng.nop(), waits=w, inc="sPH", n_dma=0)
        self.phase_idx += 1

    def begin(self):
        self.S = Sched(self.C, self.lastw)
        tgt = 5 * self.phase_idx
        if tgt:
            for e in ENGS:
                self.S.lastw.pop((e, "sPH"), None)
                self.S.ev[e].append(((lambda eng: None), [("sPH", tgt)], None, False))

    def end(self):
        self.barrier()
        emit_block(self.nc, self.S, self.sems)
        self.S = None

    def alloc(self, name, shape, dt, side=None):
        self._aseq = getattr(self, "_aseq", 0) + 1
        cm = self.nc.sbuf_tensor(f"s{self._aseq}_{name}", shape, dt, side=side)
        h = cm.__enter__()
        self._cms[name] = cm
        self._seq[name] = self._seqno = getattr(self, "_seqno", 0) + 1
        self.sb[name] = h
        return h

    def free(self, *names):
        for n in sorted(names, key=lambda n: -self._seq[n]):
            self._cms.pop(n).__exit__(None, None, None)
            self.sb.pop(n)
            self._seq.pop(n)

    def ckpt(self, eng):
        if eng == "sync":
            return self.S.op("sync", lambda e: e.nop(), waits=[("sIN", self.C["sIN"])],
                             inc="sRD", n_dma=0)
        return self.S.op("gpsimd", lambda e: e.nop(), waits=[("sGP", self.C["sGP"])],
                         inc="sGD", n_dma=0)

    def dma(self, eng, out, in_, waits=()):
        return self.S.op(eng, lambda e, o=out, i=in_: e.dma_start(out=o, in_=i),
                         waits=waits, inc="sIN" if eng == "sync" else "sGP")

    def ln_stats_apply(self, xin, n_out, x_ready, par, extra_waits=(), fused2=None, save=None,
                       sum_on_scalar=False):
        S, sb = self.S, self.sb
        b = 20 * par
        t = lambda i: sb["tiny"][:, b + i:b + i + 1]
        sq = S.op("scalar", lambda e, a=xin: e.activation(sb["sq"][:, :], a[:, :], AF.Square,
                                                          accum_out=t(1)),
                  waits=[x_ready], inc="sSC")
        if sum_on_scalar:
            v1 = 0
            sq = S.op("scalar", lambda e, a=xin: e.activation(sb["sq"][:, :], a[:, :], AF.Copy,
                                                              accum_out=t(0)),
                      waits=[x_ready], inc="sSC")
        else:
            v1 = S.op("vector", lambda e, a=xin: e.reduce_sum(t(0), a[:, :], axis=AX.X),
                      waits=[x_ready], inc="sVE")
        v2 = S.op("vector", lambda e: [
            e.tensor_scalar_mul(t(2), t(0), 1.0 / D),
            e.tensor_scalar(t(3), t(1), 1.0 / D, EPS, ALU.mult, ALU.add),
        ], waits=[("sVE", v1), ("sSC", sq)], inc="sVE")
        v3 = S.op("vector", lambda e: e.tensor_mul(t(4), t(2), t(2)), waits=[("sVE", v2)], inc="sVE")
        v4 = S.op("vector", lambda e: e.scalar_tensor_tensor(t(5), t(4), -1.0, t(3),
                                                             ALU.mult, ALU.add),
                  waits=[("sVE", v3)], inc="sVE")
        s2 = S.op("scalar", lambda e: [
            e.activation(t(6), t(5), AF.Ln),
            e.activation(t(7), t(6), AF.Exp, scale=-0.5),
        ], waits=[("sVE", v4)], inc="sSC")
        if not fused2:
            v5 = S.op("vector", lambda e: e.scalar_tensor_tensor(t(8), t(2), -1.0, t(7),
                                                                 ALU.mult, ALU.mult),
                      waits=[("sSC", s2)], inc="sVE")
            def fin(e):
                r = []
                if save is not None:
                    rbuf, bbuf, scol = save
                    r += [e.activation(rbuf[:, scol:scol + 1], t(7), AF.Identity),
                          e.activation(bbuf[:, scol:scol + 1], t(8), AF.Identity)]
                if n_out is not None:
                    r.append(e.activation(n_out[:, :], xin[:, :], AF.Identity,
                                          scale=t(7), bias=t(8)))
                return r
            return S.op("scalar", fin, waits=[("sVE", v5)] + list(extra_waits), inc="sSC")
        v5 = S.op("vector", lambda e: [
            e.scalar_tensor_tensor(t(8), t(2), -1.0, t(7), ALU.mult, ALU.mult),
            e.tensor_mul(t(10), t(7), t(7)),
            e.tensor_scalar_add(t(11), t(5), -EPS),
        ], waits=[("sSC", s2)], inc="sVE")
        v8 = S.op("vector", lambda e: e.scalar_tensor_tensor(
            t(13), t(10), t(11), sb["epsc"][:, :], ALU.mult, ALU.add),
            waits=[("sVE", v5)], inc="sVE")
        s3 = S.op("scalar", lambda e: [
            e.activation(t(14), t(13), AF.Ln),
            e.activation(t(15), t(14), AF.Exp, scale=-0.5),
        ], waits=[("sVE", v8)], inc="sSC")
        v10 = S.op("vector", lambda e: e.tensor_mul(t(16), t(7), t(15)),
                   waits=[("sSC", s3)], inc="sVE")
        v11 = S.op("vector", lambda e: e.scalar_tensor_tensor(t(18), t(2), -1.0, t(16),
                                                              ALU.mult, ALU.mult),
                   waits=[("sVE", v10)], inc="sVE")
        def fin2(e):
            r = []
            if save is not None:
                rbuf, bbuf, scol = save
                r += [e.activation(rbuf[:, scol:scol + 1], t(7), AF.Identity),
                      e.activation(bbuf[:, scol:scol + 1], t(8), AF.Identity)]
            if n_out is not None:
                r.append(e.activation(n_out[:, :], xin[:, :], AF.Identity,
                                      scale=t(16), bias=t(18)))
            return r
        return S.op("scalar", fin2, waits=[("sVE", v11)] + list(extra_waits), inc="sSC")

    def rope(self, kraw, cosb, sinb, out_view_e, out_view_o, waits):
        S, sb = self.S, self.sb
        r3 = lambda ap: ap.rearrange("p (c t) -> p c t", t=128)
        return S.op("vector", lambda e, kr=kraw, cb=cosb, sn=sinb, oe=out_view_e, oo=out_view_o: [
            e.tensor_mul(r3(sb["rsc"][:, 0:512]), r3(kr[:, 0:512]), r3(cb[:, :])),
            e.tensor_mul(r3(sb["rsc"][:, 512:1024]), r3(kr[:, 512:1024]), r3(sn[:, :])),
            e.tensor_add(oe, r3(sb["rsc"][:, 0:512]), r3(sb["rsc"][:, 512:1024])),
            e.tensor_mul(r3(sb["rsc"][:, 0:512]), r3(kr[:, 512:1024]), r3(cb[:, :])),
            e.tensor_mul(r3(sb["rsc"][:, 512:1024]), r3(kr[:, 0:512]), r3(sn[:, :])),
            e.tensor_sub(oo, r3(sb["rsc"][:, 0:512]), r3(sb["rsc"][:, 512:1024])),
        ], waits=waits, inc="sVE")

    def quant(self, h, xd_bf, z, zsq, mg, h_ready, par, vq_waits=()):
        S, sb = self.S, self.sb
        b = 96 * par
        qt = sb["qt"]
        r3 = lambda ap: ap.rearrange("p (g k) -> p g k", k=32)
        v1 = S.op("vector", lambda e: e.reduce_max(qt[:, b:b + 32], r3(h[:, :]), axis=AX.X,
                                                   apply_absolute_value=True),
                  waits=[h_ready], inc="sVE")
        v2 = S.op("vector", lambda e: e.tensor_scalar(qt[:, b + 32:b + 64], qt[:, b:b + 32],
                                                      1.0 / 7.5, 1e-30, ALU.mult, ALU.max),
                  waits=[("sVE", v1)], inc="sVE")
        v3 = S.op("vector", lambda e: e.reciprocal(qt[:, b + 64:b + 96], qt[:, b + 32:b + 64]),
                  waits=[("sVE", v2)], inc="sVE")
        vz = S.op("vector", lambda e: e.tensor_mul(
            r3(z[:, :]), r3(h[:, :]), qt[:, b + 64:b + 96, None].broadcast_to([128, 32, 32])),
            waits=[("sVE", v3)], inc="sVE")
        s1 = S.op("scalar", lambda e: e.activation(zsq[:, :], z[:, :], AF.Square),
                  waits=[("sVE", vz)], inc="sSC")
        vab = S.op("vector", lambda e: [
            e.tensor_scalar(mg[:, :], zsq[:, :], 4.0, 1.0, ALU.is_ge, ALU.add),
            e.tensor_scalar(zsq[:, :], zsq[:, :], 16.0, 1.0, ALU.is_ge, ALU.add),
            e.tensor_mul(mg[:, :], mg[:, :], zsq[:, :]),
        ], waits=[("sSC", s1)], inc="sVE")
        s2 = S.op("scalar", lambda e: e.activation(mg[:, :], mg[:, :], AF.Identity,
                                                   scale=1572864.0),
                  waits=[("sVE", vab)], inc="sSC")
        vq = S.op("vector", lambda e: [
            e.tensor_add(z[:, :], z[:, :], mg[:, :]),
            e.tensor_sub(z[:, :], z[:, :], mg[:, :]),
            e.tensor_mul(r3(xd_bf[:, :]), r3(z[:, :]),
                         qt[:, b + 32:b + 64, None].broadcast_to([128, 32, 32])),
        ], waits=[("sSC", s2)] + list(vq_waits), inc="sVE")
        return vq

    def proj_mm(self, e, psum, x_fm, w_sb, bias_row, extra_rows=()):
        r = []
        for half in range(2):
            tgt = psum[:, half * 512:(half + 1) * 512]
            if bias_row is not None:
                r.append(e.matmul(tgt, self.sb["ones_rb"][0:1, 0:128],
                                  bias_row[0:1, half * 512:(half + 1) * 512],
                                  start=True, stop=False))
            for row in extra_rows:
                r.append(e.matmul(tgt, self.sb["ones_rb"][0:1, 0:128],
                                  row[0:1, half * 512:(half + 1) * 512],
                                  start=False, stop=False))
            for jc in range(8):
                first = jc == 0 and bias_row is None and not extra_rows
                r.append(e.matmul(tgt, x_fm[:, jc * 128:(jc + 1) * 128],
                                  w_sb[:, jc * D + half * 512: jc * D + (half + 1) * 512],
                                  start=first, stop=(jc == 7)))
        return r

    def tr8(self, src_bf, pT, dst_view_or_ap, waits, evac_waits=()):
        S, sb = self.S, self.sb
        t = S.op("tensor", lambda e: [
            e.transpose(pT[:, c * 128:(c + 1) * 128], src_bf[:, c * 128:(c + 1) * 128],
                        sb["ident"][:, :]) for c in range(8)],
            waits=waits, inc="sMM")
        ev = S.op("vector", lambda e: e.tensor_copy(dst_view_or_ap, pT[:, :]),
                  waits=[("sMM", t)] + list(evac_waits), inc="sVE")
        return t, ev

    def prep(self):
        nc, sb, dr, st = self.nc, self.sb, self.dram, self.stack
        self.alloc("wkT", [128, 8 * D], BF16, side="right")
        self.alloc("wvT", [128, 8 * D], BF16, side="right")
        self.begin()
        self.S.op("vector", lambda e: [
            e.memset(sb["epsc"][:, :], EPS),
            e.memset(sb["nepsc"][:, :], -EPS),
        ], inc="sVE")
        self.dma("gpsimd", sb["ident"][:, :], dr["ident"][:, :])
        self.dma("gpsimd", sb["ones_rb"][:, :], dr["ones_r"][:, :])
        self.dma("sync", sb["ones_rf"][:, :], dr["ones_r"][:, :])
        self.dma("gpsimd", sb["ones_cb"][:, :], dr["ones_c"][:, :])
        for jc in range(8):
            self.dma("gpsimd", sb["wkT"][:, jc * D:(jc + 1) * D], dr["wkT"][jc * 128:(jc + 1) * 128, :])
            self.dma("gpsimd", sb["wvT"][:, jc * D:(jc + 1) * D], dr["wvT"][jc * 128:(jc + 1) * 128, :])
        self.end()

    def p0a(self):
        nc, sb, dr, st = self.nc, self.sb, self.dram, self.stack
        self.alloc("K_fm", [128, 8 * NSEQ], BF16)
        self.alloc("V_tm", [128, NT_ALL * D], BF16)
        with ExitStack() as ph:
            T = lambda n, s, dt: ph.enter_context(nc.sbuf_tensor(n, s, dt))
            xin = [T(f"a_xin{p}", [128, D], F32) for p in range(2)]
            cosb = [T(f"a_cos{p}", [128, 512], BF16) for p in range(2)]
            sinb = [T(f"a_sin{p}", [128, 512], BF16) for p in range(2)]
            nbf = [T(f"a_nbf{p}", [128, D], BF16) for p in range(3)]
            nfm = [T(f"a_nfm{p}", [128, D], BF16) for p in range(3)]
            ktmb = [T(f"a_ktm{p}", [128, D], BF16) for p in range(2)]
            kraw = [T(f"a_kraw{p}", [128, D], BF16) for p in range(2)]
            pT = ph.enter_context(nc.psum_tensor("a_pT", [128, 1024], BF16))
            pT2 = ph.enter_context(nc.psum_tensor("a_pT2", [128, 1024], BF16))
            pp = [ph.enter_context(nc.psum_tensor(f"a_pp{i}", [128, 1024], F32)) for i in range(3)]
            self.begin()
            S = self.S
            lnapply = [0] * (NT_ALL + 2)
            ropeK = [0] * (NT_ALL + 2)
            evK = [0] * (NT_ALL + 2)
            evV = [0] * (NT_ALL + 2)
            nfm_ev = [0] * (NT_ALL + 2)
            kraw_ev = [0] * (NT_ALL + 2)
            ntr_l = [0] * (NT_ALL + 2)

            Kv = sb["K_fm"][:, :].rearrange("p (c t) -> p c t", t=NSEQ)
            def front(tt):
                p = tt % 2
                pn = tt % 3
                t0 = tt * 128
                x_in = self.dma("sync", xin[p][:, :], dr["x0"][t0:t0 + 128, :],
                                waits=[("sSC", lnapply[tt - 2])])
                cview = dr["costab"][:, :].rearrange("p (c t) -> p c t", t=NSEQ)[:, :, t0:t0 + 128]
                sview = dr["sintab"][:, :].rearrange("p (c t) -> p c t", t=NSEQ)[:, :, t0:t0 + 128]
                self.dma("gpsimd", cosb[p][:, :].rearrange("p (c t) -> p c t", t=128), cview,
                         waits=[("sVE", ropeK[tt - 2])])
                self.dma("gpsimd", sinb[p][:, :].rearrange("p (c t) -> p c t", t=128), sview)
                cos_in[tt] = self.ckpt("gpsimd")
                lnapply[tt] = self.ln_stats_apply(xin[p], nbf[pn], ("sIN", x_in), pn,
                                                  extra_waits=[("sMM", ntr_l[tt - 3])])
                ntr_l[tt] = S.op("tensor", lambda e, i=nbf[pn]: [
                    e.transpose(pT[:, c * 128:(c + 1) * 128], i[:, c * 128:(c + 1) * 128], sb["ident"][:, :])
                    for c in range(8)],
                    waits=[("sSC", lnapply[tt]), ("sVE", nfm_ev[tt - 1])], inc="sMM")
                nfm_ev[tt] = S.op("vector", lambda e, o=nfm[pn]: e.tensor_copy(o[:, :], pT[:, :]),
                                  waits=[("sMM", ntr_l[tt])], inc="sVE")

            def mmK(tt):
                pn = tt % 3
                rK = pp[(2 * tt) % 3]
                kmmK[tt] = S.op("tensor", lambda e, n=nfm[pn], rk=rK: [
                    mm for jc in range(8) for mm in (
                        e.matmul(rk[:, 0:512], n[:, jc * 128:(jc + 1) * 128],
                                 sb["wkT"][:, jc * D:jc * D + 512], start=(jc == 0), stop=(jc == 7)),
                        e.matmul(rk[:, 512:1024], n[:, jc * 128:(jc + 1) * 128],
                                 sb["wkT"][:, jc * D + 512:(jc + 1) * D], start=(jc == 0), stop=(jc == 7)),
                    )], waits=[("sVE", nfm_ev[tt]), ("sSC", evK[tt - 1])], inc="sMM")

            def mmV(tt):
                pn = tt % 3
                rV = pp[(2 * tt + 1) % 3]
                kmmV[tt] = S.op("tensor", lambda e, n=nfm[pn], rv=rV: [
                    mm for jc in range(8) for mm in (
                        e.matmul(rv[:, 0:512], n[:, jc * 128:(jc + 1) * 128],
                                 sb["wvT"][:, jc * D:jc * D + 512], start=(jc == 0), stop=(jc == 7)),
                        e.matmul(rv[:, 512:1024], n[:, jc * 128:(jc + 1) * 128],
                                 sb["wvT"][:, jc * D + 512:(jc + 1) * D], start=(jc == 0), stop=(jc == 7)),
                    )], waits=[("sVE", nfm_ev[tt]), ("sSC", evV[tt - 1])], inc="sMM")

            def backS(tt):
                if tt < 0:
                    return
                p = tt % 2
                rK = pp[(2 * tt) % 3]
                rV = pp[(2 * tt + 1) % 3]
                evK[tt] = S.op("scalar", lambda e, o=ktmb[p], rk=rK, c_=tt: [
                    e.activation(o[:, 0:512], rk[:, 0:512], AF.Copy),
                    e.activation(o[:, 512:1024], rk[:, 512:1024], AF.Copy),
                    e.activation(sb["sq"][:, :], o[:, :], AF.Square,
                                 accum_out=sb["knorm"][:, c_:c_ + 1]),
                ], waits=[("sMM", kmmK[tt])], inc="sSC")
                evV[tt] = S.op("scalar", lambda e, rv=rV, c_=tt: [
                    e.activation(sb["V_tm"][:, c_ * D:c_ * D + 512], rv[:, 0:512], AF.Copy),
                    e.activation(sb["V_tm"][:, c_ * D + 512:(c_ + 1) * D], rv[:, 512:1024], AF.Copy),
                ], waits=[("sMM", kmmV[tt])], inc="sSC")

            def backT(tt):
                if tt < 0:
                    return
                p = tt % 2
                ktr = S.op("tensor", lambda e, i=ktmb[p]: [
                    e.transpose(pT2[:, c * 128:(c + 1) * 128], i[:, c * 128:(c + 1) * 128], sb["ident"][:, :])
                    for c in range(8)],
                    waits=[("sSC", evK[tt]), ("sVE", kraw_ev[tt - 1])], inc="sMM")
                ktr_l[tt] = ktr

            def backV(tt):
                if tt < 0:
                    return
                p = tt % 2
                t0 = tt * 128
                kraw_ev[tt] = S.op("vector", lambda e, o=kraw[p]: e.tensor_copy(o[:, :], pT2[:, :]),
                                   waits=[("sMM", ktr_l[tt])], inc="sVE")
                ropeK[tt] = self.rope(kraw[p], cosb[p], sinb[p],
                                      Kv[:, 0:4, t0:t0 + 128], Kv[:, 4:8, t0:t0 + 128],
                                      waits=[("sVE", kraw_ev[tt]), ("sGD", cos_in[tt])])

            kmmK = [0] * (NT_ALL + 2)
            kmmV = [0] * (NT_ALL + 2)
            ktr_l = [0] * (NT_ALL + 2)
            cos_in = [0] * (NT_ALL + 2)
            front(0)
            for tt in range(NT_ALL):
                mmK(tt)
                backS(tt - 1)
                backT(tt - 1)
                mmV(tt)
                backV(tt - 1)
                if tt + 1 < NT_ALL:
                    front(tt + 1)
            backS(NT_ALL - 1)
            backT(NT_ALL - 1)
            backV(NT_ALL - 1)
            self.end()

    def p0b(self):
        nc, sb, dr, st = self.nc, self.sb, self.dram, self.stack
        with ExitStack() as ph:
            T = lambda n, s, dt: ph.enter_context(nc.sbuf_tensor(n, s, dt))
            xin = [T(f"b_xin{p}", [128, D], F32) for p in range(2)]
            cosb = [T(f"b_cos{p}", [128, 512], BF16) for p in range(2)]
            sinb = [T(f"b_sin{p}", [128, 512], BF16) for p in range(2)]
            nbf = [T(f"b_nbf{p}", [128, D], BF16) for p in range(2)]
            nfm = [T(f"b_nfm{p}", [128, D], BF16) for p in range(2)]
            qtmb = [T(f"b_qtm{p}", [128, D], BF16) for p in range(2)]
            qraw = [T(f"b_qraw{p}", [128, D], BF16) for p in range(2)]
            qfmt = [T(f"b_qfmt{p}", [128, D], BF16) for p in range(2)]
            pT = ph.enter_context(nc.psum_tensor("b_pT", [128, 1024], BF16))
            pp = [ph.enter_context(nc.psum_tensor(f"b_pp{i}", [128, 1024], F32)) for i in range(2)]
            self.begin()
            S = self.S
            lnapply = [0] * (NT_OWN + 2)
            ropeQ = [0] * (NT_OWN + 2)
            evQ = [0] * (NT_OWN + 2)
            nfm_ev = [0] * (NT_OWN + 2)
            qraw_ev = [0] * (NT_OWN + 2)
            spill = [0] * (NT_OWN + 2)
            ntr_l = [0] * (NT_OWN + 2)

            for s in range(NT_OWN):
                p = s % 2
                t0 = s * 128
                x_in = self.dma("sync", xin[p][:, :], dr["x0own"][t0:t0 + 128, :],
                                waits=[("sSC", lnapply[s - 2])])
                cview = dr["costab_own"][:, :].rearrange("p (c t) -> p c t", t=2048)[:, :, t0:t0 + 128]
                sview = dr["sintab_own"][:, :].rearrange("p (c t) -> p c t", t=2048)[:, :, t0:t0 + 128]
                self.dma("gpsimd", cosb[p][:, :].rearrange("p (c t) -> p c t", t=128), cview,
                         waits=[("sVE", ropeQ[s - 2])])
                self.dma("gpsimd", sinb[p][:, :].rearrange("p (c t) -> p c t", t=128), sview)
                cos_in = self.ckpt("gpsimd")
                lnapply[s] = self.ln_stats_apply(xin[p], nbf[p], ("sIN", x_in), p,
                                                 extra_waits=[("sMM", ntr_l[s - 2])])
                ntr_l[s] = ntr = S.op("tensor", lambda e, i=nbf[p]: [
                    e.transpose(pT[:, c * 128:(c + 1) * 128], i[:, c * 128:(c + 1) * 128], sb["ident"][:, :])
                    for c in range(8)],
                    waits=[("sSC", lnapply[s]), ("sVE", qraw_ev[s - 1])], inc="sMM")
                nfm_ev[s] = S.op("vector", lambda e, o=nfm[p]: e.tensor_copy(o[:, :], pT[:, :]),
                                 waits=[("sMM", ntr)], inc="sVE")
                rQ = pp[s % 2]

                def mmQ(e, n=nfm[p], rq=rQ):
                    r = []
                    for jc in range(8):
                        st_, sp = jc == 0, jc == 7
                        lhs = n[:, jc * 128:(jc + 1) * 128]
                        r.append(e.matmul(rq[:, 0:512], lhs, sb["wqT"][:, jc * D:jc * D + 512], start=st_, stop=sp))
                        r.append(e.matmul(rq[:, 512:1024], lhs, sb["wqT"][:, jc * D + 512:(jc + 1) * D], start=st_, stop=sp))
                    return r
                qmm = S.op("tensor", mmQ, waits=[("sVE", nfm_ev[s]), ("sSC", evQ[s - 2]),
                                                 ("sGP", self.wq_loaded)], inc="sMM")
                evQ[s] = S.op("scalar", lambda e, o=qtmb[p], rq=rQ, c_=s: [
                    e.activation(o[:, 0:512], rq[:, 0:512], AF.Copy),
                    e.activation(o[:, 512:1024], rq[:, 512:1024], AF.Copy),
                    e.activation(sb["sq"][:, :], o[:, :], AF.Square, accum_out=sb["qnorm"][:, c_:c_ + 1]),
                ], waits=[("sMM", qmm)], inc="sSC")
                qtr = S.op("tensor", lambda e, i=qtmb[p]: [
                    e.transpose(pT[:, c * 128:(c + 1) * 128], i[:, c * 128:(c + 1) * 128], sb["ident"][:, :])
                    for c in range(8)],
                    waits=[("sSC", evQ[s]), ("sVE", nfm_ev[s])], inc="sMM")
                qraw_ev[s] = S.op("vector", lambda e, o=qraw[p]: e.tensor_copy(o[:, :], pT[:, :]),
                                  waits=[("sMM", qtr)], inc="sVE")
                qv = qfmt[p][:, :].rearrange("p (c t) -> p c t", t=128)
                ropeQ[s] = self.rope(qraw[p], cosb[p], sinb[p], qv[:, 0:4, :], qv[:, 4:8, :],
                                     waits=[("sVE", qraw_ev[s]), ("sGD", cos_in),
                                            ("sGP", spill[s - 2])])
                qview = dr["qfm_s"][:, :].rearrange("p (c t) -> p c t", t=2048)[:, :, t0:t0 + 128]
                spill[s] = self.dma("gpsimd", qview,
                                    qfmt[p][:, :].rearrange("p (c t) -> p c t", t=128),
                                    waits=[("sVE", ropeQ[s])])
            vmax = S.op("vector", lambda e: [
                e.reduce_max(sb["tiny"][:, 0:1], sb["knorm"][:, :], axis=AX.X),
                e.reduce_max(sb["tiny"][:, 1:2], sb["qnorm"][:, :], axis=AX.X),
            ], waits=[("sSC", evQ[NT_OWN - 1])], inc="sVE")
            scast = S.op("scalar", lambda e: e.activation(sb["rsc"][:, 0:2], sb["tiny"][:, 0:2], AF.Copy),
                         waits=[("sVE", vmax)], inc="sSC")
            ttr = S.op("tensor", lambda e: e.transpose(pT[0:2, 0:128], sb["rsc"][:, 0:2], sb["ident"][:, :]),
                       waits=[("sSC", scast), ("sVE", qraw_ev[NT_OWN - 1])], inc="sMM")
            vred = S.op("vector", lambda e: e.reduce_max(sb["tiny"][0:2, 3:4], pT[0:2, 0:128], axis=AX.X),
                        waits=[("sMM", ttr)], inc="sVE")
            scast2 = S.op("scalar", lambda e: e.activation(sb["rsc"][0:2, 2:3], sb["tiny"][0:2, 3:4], AF.Copy),
                          waits=[("sVE", vred)], inc="sSC")
            ttr2 = S.op("tensor", lambda e: e.transpose(pT[0:1, 128:130], sb["rsc"][0:2, 2:3], sb["ident"][0:2, 0:2]),
                        waits=[("sSC", scast2)], inc="sMM")
            vcp = S.op("vector", lambda e: e.tensor_copy(sb["tiny"][0:1, 9:11], pT[0:1, 128:130]),
                       waits=[("sMM", ttr2)], inc="sVE")
            vprod = S.op("vector", lambda e: e.tensor_mul(
                sb["tiny"][0:1, 3:4], sb["tiny"][0:1, 9:10], sb["tiny"][0:1, 10:11]),
                waits=[("sVE", vcp)], inc="sVE")
            sneg = S.op("scalar", lambda e: [
                e.activation(sb["tiny"][0:1, 4:5], sb["tiny"][0:1, 3:4], AF.Ln),
                e.activation(sb["tiny"][0:1, 5:6], sb["tiny"][0:1, 4:5], AF.Exp, scale=0.5),
            ], waits=[("sVE", vprod)], inc="sSC")
            vneg = S.op("vector", lambda e: e.tensor_scalar_mul(
                sb["tiny"][0:1, 6:7], sb["tiny"][0:1, 5:6], -1.05), waits=[("sSC", sneg)], inc="sVE")
            S.op("scalar", lambda e: e.activation(sb["negm_row"][:, :], sb["ones_rf"][0:1, 0:256],
                                                  AF.Identity, scale=sb["tiny"][0:1, 6:7]),
                 waits=[("sVE", vneg)], inc="sSC")
            if self.dbg == "p0":
                for c in range(8):
                    self.dma("gpsimd", dr["dbg_kfm"][:, c * NSEQ:(c + 1) * NSEQ],
                             sb["K_fm"][:, c * NSEQ:(c + 1) * NSEQ],
                             waits=[("sVE", self.C["sVE"]), ("sSC", self.C["sSC"])])
                    self.dma("gpsimd", dr["dbg_vtm"][:, c * NSEQ:(c + 1) * NSEQ],
                             sb["V_tm"][:, c * NSEQ:(c + 1) * NSEQ])
                self.dma("gpsimd", dr["dbg_negm"][:, :], sb["negm_row"][:, :])
                self.dma("gpsimd", dr["dbg_tiny"][:, :], sb["tiny"][:, :])
                self.dma("gpsimd", dr["dbg_kn"][:, :], sb["knorm"][:, :])
                self.dma("gpsimd", dr["dbg_qn"][:, :], sb["qnorm"][:, :])
            self.end()

    def p1(self):
        nc, sb, dr, st = self.nc, self.sb, self.dram, self.stack
        with ExitStack() as ph:
            T = lambda n, s, dt: ph.enter_context(nc.sbuf_tensor(n, s, dt))
            qg = [T(f"c_qg{p}", [128, 2048], BF16) for p in range(2)]
            maskb = [T(f"c_mask{p}", [128, 1024], F32) for p in range(2)]
            x0t = [T(f"c_x0t{p}", [128, D], F32) for p in range(2)]
            x1t = [T(f"c_x1t{p}", [128, D], F32) for p in range(2)]
            PT = [T(f"c_PT{p}", [128, 256], BF16) for p in range(3)]
            rcp = T("c_rcp", [128, 2], F32)
            pS = [ph.enter_context(nc.psum_tensor(f"c_pS{i}", [128, 512], F32)) for i in range(3)]
            av = [ph.enter_context(nc.psum_tensor(f"c_av{i}", [128, 512], F32)) for i in range(4)]
            pden = ph.enter_context(nc.psum_tensor("c_pden", [128, 512], F32))
            self.begin()
            S = self.S
            Kv = sb["K_fm"]
            n = 0
            expd = {}
            avmm = {}
            qg_in = [0, 0]
            mk_in = [0, 0]
            qg_lastuse = [0, 0]
            mk_lastuse = [0, 0]
            x0_in = [0, 0]
            epi_v = 0
            epi_prev = 0
            x1_store = [0] * (NT_OWN + 2)
            recip_done = 0

            for g in range(8):
                gp = g % 2
                L = 4 * g + 4
                qview = dr["qfm_s"][:, :].rearrange("p (c t) -> p c t", t=2048)[:, :, g * 256:(g + 1) * 256]
                qg_in[gp] = self.dma("sync", qg[gp][:, :].rearrange("p (c t) -> p c t", t=256),
                                     qview, waits=[("sMM", qg_lastuse[gp])])
                mk_in[gp] = self.dma("sync", maskb[gp][:, :], dr["p1mask"][g, :, :],
                                     waits=[("sVE", mk_lastuse[gp])])
                for qc in range(2):
                    slot = 2 * g + qc
                    self.dma("sync", x0t[qc][:, :],
                             dr["x0own"][slot * 128:(slot + 1) * 128, :],
                             waits=[("sVE", epi_prev)])
                grp_rdy = self.ckpt("sync")
                x0_in[0] = x0_in[1] = grp_rdy
                for kt in range(L):
                    p = n % 3
                    def mmS(e, g_=g, kt_=kt, p_=p, qb=qg[gp]):
                        r = []
                        for dch in range(8):
                            r.append(e.matmul(
                                pS[p_][:, 0:256],
                                Kv[:, dch * NSEQ + kt_ * 128: dch * NSEQ + (kt_ + 1) * 128],
                                qb[:, dch * 256:(dch + 1) * 256],
                                start=(dch == 0), stop=False))
                        r.append(e.matmul(pS[p_][:, 0:256], sb["ones_rb"][0:1, 0:128],
                                          sb["negm_row"][0:1, 0:256], start=False, stop=True))
                        return r
                    smm = S.op("tensor", mmS,
                               waits=[("sRD", grp_rdy), ("sSC", expd.get(n - 3, 0))],
                               inc="sMM")
                    qg_lastuse[gp] = smm
                    m = kt - 4 * g
                    if m >= 0:
                        maskd = S.op("vector", lambda e, p_=p, m_=m, gp_=gp: e.tensor_add(
                            pS[p_][:, 0:256], pS[p_][:, 0:256], maskb[gp_][:, m_ * 256:(m_ + 1) * 256]),
                            waits=[("sMM", smm), ("sRD", grp_rdy)], inc="sVE")
                        mk_lastuse[gp] = maskd
                        ew = [("sMM", smm), ("sVE", maskd), ("sMM", avmm.get(n - 3, 0))]
                    else:
                        ew = [("sMM", smm), ("sMM", avmm.get(n - 3, 0))]
                    expd[n] = S.op("scalar", lambda e, p_=p: e.activation(
                        PT[p_][:, :], pS[p_][:, 0:256], AF.Exp, scale=1.0 / 32.0), waits=ew, inc="sSC")
                    def mmAV(e, kt_=kt, p_=p, st_=(kt == 0), sp_=(kt == L - 1)):
                        r = []
                        for qc in range(2):
                            for ec in range(2):
                                r.append(e.matmul(
                                    av[qc * 2 + ec][:, 0:512],
                                    PT[p_][:, qc * 128:(qc + 1) * 128],
                                    sb["V_tm"][:, kt_ * D + ec * 512: kt_ * D + (ec + 1) * 512],
                                    start=st_, stop=sp_))
                        for qc in range(2):
                            r.append(e.matmul(pden[:, qc:qc + 1], PT[p_][:, qc * 128:(qc + 1) * 128],
                                              sb["ones_cb"][:, :],
                                              start=(st_ and qc == 0), stop=(sp_ and qc == 1)))
                        return r
                    w = [("sSC", expd[n])]
                    if kt == 0:
                        w += [("sVE", epi_v), ("sVE", recip_done)]
                    avmm[n] = S.op("tensor", mmAV, waits=w, inc="sMM")
                    n += 1
                recip_done = S.op("vector", lambda e: e.reciprocal(rcp[:, 0:2], pden[:, 0:2]),
                                  waits=[("sMM", avmm[n - 1])], inc="sVE")
                for qc in range(2):
                    slot = 2 * g + qc
                    epi = S.op("vector", lambda e, qc_=qc: [
                        e.scalar_tensor_tensor(x1t[qc_][:, ec * 512:(ec + 1) * 512],
                                               av[qc_ * 2 + ec][:, 0:512], rcp[:, qc_:qc_ + 1],
                                               x0t[qc_][:, ec * 512:(ec + 1) * 512],
                                               ALU.mult, ALU.add)
                        for ec in range(2)],
                        waits=[("sVE", recip_done), ("sRD", x0_in[qc]),
                               ("sGP", x1_store[slot - 2])], inc="sVE")
                    x1_store[slot] = self.dma("gpsimd", dr["x1s"][slot * 128:(slot + 1) * 128, :],
                                              x1t[qc][:, :], waits=[("sVE", epi)])
                    epi_v = epi
                epi_prev = epi_v
            self.end()

    def p2w(self):
        nc, sb, dr = self.nc, self.sb, self.dram
        self.alloc("wdqT", [128, 8 * D], BF16, side="right")
        self.alloc("mxb_bf", [1, D], BF16)
        with ExitStack() as ph:
            T = lambda n, s, dt: ph.enter_context(nc.sbuf_tensor(n, s, dt))
            wbuf = [T(f"w_wbuf{p}", [128, D], F32) for p in range(2)]
            wq_bf = [T(f"w_wq{p}", [128, D], BF16) for p in range(2)]
            z = [T(f"w_z{i}", [128, D], F32) for i in range(2)]
            zsq = [T(f"w_zsq{i}", [128, D], F32) for i in range(2)]
            mg = [T(f"w_mg{i}", [128, D], F32) for i in range(2)]
            mxbf = T("w_mxbf", [1, D], F32)
            pT = ph.enter_context(nc.psum_tensor("w_pT", [128, 1024], BF16))
            self.begin()
            S = self.S
            self.dma("sync", mxbf[:, :], dr["mxb"][:, :])
            rdy0 = self.ckpt("sync")
            S.op("scalar", lambda e: e.activation(sb["mxb_bf"][:, :], mxbf[:, :], AF.Copy),
                 waits=[("sRD", rdy0)], inc="sSC")
            vq_l = [0] * (8 + 2)
            tr_l = [0] * (8 + 2)
            ev_prev = 0
            wv = sb["wdqT"][:, :].rearrange("p (c t) -> p c t", t=D)
            for wt in range(8):
                p = wt % 2
                self.dma("sync", wbuf[p][:, :], dr["mxw"][wt * 128:(wt + 1) * 128, :],
                         waits=[("sVE", vq_l[wt - 2])])
                rdy = self.ckpt("sync")
                vq_l[wt] = self.quant(wbuf[p], wq_bf[p], z[p], zsq[p], mg[p], ("sRD", rdy), p,
                                      vq_waits=[("sMM", tr_l[wt - 2])])
                tr_l[wt], ev_prev = self.tr8(wq_bf[p], pT, wv[:, :, wt * 128:(wt + 1) * 128],
                                             waits=[("sVE", vq_l[wt]), ("sVE", ev_prev)])
            self.end()

    def p2(self):
        nc, sb, dr = self.nc, self.sb, self.dram
        with ExitStack() as ph:
            T = lambda n, s, dt: ph.enter_context(nc.sbuf_tensor(n, s, dt))
            x1t = [T(f"f_x1{p}", [128, D], F32) for p in range(3)]
            ht = [T(f"f_h{p}", [128, D], F32) for p in range(3)]
            xd = [T(f"f_xd{p}", [128, D], BF16) for p in range(2)]
            xdfm = [T(f"f_xdfm{p}", [128, D], BF16) for p in range(2)]
            x2t = [T(f"f_x2{p}", [128, D], F32) for p in range(2)]
            z = [T(f"f_z{i}", [128, D], F32) for i in range(2)]
            zsq = [T(f"f_zsq{i}", [128, D], F32) for i in range(2)]
            mg = [T(f"f_mg{i}", [128, D], F32) for i in range(2)]
            pT = ph.enter_context(nc.psum_tensor("f_pT", [128, 1024], BF16))
            pp = [ph.enter_context(nc.psum_tensor(f"f_pp{i}", [128, 1024], F32)) for i in range(2)]
            self.begin()
            S = self.S
            lnap = [0] * (NT_OWN + 2)
            vq_l = [0] * (NT_OWN + 2)
            trx = [0] * (NT_OWN + 2)
            fmev = [0] * (NT_OWN + 2)
            mmv = [0] * (NT_OWN + 2)
            epi = [0] * (NT_OWN + 2)
            st_l = [0] * (NT_OWN + 2)
            def front(s):
                p3 = s % 3
                p = s % 2
                p4 = s % 4
                self.dma("sync", x1t[p3][:, :], dr["x1s"][s * 128:(s + 1) * 128, :],
                         waits=[("sVE", epi[s - 3])])
                rdy = self.ckpt("sync")
                lnap[s] = self.ln_stats_apply(x1t[p3], ht[p3], ("sRD", rdy), p4,
                                              extra_waits=[("sVE", vq_l[s - 3])],
                                              sum_on_scalar=True)
                vq_l[s] = self.quant(ht[p3], xd[p], z[p], zsq[p], mg[p], ("sSC", lnap[s]), p4,
                                     vq_waits=[("sMM", trx[s - 2])])
                trx[s], fmev[s] = self.tr8(xd[p], pT, xdfm[p][:, :],
                                           waits=[("sVE", vq_l[s]), ("sVE", fmev[s - 1])],
                                           evac_waits=[("sMM", mmv[s - 1])])
                mmv[s] = S.op("tensor", lambda e, xf=xdfm[p], rp=pp[s % 2]: self.proj_mm(
                    e, rp, xf, sb["wdqT"], sb["mxb_bf"]),
                    waits=[("sVE", fmev[s]), ("sVE", epi[s - 2])], inc="sMM")

            def back(s):
                if s < 0:
                    return
                p = s % 2
                p3 = s % 3
                epi[s] = S.op("vector", lambda e, p_=p, p3_=p3, rp=pp[s % 2]: [
                    e.tensor_add(x2t[p_][:, 0:512], rp[:, 0:512], x1t[p3_][:, 0:512]),
                    e.tensor_add(x2t[p_][:, 512:1024], rp[:, 512:1024], x1t[p3_][:, 512:1024]),
                ], waits=[("sMM", mmv[s]), ("sGP", st_l[s - 2])], inc="sVE")
                st_l[s] = self.dma("gpsimd", dr["x2s"][s * 128:(s + 1) * 128, :], x2t[p][:, :],
                                   waits=[("sVE", epi[s])])

            front(0)
            for s in range(NT_OWN):
                if s + 1 < NT_OWN:
                    front(s + 1)
                back(s)
            self.end()

    def p3w(self):
        nc, sb, dr = self.nc, self.sb, self.dram
        self.alloc("bo_bf", [1, D], BF16)
        self.alloc("bf2_bf", [1, D], BF16)
        self.alloc("bd_bf", [1, D], BF16)
        self.alloc("bq_r", [1, D], BF16)
        self.alloc("bk_r", [1, D], BF16)
        self.alloc("bv_r", [1, D], BF16)
        self.alloc("Wff1", [128, 8 * 4 * D], BF16)
        self.alloc("negmk_row", [1, D], BF16)
        self.alloc("wo_bf", [128, 8 * D], BF16, side="right")
        self.alloc("Wq_b", [128, 8 * D], BF16, side="right")
        self.alloc("Wk_b", [128, 8 * D], BF16, side="right")
        self.alloc("Wv_b", [128, 8 * D], BF16, side="right")
        self.alloc("b1row", [1, 4 * D], BF16, side="right")
        with ExitStack() as ph:
            T = lambda n, s, dt: ph.enter_context(nc.sbuf_tensor(n, s, dt))
            stage = [T(f"g_st{p}", [128, D], F32) for p in range(2)]
            raw_bf = [T(f"g_raw{p}", [128, D], BF16) for p in range(2)]
            g1col = T("g_g1", [128, 8], F32)
            g2col = T("g_g2", [128, 8], F32)
            b1bf = T("g_b1bf", [128, 8], BF16)
            b2bf = T("g_b2bf", [128, 8], BF16)
            sqb = T("g_sqb", [128, D], BF16)
            rowf = T("g_rowf", [1, D], F32)
            rowf2 = T("g_rowf2", [1, D], F32)
            self.begin()
            S = self.S
            self.dma("sync", g1col[:, :], dr["g1c"][:, :])
            self.dma("sync", g2col[:, :], dr["g2c"][:, :])
            self.dma("gpsimd", sb["bo_bf"][:, :], dr["bo_r"][:, :])
            self.dma("gpsimd", sb["bf2_bf"][:, :], dr["bf2_r"][:, :])
            self.dma("gpsimd", sb["bd_bf"][:, :], dr["bd_r"][:, :])
            self.dma("gpsimd", b1bf[:, :], dr["b1c"][:, :])
            self.dma("gpsimd", b2bf[:, :], dr["b2c"][:, :])
            bcast_rdy = self.ckpt("gpsimd")
            for jc in range(8):
                self.dma("gpsimd", sb["wo_bf"][:, jc * D:(jc + 1) * D],
                         dr["bwo"][jc * 128:(jc + 1) * 128, :])

            with ExitStack() as pA:
                pRow = [pA.enter_context(nc.psum_tensor(f"g_pR{i}", [1, 512], F32)) for i in range(2)]
                pCS = [pA.enter_context(nc.psum_tensor(f"g_pC{i}", [1, 512], F32)) for i in range(2)]

                def wprep(wname, dest, brow, colsq):
                    last_mm = 0
                    last_raw = [0, 0]
                    cs = 0
                    for jc in range(8):
                        p = jc % 2
                        self.dma("sync", stage[p][:, :], dr[wname][jc * 128:(jc + 1) * 128, :],
                                 waits=[("sSC", last_raw[p])])
                        rd = self.ckpt("sync")
                        rc = S.op("scalar", lambda e, st_=stage[p], rb=raw_bf[p]: e.activation(
                            rb[:, :], st_[:, :], AF.Copy),
                            waits=[("sRD", rd), ("sMM", last_mm)], inc="sSC")
                        last_mm = S.op("tensor", lambda e, jc_=jc, rb=raw_bf[p]: [
                            e.matmul(pRow[h][0:1, 0:512], b1bf[:, jc_:jc_ + 1],
                                     rb[:, h * 512:(h + 1) * 512],
                                     start=(jc_ == 0), stop=(jc_ == 7))
                            for h in range(2)],
                            waits=[("sSC", rc), ("sGD", bcast_rdy)], inc="sMM")
                        gsc = S.op("scalar", lambda e, st_=stage[p], jc_=jc: e.activation(
                            dest[:, jc_ * D:(jc_ + 1) * D], st_[:, :], AF.Identity,
                            scale=g1col[:, jc_:jc_ + 1]),
                            waits=[("sRD", rd)], inc="sSC")
                        last_raw[p] = gsc
                        if colsq:
                            sq2 = S.op("scalar", lambda e, jc_=jc: e.activation(
                                sqb[:, :], dest[:, jc_ * D:(jc_ + 1) * D], AF.Square),
                                waits=[("sSC", gsc)], inc="sSC")
                            cs = S.op("tensor", lambda e, jc_=jc: [
                                e.matmul(pCS[h][0:1, 0:512], sb["ones_cb"][:, :],
                                         sqb[:, h * 512:(h + 1) * 512],
                                         start=(jc_ == 0), stop=(jc_ == 7))
                                for h in range(2)],
                                waits=[("sSC", sq2)], inc="sMM")
                    ev = S.op("vector", lambda e: [
                        e.tensor_copy(brow[0:1, 0:512], pRow[0][0:1, 0:512]),
                        e.tensor_copy(brow[0:1, 512:1024], pRow[1][0:1, 0:512]),
                    ], waits=[("sMM", last_mm)], inc="sVE")
                    return ev, cs

                evq, _ = wprep("bwq", sb["Wq_b"], sb["bq_r"], False)
                evk, cs_done = wprep("bwk", sb["Wk_b"], sb["bk_r"], True)
                evv, _ = wprep("bwv", sb["Wv_b"], sb["bv_r"], False)
                vcs = S.op("vector", lambda e: [
                    e.tensor_copy(rowf2[0:1, 0:512], pCS[0][0:1, 0:512]),
                    e.tensor_copy(rowf2[0:1, 512:1024], pCS[1][0:1, 0:512]),
                ], waits=[("sMM", cs_done)], inc="sVE")
                vm1 = S.op("vector", lambda e: e.reduce_max(sb["tiny"][0:1, 0:1], rowf2[0:1, 0:1024],
                                                            axis=AX.X), waits=[("sVE", vcs)], inc="sVE")
                vm2 = S.op("vector", lambda e: e.reduce_max(sb["tiny"][0:1, 1:2], sb["bk_r"][0:1, :],
                                                            axis=AX.X, apply_absolute_value=True),
                           waits=[("sVE", evk)], inc="sVE")
                sm1 = S.op("scalar", lambda e: [
                    e.activation(sb["tiny"][0:1, 2:3], sb["tiny"][0:1, 0:1], AF.Ln),
                    e.activation(sb["tiny"][0:1, 3:4], sb["tiny"][0:1, 2:3], AF.Exp, scale=0.5),
                ], waits=[("sVE", vm1)], inc="sSC")
                vm3 = S.op("vector", lambda e: e.tensor_scalar_mul(
                    sb["tiny"][0:1, 4:5], sb["tiny"][0:1, 3:4], 32.1),
                    waits=[("sSC", sm1)], inc="sVE")
                vm4 = S.op("vector", lambda e: e.tensor_add(
                    sb["tiny"][0:1, 5:6], sb["tiny"][0:1, 4:5], sb["tiny"][0:1, 1:2]),
                    waits=[("sVE", vm3), ("sVE", vm2)], inc="sVE")
                vm5 = S.op("vector", lambda e: e.tensor_scalar_mul(
                    sb["tiny"][0:1, 6:7], sb["tiny"][0:1, 5:6], -1.02),
                    waits=[("sVE", vm4)], inc="sVE")
                S.op("scalar", lambda e: e.activation(sb["negmk_row"][:, :], sb["ones_rf"][0:1, :],
                                                      AF.Identity, scale=sb["tiny"][0:1, 6:7]),
                     waits=[("sVE", vm5)], inc="sSC")
            with ExitStack() as pB:
                pFF = [pB.enter_context(nc.psum_tensor(f"g_pF{i}", [1, 512], F32)) for i in range(8)]
                last_mm = 0
                last_raw = [0, 0]
                for jc in range(8):
                    for qtr in range(4):
                        p = (jc * 4 + qtr) % 2
                        self.dma("sync", stage[p][:, :],
                                 dr["bff1"][jc * 128:(jc + 1) * 128, qtr * D:(qtr + 1) * D],
                                 waits=[("sSC", last_raw[p])])
                        rd = self.ckpt("sync")
                        rc = S.op("scalar", lambda e, st_=stage[p], rb=raw_bf[p]: e.activation(
                            rb[:, :], st_[:, :], AF.Copy),
                            waits=[("sRD", rd), ("sMM", last_mm)], inc="sSC")
                        last_mm = S.op("tensor", lambda e, jc_=jc, q_=qtr, rb=raw_bf[p]: [
                            e.matmul(pFF[q_ * 2 + h][0:1, 0:512], b2bf[:, jc_:jc_ + 1],
                                     rb[:, h * 512:(h + 1) * 512],
                                     start=(jc_ == 0), stop=(jc_ == 7))
                            for h in range(2)],
                            waits=[("sSC", rc)], inc="sMM")
                        last_raw[p] = S.op("scalar", lambda e, st_=stage[p], jc_=jc, q_=qtr: e.activation(
                            sb["Wff1"][:, jc_ * 4 * D + q_ * D: jc_ * 4 * D + (q_ + 1) * D],
                            st_[:, :], AF.Identity, scale=g2col[:, jc_:jc_ + 1]),
                            waits=[("sRD", rd)], inc="sSC")
                prev_cast = 0
                for i in range(4):
                    self.dma("sync", rowf[0:1, :], dr["bf1_r"][0:1, i * D:(i + 1) * D],
                             waits=[("sSC", prev_cast)])
                    rdf = self.ckpt("sync")
                    vr = S.op("vector", lambda e, i_=i: [
                        e.tensor_copy(rowf2[0:1, 0:512], pFF[2 * i_][0:1, 0:512]),
                        e.tensor_copy(rowf2[0:1, 512:1024], pFF[2 * i_ + 1][0:1, 0:512]),
                    ], waits=[("sMM", last_mm), ("sSC", prev_cast)], inc="sVE")
                    vr2 = S.op("vector", lambda e: e.tensor_add(rowf2[0:1, :], rowf2[0:1, :],
                                                                rowf[0:1, :]),
                               waits=[("sVE", vr), ("sRD", rdf)], inc="sVE")
                    prev_cast = S.op("scalar", lambda e, i_=i: e.activation(
                        sb["b1row"][0:1, i_ * D:(i_ + 1) * D], rowf2[0:1, :], AF.Copy),
                        waits=[("sVE", vr2)], inc="sSC")
                self.dma("gpsimd", dr["b1row_s"][:, :], sb["b1row"][0:1, :],
                         waits=[("sSC", prev_cast)])
            self.end()

    def p3a(self):
        nc, sb, dr = self.nc, self.sb, self.dram
        with ExitStack() as ph:
            T = lambda n, s, dt: ph.enter_context(nc.sbuf_tensor(n, s, dt))
            x2t = [T(f"h_x2{p}", [128, D], F32) for p in range(2)]
            n2b = [T(f"h_n2{p}", [128, D], BF16) for p in range(2)]
            n2fm = [T(f"h_n2fm{p}", [128, D], BF16) for p in range(2)]
            qet = [T(f"h_qe{p}", [128, D], BF16) for p in range(2)]
            qefm = [T(f"h_qefm{p}", [128, D], BF16) for p in range(2)]
            ekt = [T(f"h_ek{p}", [128, D], BF16) for p in range(2)]
            v2b = [T(f"h_v2{p}", [128, D], BF16) for p in range(2)]
            U_sb = T("h_Usb", [128, 2056], F32)
            pT = ph.enter_context(nc.psum_tensor("h_pT", [128, 1024], BF16))
            pP = ph.enter_context(nc.psum_tensor("h_pP", [128, 1024], F32))
            Ub = [ph.enter_context(nc.psum_tensor(f"h_U{i}", [128, 512], F32)) for i in range(4)]
            Ucol = ph.enter_context(nc.psum_tensor("h_Uc", [128, 512], F32))
            self.begin()
            S = self.S
            lnap = [0] * (NT_OWN + 2)
            trn = [0] * (NT_OWN + 2)
            fmev = [0] * (NT_OWN + 2)
            qev = [0] * (NT_OWN + 2)
            qtr_l = [0] * (NT_OWN + 2)
            qfm_ev = [0] * (NT_OWN + 2)
            spl = [0] * (NT_OWN + 2)
            ekev = [0] * (NT_OWN + 2)
            vev = [0] * (NT_OWN + 2)
            umm = [0] * (NT_OWN + 2)
            vrq_l = [0] * (NT_OWN + 2)
            def front(s):
                p = s % 2
                self.dma("sync", x2t[p][:, :], dr["x2s"][s * 128:(s + 1) * 128, :],
                         waits=[("sSC", lnap[s - 2])])
                rdy = self.ckpt("sync")
                lnap[s] = self.ln_stats_apply(x2t[p], n2b[p], ("sRD", rdy), s % 4,
                                              extra_waits=[("sMM", trn[s - 2])],
                                              fused2=True,
                                              save=(sb["ln1_r"], sb["ln1_b"], s))

            def trn_emit(s):
                p = s % 2
                trn[s], fmev[s] = self.tr8(n2b[p], pT, n2fm[p][:, :],
                                           waits=[("sSC", lnap[s]), ("sVE", qfm_ev[s - 1])],
                                           evac_waits=[("sMM", umm[s - 1])])

            def back(s):
                p = s % 2
                mmq = S.op("tensor", lambda e, xf=n2fm[p]: self.proj_mm(
                    e, pP, xf, sb["Wq_b"], sb["bq_r"]),
                    waits=[("sVE", fmev[s]), ("sSC", vev[s - 1]), ("sVE", vrq_l[s - 1])], inc="sMM")
                if s + 1 < NT_OWN:
                    front(s + 1)
                vq1 = S.op("vector", lambda e: [
                    e.reduce_max(sb["qt"][:, hh:hh + 1], pP[:, hh * 256:(hh + 1) * 256], axis=AX.X)
                    for hh in range(4)], waits=[("sMM", mmq)], inc="sVE")
                vq2 = S.op("vector", lambda e: e.tensor_scalar_mul(sb["qt"][:, 4:8], sb["qt"][:, 0:4],
                                                                   -0.25),
                           waits=[("sVE", vq1)], inc="sVE")
                qev[s] = S.op("scalar", lambda e, o=qet[p]: [
                    e.activation(o[:, hh * 256:(hh + 1) * 256], pP[:, hh * 256:(hh + 1) * 256],
                                 AF.Exp, scale=0.25, bias=sb["qt"][:, 4 + hh:5 + hh],
                                 accum_out=sb["qt"][:, 8 + hh:9 + hh])
                    for hh in range(4)], waits=[("sVE", vq2), ("sMM", qtr_l[s - 2])], inc="sSC")
                vrq_l[s] = S.op("vector", lambda e, s_=s: e.reciprocal(
                    sb["rq"][:, 4 * s_:4 * s_ + 4], sb["qt"][:, 8:12]),
                    waits=[("sSC", qev[s])], inc="sVE")
                qtr_l[s], qfm_ev[s] = self.tr8(qet[p], pT, qefm[p][:, :],
                                               waits=[("sSC", qev[s]), ("sVE", fmev[s])],
                                               evac_waits=[("sGP", spl[s - 2])])
                qv = dr["qefm_s"][:, :].rearrange("p (c t) -> p c t", t=2048)[:, :, s * 128:(s + 1) * 128]
                spl[s] = self.dma("gpsimd", qv,
                                  qefm[p][:, :].rearrange("p (c t) -> p c t", t=128),
                                  waits=[("sVE", qfm_ev[s])])
                mmk = S.op("tensor", lambda e, xf=n2fm[p]: self.proj_mm(
                    e, pP, xf, sb["Wk_b"], sb["bk_r"], extra_rows=(sb["negmk_row"],)),
                    waits=[("sSC", qev[s]), ("sVE", vq1)], inc="sMM")
                ekev[s] = S.op("scalar", lambda e, o=ekt[p]: [
                    e.activation(o[:, 0:512], pP[:, 0:512], AF.Exp, scale=0.25),
                    e.activation(o[:, 512:1024], pP[:, 512:1024], AF.Exp, scale=0.25),
                ], waits=[("sMM", mmk), ("sMM", umm[s - 1])], inc="sSC")
                mmv = S.op("tensor", lambda e, xf=n2fm[p]: self.proj_mm(
                    e, pP, xf, sb["Wv_b"], sb["bv_r"]),
                    waits=[("sSC", ekev[s])], inc="sMM")
                vev[s] = S.op("scalar", lambda e, o=v2b[p]: [
                    e.activation(o[:, 0:512], pP[:, 0:512], AF.Copy),
                    e.activation(o[:, 512:1024], pP[:, 512:1024], AF.Copy),
                ], waits=[("sMM", mmv)], inc="sSC")
                def mmU(e, s_=s, ek=ekt[p], v_=v2b[p]):
                    r = []
                    for dc in range(8):
                        hh = dc >> 1
                        r.append(e.matmul(Ub[hh][:, (dc & 1) * 256:((dc & 1) + 1) * 256],
                                          ek[:, dc * 128:(dc + 1) * 128],
                                          v_[:, hh * 256:(hh + 1) * 256],
                                          start=(s_ == 0 and (dc & 1) == 0),
                                          stop=(s_ == NT_OWN - 1 and (dc & 1) == 1)))
                    for dc in range(8):
                        r.append(e.matmul(Ucol[:, dc:dc + 1], ek[:, dc * 128:(dc + 1) * 128],
                                          sb["ones_cb"][:, :],
                                          start=(s_ == 0 and dc == 0),
                                          stop=(s_ == NT_OWN - 1 and dc == 7)))
                    return r
                umm[s] = S.op("tensor", mmU, waits=[("sSC", ekev[s]), ("sSC", vev[s])], inc="sMM")

            front(0)
            trn_emit(0)
            for s in range(NT_OWN):
                back(s)
                if s + 1 < NT_OWN:
                    trn_emit(s + 1)
            uev = S.op("scalar", lambda e: [
                e.activation(U_sb[:, bh * 512:(bh + 1) * 512], Ub[bh][:, 0:512], AF.Copy)
                for bh in range(4)], waits=[("sMM", umm[NT_OWN - 1])], inc="sSC")
            ucv = S.op("vector", lambda e: e.tensor_copy(U_sb[:, 2048:2056], Ucol[:, 0:8]),
                       waits=[("sMM", umm[NT_OWN - 1])], inc="sVE")
            self.dma("gpsimd", dr["cc_in"][:, :], U_sb[:, :],
                     waits=[("sSC", uev), ("sVE", ucv)])
            self.end()

    def p3x(self):
        nc, sb, dr = self.nc, self.sb, self.dram
        self.alloc("C_fm", [128, 2 * D], BF16)
        with ExitStack() as ph:
            T = lambda n, s, dt: ph.enter_context(nc.sbuf_tensor(n, s, dt))
            Ug = T("x_Ug", [128, 2056], F32)
            rcb = T("x_rc", [128, 8], F32)
            self.begin()
            S = self.S
            cc = S.op("gpsimd", lambda e: e.collective_compute(
                "AllReduce", ALU.add, replica_groups=[[0, 1], [2, 3], [4, 5], [6, 7]],
                ins=[dr["cc_in"][:, :]], outs=[dr["cc_out"][:, :]]), inc="sCC")
            self.dma("sync", Ug[:, :], dr["cc_out"][:, :], waits=[("sCC", cc)])
            rdy = self.ckpt("sync")
            vr = S.op("vector", lambda e: e.reciprocal(rcb[:, 0:8], Ug[:, 2048:2056]),
                      waits=[("sRD", rdy)], inc="sVE")
            S.op("scalar", lambda e: [
                e.activation(sb["C_fm"][:, dc * 256:(dc + 1) * 256], Ug[:, dc * 256:(dc + 1) * 256],
                             AF.Identity, scale=rcb[:, dc:dc + 1])
                for dc in range(8)], waits=[("sVE", vr)], inc="sSC")
            self.end()

    def p3e1(self):
        nc, sb, dr = self.nc, self.sb, self.dram
        with ExitStack() as ph:
            T = lambda n, s, dt: ph.enter_context(nc.sbuf_tensor(n, s, dt))
            x2t = [T(f"i_x2{p}", [128, D], F32) for p in range(2)]
            qet = [T(f"i_qe{p}", [128, D], BF16) for p in range(2)]
            atb = [T(f"i_at{p}", [128, D], BF16) for p in range(2)]
            atfm = [T(f"i_atfm{p}", [128, D], BF16) for p in range(2)]
            inpt = [T(f"i_inp{p}", [128, D], F32) for p in range(2)]
            xxt = [T(f"i_xx{p}", [128, D], F32) for p in range(2)]
            pT = ph.enter_context(nc.psum_tensor("i_pT", [128, 1024], BF16))
            pP = ph.enter_context(nc.psum_tensor("i_pP", [128, 1024], F32))
            pW = ph.enter_context(nc.psum_tensor("i_pW", [128, 1024], F32))
            self.begin()
            S = self.S
            atev = [0] * (NT_OWN + 2)
            trat = [0] * (NT_OWN + 2)
            fmev = [0] * (NT_OWN + 2)
            mmw_l = [0] * (NT_OWN + 2)
            mma_l = [0] * (NT_OWN + 2)
            xxv = [0] * (NT_OWN + 2)
            inps = [0] * (NT_OWN + 2)
            stl = [0] * (NT_OWN + 2)
            sq_l = [0] * (NT_OWN + 2)
            for s in range(NT_OWN):
                p = s % 2
                qv = dr["qefm_s"][:, :].rearrange("p (c t) -> p c t", t=2048)[:, :, s * 128:(s + 1) * 128]
                self.dma("sync", qet[p][:, :].rearrange("p (c t) -> p c t", t=128), qv,
                         waits=[("sMM", mma_l[s - 2])])
                self.dma("sync", x2t[p][:, :], dr["x2s"][s * 128:(s + 1) * 128, :],
                         waits=[("sSC", inps[s - 2]), ("sSC", sq_l[s - 2])])
                rdy = self.ckpt("sync")
                def mmA(e, q_=qet[p]):
                    r = []
                    for hh in range(4):
                        for j, dc in enumerate((2 * hh, 2 * hh + 1)):
                            bank_first = (hh % 2 == 0) and j == 0
                            r.append(e.matmul(pP[:, hh * 256:(hh + 1) * 256],
                                              q_[:, dc * 128:(dc + 1) * 128],
                                              sb["C_fm"][:, dc * 256:(dc + 1) * 256],
                                              start=bank_first, stop=(j == 1 and hh % 2 == 1)))
                    return r
                mma_l[s] = S.op("tensor", mmA, waits=[("sRD", rdy), ("sSC", atev[s - 1])], inc="sMM")
                atev[s] = S.op("scalar", lambda e, o=atb[p], s_=s: [
                    e.activation(o[:, hh * 256:(hh + 1) * 256], pP[:, hh * 256:(hh + 1) * 256],
                                 AF.Identity, scale=sb["rq"][:, 4 * s_ + hh:4 * s_ + hh + 1])
                    for hh in range(4)], waits=[("sMM", mma_l[s]), ("sMM", trat[s - 2])], inc="sSC")
                trat[s], fmev[s] = self.tr8(atb[p], pT, atfm[p][:, :],
                                            waits=[("sSC", atev[s]), ("sVE", fmev[s - 1])],
                                            evac_waits=[("sMM", mmw_l[s - 1])])
                mmw_l[s] = S.op("tensor", lambda e, xf=atfm[p]: self.proj_mm(
                    e, pW, xf, sb["wo_bf"], sb["bo_bf"]),
                    waits=[("sVE", fmev[s]), ("sVE", xxv[s - 1])], inc="sMM")
                inps[s] = S.op("scalar", lambda e, a=x2t[p], o=inpt[p], s_=s: e.activation(
                    o[:, :], a[:, :], AF.Identity, scale=sb["ln1_r"][:, s_:s_ + 1],
                    bias=sb["ln1_b"][:, s_:s_ + 1]),
                    waits=[("sRD", rdy), ("sVE", xxv[s - 2])], inc="sSC")
                xxv[s] = S.op("vector", lambda e, p_=p: [
                    e.tensor_add(xxt[p_][:, 0:512], pW[:, 0:512], inpt[p_][:, 0:512]),
                    e.tensor_add(xxt[p_][:, 512:1024], pW[:, 512:1024], inpt[p_][:, 512:1024]),
                ], waits=[("sMM", mmw_l[s]), ("sSC", inps[s]), ("sGP", stl[s - 2])], inc="sVE")
                sq_l[s] = self.ln_stats_apply(xxt[p], None, ("sVE", xxv[s]), p,
                                              save=(sb["ln2_r"], sb["ln2_b"], s))
                stl[s] = self.dma("gpsimd", dr["xxs"][s * 128:(s + 1) * 128, :], xxt[p][:, :],
                                  waits=[("sVE", xxv[s])])
            self.end()

    def p3e2(self):
        nc, sb, dr = self.nc, self.sb, self.dram
        self.alloc("ff2_bf", [128, 32 * D], BF16)
        self.alloc("b1row", [1, 4 * D], BF16)
        with ExitStack() as ph:
            T = lambda n, s, dt: ph.enter_context(nc.sbuf_tensor(n, s, dt))
            xxt = [T(f"j_xx{p}", [128, D], F32) for p in range(2)]
            n3b = [T(f"j_n3{p}", [128, D], BF16) for p in range(2)]
            n3fm = [T(f"j_n3fm{p}", [128, D], BF16) for p in range(2)]
            u_tm = T("j_utm", [128, 4 * D], BF16)
            u_fm = T("j_ufm", [128, 4 * D], BF16)
            xx2t = [T(f"j_xx2{p}", [128, D], F32) for p in range(2)]
            pT = ph.enter_context(nc.psum_tensor("j_pT", [128, 1024], BF16))
            pF1 = [ph.enter_context(nc.psum_tensor(f"j_f1{i}", [128, 512], F32)) for i in range(4)]
            pF2 = ph.enter_context(nc.psum_tensor("j_f2", [128, 1024], F32))
            self.begin()
            S = self.S
            self.dma("gpsimd", sb["b1row"][0:1, :], dr["b1row_s"][:, :])
            for c in range(32):
                self.dma("gpsimd", sb["ff2_bf"][:, c * D:(c + 1) * D],
                         dr["bff2"][c * 128:(c + 1) * 128, :])
            ff2_rdy = self.ckpt("gpsimd")
            lnap = [0] * (NT_OWN + 2)
            trn = [0] * (NT_OWN + 2)
            fmev = [0] * (NT_OWN + 2)
            epi = [0] * (NT_OWN + 2)
            stl = [0] * (NT_OWN + 2)
            mmf2_last = [0] * (NT_OWN + 2)
            pT_prev = [0]
            utr_prev = [0, 0]
            gel_prev1 = [0]
            gel_cur = {}

            def front(s):
                p = s % 2
                self.dma("sync", xxt[p][:, :], dr["xxs"][s * 128:(s + 1) * 128, :],
                         waits=[("sVE", epi[s - 2])])
                rdy = self.ckpt("sync")
                lnap[s] = S.op("scalar", lambda e, a=xxt[p], o=n3b[p], s_=s: e.activation(
                    o[:, :], a[:, :], AF.Identity, scale=sb["ln2_r"][:, s_:s_ + 1],
                    bias=sb["ln2_b"][:, s_:s_ + 1]),
                    waits=[("sRD", rdy), ("sMM", trn[s - 2])], inc="sSC")
                trn[s], fmev[s] = self.tr8(n3b[p], pT, n3fm[p][:, :],
                                           waits=[("sSC", lnap[s]), ("sVE", pT_prev[0])],
                                           evac_waits=[("sMM", mmf2_last[s - 2])])
                pT_prev[0] = fmev[s]

            def half(s, h):
                p = s % 2
                def mmF1(e, xf=n3fm[p], h_=h):
                    r = []
                    for q in range(4):
                        oc = h_ * 4 + q
                        r.append(e.matmul(pF1[q][:, 0:512], sb["ones_rb"][0:1, 0:128],
                                          sb["b1row"][0:1, oc * 512:(oc + 1) * 512],
                                          start=True, stop=False))
                    for ic in range(8):
                        for q in range(4):
                            oc = h_ * 4 + q
                            r.append(e.matmul(pF1[q][:, 0:512], xf[:, ic * 128:(ic + 1) * 128],
                                              sb["Wff1"][:, ic * 4 * D + oc * 512: ic * 4 * D + (oc + 1) * 512],
                                              start=False, stop=(ic == 7)))
                    return r
                mmf1 = S.op("tensor", mmF1,
                            waits=[("sVE", fmev[s]),
                                   ("sSC", gel_cur[(s, 0)] if h else gel_prev1[0]),
                                   ("sGD", ff2_rdy)], inc="sMM")
                if h == 0 and s + 1 < NT_OWN:
                    front(s + 1)
                gel_cur[(s, h)] = S.op("scalar", lambda e, h_=h: [
                    e.activation(u_tm[:, (h_ * 4 + q) * 512:(h_ * 4 + q + 1) * 512],
                                 pF1[q][:, 0:512], AF.Gelu_apprx_tanh)
                    for q in range(4)], waits=[("sMM", mmf1), ("sMM", utr_prev[h])],
                    inc="sSC")
                tev = [0, 0]
                for b2 in range(2):
                    cb = h * 16 + b2 * 8
                    tt = S.op("tensor", lambda e, cb_=cb: [
                        e.transpose(pT[:, c * 128:(c + 1) * 128],
                                    u_tm[:, (cb_ + c) * 128:(cb_ + c + 1) * 128],
                                    sb["ident"][:, :]) for c in range(8)],
                        waits=[("sSC", gel_cur[(s, h)]), ("sVE", pT_prev[0])], inc="sMM")
                    tev[b2] = S.op("vector", lambda e, cb_=cb: e.tensor_copy(
                        u_fm[:, cb_ * 128:(cb_ + 8) * 128], pT[:, :]),
                        waits=[("sMM", tt), ("sMM", mmf2_last[s - 1])], inc="sVE")
                    pT_prev[0] = tev[b2]
                    utr_prev[h] = tt
                def mmF2(e, h_=h):
                    r = []
                    if h_ == 0:
                        for hx in range(2):
                            r.append(e.matmul(pF2[:, hx * 512:(hx + 1) * 512],
                                              sb["ones_rb"][0:1, 0:128],
                                              sb["bf2_bf"][0:1, hx * 512:(hx + 1) * 512],
                                              start=True, stop=False))
                    for c in range(16):
                        cc_ = h_ * 16 + c
                        for hx in range(2):
                            r.append(e.matmul(pF2[:, hx * 512:(hx + 1) * 512],
                                              u_fm[:, cc_ * 128:(cc_ + 1) * 128],
                                              sb["ff2_bf"][:, cc_ * D + hx * 512: cc_ * D + (hx + 1) * 512],
                                              start=False, stop=(h_ == 1 and c == 15)))
                    return r
                return S.op("tensor", mmF2,
                            waits=[("sVE", tev[1]), ("sGD", ff2_rdy),
                                   ("sVE", epi[s - 1])], inc="sMM")

            def back(s):
                if s < 0:
                    return
                p = s % 2
                epi[s] = S.op("vector", lambda e, p_=p: [
                    e.tensor_add(xx2t[p_][:, 0:512], pF2[:, 0:512], xxt[p_][:, 0:512]),
                    e.tensor_add(xx2t[p_][:, 512:1024], pF2[:, 512:1024], xxt[p_][:, 512:1024]),
                ], waits=[("sMM", mmf2_last[s]), ("sGP", stl[s - 2])], inc="sVE")
                stl[s] = self.dma("gpsimd", dr["xx2s"][s * 128:(s + 1) * 128, :], xx2t[p][:, :],
                                  waits=[("sVE", epi[s])])

            front(0)
            for s in range(NT_OWN):
                half(s, 0)
                mmf2_last[s] = half(s, 1)
                gel_prev1[0] = gel_cur[(s, 1)]
                back(s)
            self.end()

    def p3e3(self):
        nc, sb, dr = self.nc, self.sb, self.dram
        self.alloc("wd_bf", [128, 8 * D], BF16, side="right")
        self.alloc("wdqT", [128, 8 * D], BF16)
        with ExitStack() as ph:
            T = lambda n, s, dt: ph.enter_context(nc.sbuf_tensor(n, s, dt))
            xx2 = [T(f"k_xx2{p}", [128, D], F32) for p in range(2)]
            x2t = [T(f"k_x2{p}", [128, D], F32) for p in range(2)]
            xxb = [T(f"k_xxb{p}", [128, D], BF16) for p in range(2)]
            xxfm = [T(f"k_xxfm{p}", [128, D], BF16) for p in range(2)]
            x3t = [T(f"k_x3{p}", [128, D], F32) for p in range(3)]
            ht = [T(f"k_h{p}", [128, D], F32) for p in range(3)]
            xd = [T(f"k_xd{p}", [128, D], BF16) for p in range(2)]
            xdfm = [T(f"k_xdfm{p}", [128, D], BF16) for p in range(2)]
            x4t = [T(f"k_x4{p}", [128, D], F32) for p in range(2)]
            z = [T(f"k_z{i}", [128, D], F32) for i in range(2)]
            zsq = [T(f"k_zsq{i}", [128, D], F32) for i in range(2)]
            mg = [T(f"k_mg{i}", [128, D], F32) for i in range(2)]
            pT = ph.enter_context(nc.psum_tensor("k_pT", [128, 1024], BF16))
            pD = [ph.enter_context(nc.psum_tensor(f"k_pD{i}", [128, 1024], F32)) for i in range(2)]
            pM = ph.enter_context(nc.psum_tensor("k_pM", [128, 1024], F32))
            self.begin()
            S = self.S
            self.dma("gpsimd", sb["wdqT"][:, :], dr["wdq_s"][:, :])
            for jc in range(8):
                self.dma("gpsimd", sb["wd_bf"][:, jc * D:(jc + 1) * D],
                         dr["bwd"][jc * 128:(jc + 1) * 128, :])
            wd_rdy = self.ckpt("gpsimd")
            cast_l = [0] * (NT_OWN + 2)
            trx = [0] * (NT_OWN + 2)
            fmev = [0] * (NT_OWN + 2)
            mmd = [0] * (NT_OWN + 2)
            x3v = [0] * (NT_OWN + 2)
            lnap = [0] * (NT_OWN + 2)
            vq_l = [0] * (NT_OWN + 2)
            trq = [0] * (NT_OWN + 2)
            fmq = [0] * (NT_OWN + 2)
            mmx = [0] * (NT_OWN + 2)
            x4v = [0] * (NT_OWN + 2)
            stl = [0] * (NT_OWN + 2)
            def stA(s):
                p = s % 2
                self.dma("sync", xx2[p][:, :], dr["xx2s"][s * 128:(s + 1) * 128, :],
                         waits=[("sSC", cast_l[s - 2])])
                self.dma("sync", x2t[p][:, :], dr["x2s"][s * 128:(s + 1) * 128, :],
                         waits=[("sVE", x3v[s - 2])])
                rdy = self.ckpt("sync")
                cast_l[s] = S.op("scalar", lambda e, a=xx2[p], o=xxb[p]: e.activation(
                    o[:, :], a[:, :], AF.Copy),
                    waits=[("sRD", rdy), ("sMM", trx[s - 2])], inc="sSC")
                trx[s], fmev[s] = self.tr8(xxb[p], pT, xxfm[p][:, :],
                                           waits=[("sSC", cast_l[s]), ("sVE", fmq[s - 3])],
                                           evac_waits=[("sMM", mmd[s - 1])])
                mmd[s] = S.op("tensor", lambda e, xf=xxfm[p], rp=pD[s % 2]: self.proj_mm(
                    e, rp, xf, sb["wd_bf"], sb["bd_bf"]),
                    waits=[("sVE", fmev[s]), ("sGD", wd_rdy), ("sVE", x3v[s - 2])], inc="sMM")

            def stB(s):
                if s < 0 or s >= NT_OWN:
                    return
                p = s % 2
                p3 = s % 3
                p4 = s % 4
                x3v[s] = S.op("vector", lambda e, p_=p3, p2_=p, rp=pD[s % 2]: [
                    e.tensor_add(x3t[p_][:, 0:512], rp[:, 0:512], x2t[p2_][:, 0:512]),
                    e.tensor_add(x3t[p_][:, 512:1024], rp[:, 512:1024], x2t[p2_][:, 512:1024]),
                ], waits=[("sMM", mmd[s]), ("sSC", lnap[s - 3]), ("sVE", x4v[s - 3])], inc="sVE")
                lnap[s] = self.ln_stats_apply(x3t[p3], ht[p3], ("sVE", x3v[s]), p4,
                                              extra_waits=[("sVE", vq_l[s - 3])],
                                              sum_on_scalar=True)
                vq_l[s] = self.quant(ht[p3], xd[p], z[p], zsq[p], mg[p], ("sSC", lnap[s]), p4,
                                     vq_waits=[("sMM", trq[s - 2])])

            def stC(s):
                if s < 0 or s >= NT_OWN:
                    return
                p = s % 2
                p3 = s % 3
                trq[s], fmq[s] = self.tr8(xd[p], pT, xdfm[p][:, :],
                                          waits=[("sVE", vq_l[s]),
                                                 ("sVE", fmev[s + 2] if s + 2 < NT_OWN else fmev[NT_OWN - 1])],
                                          evac_waits=[("sMM", mmx[s - 1])])
                mmx[s] = S.op("tensor", lambda e, xf=xdfm[p]: self.proj_mm(
                    e, pM, xf, sb["wdqT"], sb["mxb_bf"]),
                    waits=[("sVE", fmq[s]), ("sVE", x4v[s - 1]), ("sGD", wd_rdy)], inc="sMM")
                x4v[s] = S.op("vector", lambda e, p_=p, p3_=p3: [
                    e.tensor_add(x4t[p_][:, 0:512], pM[:, 0:512], x3t[p3_][:, 0:512]),
                    e.tensor_add(x4t[p_][:, 512:1024], pM[:, 512:1024], x3t[p3_][:, 512:1024]),
                ], waits=[("sMM", mmx[s]), ("sGP", stl[s - 2])], inc="sVE")
                stl[s] = self.dma("gpsimd", dr["out"][s * 128:(s + 1) * 128, :], x4t[p][:, :],
                                  waits=[("sVE", x4v[s])])

            for i in range(NT_OWN + 2):
                if i < NT_OWN:
                    stA(i)
                stB(i - 1)
                stC(i - 2)
            self.end()

    def build(self):
        ph = self.phases
        if "prep" in ph:
            self.prep()
        if "p0a" in ph:
            self.p0a()
            self.free("wkT", "wvT")
        if "p0b" in ph:
            self.alloc("wqT", [128, 8 * D], BF16, side="right")
            self.begin()
            for jc in range(8):
                self.wq_loaded = self.dma("gpsimd", self.sb["wqT"][:, jc * D:(jc + 1) * D],
                                          self.dram["wqT"][jc * 128:(jc + 1) * 128, :])
            self.end()
            self.p0b()
            self.free("wqT")
        if "p1" in ph:
            self.p1()
            self.free("V_tm", "K_fm")
        if "p2w" in ph:
            self.p2w()
        if "p2" in ph:
            self.p2()
        if "p2w" in ph:
            self.begin()
            self.dma("gpsimd", self.dram["wdq_s"][:, :], self.sb["wdqT"][:, :])
            self.end()
            self.free("wdqT")
        if "p3w" in ph:
            self.p3w()
            self.free("b1row")
        if "p3a" in ph:
            self.p3a()
            self.free("Wv_b", "Wk_b", "Wq_b")
        if "p3x" in ph:
            self.p3x()
        if "p3e1" in ph:
            self.p3e1()
            self.free("wo_bf", "C_fm")
        if "p3e2" in ph:
            self.p3e2()
            self.free("b1row", "ff2_bf", "negmk_row", "Wff1")
        if "p3e3" in ph:
            self.p3e3()
        if self._cms:
            self.free(*list(self._cms.keys()))
        self.stack.close()
        return self.nc


def _host_tables():
    i = np.arange(512, dtype=np.float64)
    phi = 10000.0 ** (-2.0 * (i - 1.0) / D)
    pos = np.arange(NSEQ, dtype=np.float64)
    ang = phi[:, None] * pos[None, :]
    cos = np.cos(ang).astype(np.float32)
    sin = np.sin(ang).astype(np.float32)
    cos_fm = np.ascontiguousarray(cos.reshape(4, 128, NSEQ).transpose(1, 0, 2).reshape(128, 4 * NSEQ))
    sin_fm = np.ascontiguousarray(sin.reshape(4, 128, NSEQ).transpose(1, 0, 2).reshape(128, 4 * NSEQ))
    return cos_fm, sin_fm


def _own_slices(tab_fm, h):
    t = tab_fm.reshape(128, 4, NT_ALL, 128)
    own = t[:, :, h::2, :]
    return np.ascontiguousarray(own.reshape(128, 4 * 2048))


def _p1_masks(h):
    M = np.zeros((8, 128, 4, 256), np.float32)
    for g in range(8):
        for m in range(4):
            kt = 4 * g + m
            kg = kt * 128 + np.arange(128)
            tA = (4 * g + h) * 128
            tB = (4 * g + 2 + h) * 128
            qg = np.concatenate([tA + np.arange(128), tB + np.arange(128)])
            M[g, :, m, :] = np.where(qg[None, :] >= kg[:, None], 0.0, NEG)
    return np.ascontiguousarray(M.reshape(8, 128, 1024))


def prepare_inputs(inputs):
    x = np.ascontiguousarray(inputs["L0_attn_a_x"], dtype=np.float32)
    perm = np.concatenate([np.arange(0, D, 2), np.arange(1, D, 2)])
    wqT = np.ascontiguousarray(np.asarray(inputs["L0_attn_a_wq"], np.float32).T[:, perm])
    wkT = np.ascontiguousarray(np.asarray(inputs["L0_attn_a_wk"], np.float32).T[:, perm])
    wvT = np.ascontiguousarray(np.asarray(inputs["L0_attn_a_wv"], np.float32).T)
    cos_fm, sin_fm = _host_tables()
    colsify = lambda v: np.ascontiguousarray(np.asarray(v, np.float32).reshape(8, 128).T)
    common = dict(
        wqT=wqT, wkT=wkT, wvT=wvT, costab=cos_fm, sintab=sin_fm,
        mxw=np.ascontiguousarray(np.asarray(inputs["L0_ffn_weight"], np.float32)),
        mxb=np.asarray(inputs["L0_ffn_bias"], np.float32).reshape(1, D),
        bwq=np.ascontiguousarray(np.asarray(inputs["L0_attn_b_wq"], np.float32)),
        bwk=np.ascontiguousarray(np.asarray(inputs["L0_attn_b_wk"], np.float32)),
        bwv=np.ascontiguousarray(np.asarray(inputs["L0_attn_b_wv"], np.float32)),
        bwo=np.ascontiguousarray(np.asarray(inputs["L0_attn_b_wo"], np.float32)),
        bwd=np.ascontiguousarray(np.asarray(inputs["L0_attn_b_w_d"], np.float32)),
        bff1=np.ascontiguousarray(np.asarray(inputs["L0_attn_b_w_ff1"], np.float32)),
        bff2=np.ascontiguousarray(np.asarray(inputs["L0_attn_b_w_ff2"], np.float32)),
        g1c=colsify(inputs["L0_attn_b_ln1_g"]), b1c=colsify(inputs["L0_attn_b_ln1_b"]),
        g2c=colsify(inputs["L0_attn_b_ln2_g"]), b2c=colsify(inputs["L0_attn_b_ln2_b"]),
        bo_r=np.asarray(inputs["L0_attn_b_bo"], np.float32).reshape(1, D),
        bf1_r=np.asarray(inputs["L0_attn_b_b_ff1"], np.float32).reshape(1, 4 * D),
        bf2_r=np.asarray(inputs["L0_attn_b_b_ff2"], np.float32).reshape(1, D),
        bd_r=np.asarray(inputs["L0_attn_b_b_d"], np.float32).reshape(1, D),
        ident=np.eye(128, dtype=np.float32),
        ones_r=np.ones((1, 1024), np.float32),
        ones_c=np.ones((128, 1), np.float32),
    )
    masks = [_p1_masks(0), _p1_masks(1)]
    cos_own = [_own_slices(cos_fm, 0), _own_slices(cos_fm, 1)]
    sin_own = [_own_slices(sin_fm, 0), _own_slices(sin_fm, 1)]
    in_maps = []
    for c in range(8):
        b, h = c >> 1, c & 1
        m = dict(common)
        m["x0"] = np.ascontiguousarray(x[b])
        m["x0own"] = np.ascontiguousarray(
            x[b].reshape(NT_ALL, 128, D)[h::2].reshape(NT_OWN * 128, D))
        m["p1mask"] = masks[h]
        m["costab_own"] = cos_own[h]
        m["sintab_own"] = sin_own[h]
        in_maps.append(m)

    def assemble(results):
        out = np.empty((4, NSEQ, D), np.float32)
        for c in range(8):
            b, h = c >> 1, c & 1
            o = np.asarray(results[c]["out"], np.float32)
            for s in range(NT_OWN):
                t = 2 * s + h
                out[b, t * 128:(t + 1) * 128, :] = o[s * 128:(s + 1) * 128, :]
        return out
    return in_maps, assemble


def build_program():
    return Builder(PHASES, DBG).build()


def kernel(**inputs):
    in_maps, assemble = prepare_inputs(inputs)
    nc = build_program()
    res = run_bass_kernel_spmd(nc, in_maps, core_ids=list(range(8)))
    return assemble(res.results)


# version 10
# speedup vs baseline: 1.0832x; 1.0832x over previous
import numpy as np
import ml_dtypes
from contextlib import ExitStack
import concourse.bass as bass
import concourse.mybir as mybir
from concourse.bass_utils import run_bass_kernel_spmd

F32 = mybir.dt.float32
BF16 = mybir.dt.bfloat16
AF = mybir.ActivationFunctionType
ALU = mybir.AluOpType
AX = mybir.AxisListType

D = 1024
NSEQ = 4096
NT_ALL = 32
NT_OWN = 16
EPS = 1e-5
NEG = -3.0e38
MAGIC = 12582912.0
ENGS = ("sync", "gpsimd", "tensor", "scalar", "vector")

PHASES = ["prep", "p0a", "p0b", "p1", "p2w", "p2", "p3w", "p3a", "p3x", "p3e1", "p3e2", "p3e3"]
DBG = None
STRIP = 0
SERIAL = False


class Sched:
    def __init__(self, counts, last_wait):
        self.ev = {e: [] for e in ENGS}
        self.C = counts
        self.lastw = last_wait

    def op(self, eng, fn, waits=(), inc=None, n_dma=1):
        if SERIAL:
            waits = list(waits) + [(s, v) for s, v in self.C.items()
                                   if s not in ("sPH",) and v > 0]
        w = []
        for s, v in waits:
            if v > 0 and self.lastw.get((eng, s), -1) < v:
                w.append((s, v))
                self.lastw[(eng, s)] = v
        dma = inc in ("sIN", "sGP")
        self.ev[eng].append((fn, w, inc, dma))
        if inc is not None:
            self.C[inc] += 16 * n_dma if dma else 1
        return self.C[inc] if inc is not None else None


def emit_block(nc, sched, sems):
    with nc.Block() as blk:
        def mk(eng_name):
            def closure(eng):
                for fn, waits, inc, dma in sched.ev[eng_name]:
                    for s, v in waits:
                        eng.wait_ge(sems[s], v)
                    r = fn(eng)
                    if inc is None:
                        continue
                    instrs = r if isinstance(r, (list, tuple)) else [r]
                    if dma:
                        for i in instrs:
                            i.then_inc(sems[inc], 16)
                    else:
                        instrs[-1].then_inc(sems[inc], 1)
            return closure
        blk.sync(mk("sync"))
        blk.gpsimd(mk("gpsimd"))
        blk.tensor(mk("tensor"))
        blk.scalar(mk("scalar"))
        blk.vector(mk("vector"))


class Builder:
    def __init__(self, phases, dbg=None):
        self.phases = phases
        self.dbg = dbg
        nc = bass.Bass("TRN2", target_bir_lowering=False)
        self.nc = nc
        self.C = {k: 0 for k in ("sIN", "sGP", "sMM", "sSC", "sVE", "sPH", "sCC", "sRD", "sGD")}
        self.lastw = {}
        self.stack = ExitStack()
        self.sems = {k: self.stack.enter_context(nc.semaphore(k)) for k in self.C}
        self.phase_idx = 0
        self.dram = {}
        self.sb = {}
        self._cms = {}
        self._seq = {}
        self._declare_dram()
        self._alloc_persistent()

    def _declare_dram(self):
        nc = self.nc
        inp = lambda name, shape, dt=F32: self.dram.__setitem__(
            name, nc.dram_tensor(name, shape, dt, kind="ExternalInput"))
        inp("x0", [NSEQ, D])
        inp("x0own", [NT_OWN * 128, D])
        inp("p1mask", [8, 128, 1024])
        inp("wqT", [D, D]); inp("wkT", [D, D]); inp("wvT", [D, D])
        inp("costab", [128, 4 * NSEQ]); inp("sintab", [128, 4 * NSEQ])
        inp("costab_own", [128, 4 * 2048]); inp("sintab_own", [128, 4 * 2048])
        inp("mxw", [D, D]); inp("mxb", [1, D])
        inp("bwq", [D, D]); inp("bwk", [D, D]); inp("bwv", [D, D])
        inp("bwo", [D, D]); inp("bwd", [D, D])
        inp("bff1", [D, 4 * D]); inp("bff2", [4 * D, D])
        inp("g1c", [128, 8]); inp("b1c", [128, 8]); inp("g2c", [128, 8]); inp("b2c", [128, 8])
        inp("bo_r", [1, D]); inp("bf1_r", [1, 4 * D]); inp("bf2_r", [1, D]); inp("bd_r", [1, D])
        inp("ident", [128, 128]); inp("ones_r", [1, 1024]); inp("ones_c", [128, 1])
        self.dram["out"] = nc.dram_tensor("out", [NT_OWN * 128, D], F32, kind="ExternalOutput")
        d = self.dbg
        scr = lambda name, shape, dt=F32, dbg_out=False: self.dram.__setitem__(
            name, nc.dram_tensor(name, shape, dt, kind="ExternalOutput" if dbg_out else "Internal"))
        scr("x1s", [NT_OWN * 128, D], F32, d in ("p1", "all"))
        scr("x2s", [NT_OWN * 128, D], F32, d in ("p2", "all"))
        scr("xxs", [NT_OWN * 128, D], F32, d in ("p3e1", "all"))
        scr("xx2s", [NT_OWN * 128, D], F32, d in ("p3e2", "all"))
        scr("qfm_s", [128, 8 * 2048], BF16, d == "p0")
        scr("qefm_s", [128, 8 * 2048], BF16, d == "p3a")
        scr("cc_in", [128, 2056], F32)
        scr("cc_out", [128, 2056], F32)
        scr("wdq_s", [128, 8 * D], BF16, d == "all")
        scr("b1row_s", [1, 4 * D], BF16, d == "all")
        if d == "all":
            scr("dbg_wff1", [128, 4 * D], BF16, True)
            scr("dbg_u", [128, 8 * D], BF16, True)
        if d == "p0":
            scr("dbg_kfm", [128, 8 * NSEQ], BF16, True)
            scr("dbg_vtm", [128, NT_ALL * D], BF16, True)
            scr("dbg_negm", [1, 256], F32, True)
            scr("dbg_tiny", [128, 16], F32, True)

            scr("dbg_kn", [128, NT_ALL], F32, True)
            scr("dbg_qn", [128, NT_OWN], F32, True)

    def _alloc_persistent(self):
        nc, st = self.nc, self.stack
        T = lambda n, s, dt: st.enter_context(nc.sbuf_tensor(n, s, dt))
        self.sb["ident"] = T("c_ident", [128, 128], BF16)
        self.sb["ones_rb"] = T("c_ones_rb", [1, 1024], BF16)
        self.sb["ones_rf"] = T("c_ones_rf", [1, 1024], F32)
        self.sb["ones_cb"] = T("c_ones_cb", [128, 1], BF16)
        self.sb["negm_row"] = T("negm_row", [1, 256], BF16)
        self.sb["sq"] = T("sq_scratch", [128, 1024], F32)
        self.sb["rsc"] = T("rope_scratch", [128, 1024], BF16)
        self.sb["tiny"] = T("tiny", [128, 80], F32)
        self.sb["epsc"] = T("epsc", [128, 1], F32)
        self.sb["nepsc"] = T("nepsc", [128, 1], F32)
        self.sb["mgc"] = T("mgc", [128, 1], F32)
        self.sb["knorm"] = T("knorm", [128, NT_ALL], F32)
        self.sb["qnorm"] = T("qnorm", [128, NT_OWN], F32)
        self.sb["ln1_r"] = T("ln1_r", [128, NT_OWN], F32)
        self.sb["ln1_b"] = T("ln1_b", [128, NT_OWN], F32)
        self.sb["ln2_r"] = T("ln2_r", [128, NT_OWN], F32)
        self.sb["ln2_b"] = T("ln2_b", [128, NT_OWN], F32)
        self.sb["rq"] = T("rq", [128, NT_OWN * 4], F32)
        self.sb["qt"] = T("qt", [128, 384], F32)


    def barrier(self):
        S = self.S
        for e in ENGS:
            w = [("sIN", self.C["sIN"])] if e == "sync" else (
                [("sGP", self.C["sGP"])] if e == "gpsimd" else [])
            S.op(e, lambda eng: eng.nop(), waits=w, inc="sPH", n_dma=0)
        self.phase_idx += 1

    def begin(self):
        self.S = Sched(self.C, self.lastw)
        tgt = 5 * self.phase_idx
        if tgt:
            for e in ENGS:
                self.S.lastw.pop((e, "sPH"), None)
                self.S.ev[e].append(((lambda eng: None), [("sPH", tgt)], None, False))

    def end(self):
        self.barrier()
        emit_block(self.nc, self.S, self.sems)
        self.S = None

    def alloc(self, name, shape, dt, side=None):
        self._aseq = getattr(self, "_aseq", 0) + 1
        cm = self.nc.sbuf_tensor(f"s{self._aseq}_{name}", shape, dt, side=side)
        h = cm.__enter__()
        self._cms[name] = cm
        self._seq[name] = self._seqno = getattr(self, "_seqno", 0) + 1
        self.sb[name] = h
        return h

    def free(self, *names):
        for n in sorted(names, key=lambda n: -self._seq[n]):
            self._cms.pop(n).__exit__(None, None, None)
            self.sb.pop(n)
            self._seq.pop(n)

    def ckpt(self, eng):
        if eng == "sync":
            return self.S.op("sync", lambda e: e.nop(), waits=[("sIN", self.C["sIN"])],
                             inc="sRD", n_dma=0)
        return self.S.op("gpsimd", lambda e: e.nop(), waits=[("sGP", self.C["sGP"])],
                         inc="sGD", n_dma=0)

    def dma(self, eng, out, in_, waits=()):
        return self.S.op(eng, lambda e, o=out, i=in_: e.dma_start(out=o, in_=i),
                         waits=waits, inc="sIN" if eng == "sync" else "sGP")

    def ln_stats_apply(self, xin, n_out, x_ready, par, extra_waits=(), fused2=None, save=None,
                       sum_on_scalar=False):
        S, sb = self.S, self.sb
        b = 20 * par
        t = lambda i: sb["tiny"][:, b + i:b + i + 1]
        sq = S.op("scalar", lambda e, a=xin: e.activation(sb["sq"][:, :], a[:, :], AF.Square,
                                                          accum_out=t(1)),
                  waits=[x_ready], inc="sSC")
        if sum_on_scalar:
            v1 = 0
            sq = S.op("scalar", lambda e, a=xin: e.activation(sb["sq"][:, :], a[:, :], AF.Copy,
                                                              accum_out=t(0)),
                      waits=[x_ready], inc="sSC")
        else:
            v1 = S.op("vector", lambda e, a=xin: e.reduce_sum(t(0), a[:, :], axis=AX.X),
                      waits=[x_ready], inc="sVE")
        v2 = S.op("vector", lambda e: [
            e.tensor_scalar_mul(t(2), t(0), 1.0 / D),
            e.tensor_scalar(t(3), t(1), 1.0 / D, EPS, ALU.mult, ALU.add),
        ], waits=[("sVE", v1), ("sSC", sq)], inc="sVE")
        v3 = S.op("vector", lambda e: e.tensor_mul(t(4), t(2), t(2)), waits=[("sVE", v2)], inc="sVE")
        v4 = S.op("vector", lambda e: e.scalar_tensor_tensor(t(5), t(4), -1.0, t(3),
                                                             ALU.mult, ALU.add),
                  waits=[("sVE", v3)], inc="sVE")
        s2 = S.op("scalar", lambda e: [
            e.activation(t(6), t(5), AF.Ln),
            e.activation(t(7), t(6), AF.Exp, scale=-0.5),
        ], waits=[("sVE", v4)], inc="sSC")
        if not fused2:
            v5 = S.op("vector", lambda e: e.scalar_tensor_tensor(t(8), t(2), -1.0, t(7),
                                                                 ALU.mult, ALU.mult),
                      waits=[("sSC", s2)], inc="sVE")
            def fin(e):
                r = []
                if save is not None:
                    rbuf, bbuf, scol = save
                    r += [e.activation(rbuf[:, scol:scol + 1], t(7), AF.Identity),
                          e.activation(bbuf[:, scol:scol + 1], t(8), AF.Identity)]
                if n_out is not None:
                    r.append(e.activation(n_out[:, :], xin[:, :], AF.Identity,
                                          scale=t(7), bias=t(8)))
                return r
            return S.op("scalar", fin, waits=[("sVE", v5)] + list(extra_waits), inc="sSC")
        v5 = S.op("vector", lambda e: [
            e.scalar_tensor_tensor(t(8), t(2), -1.0, t(7), ALU.mult, ALU.mult),
            e.tensor_mul(t(10), t(7), t(7)),
            e.tensor_scalar_add(t(11), t(5), -EPS),
        ], waits=[("sSC", s2)], inc="sVE")
        v8 = S.op("vector", lambda e: e.scalar_tensor_tensor(
            t(13), t(10), t(11), sb["epsc"][:, :], ALU.mult, ALU.add),
            waits=[("sVE", v5)], inc="sVE")
        s3 = S.op("scalar", lambda e: [
            e.activation(t(14), t(13), AF.Ln),
            e.activation(t(15), t(14), AF.Exp, scale=-0.5),
        ], waits=[("sVE", v8)], inc="sSC")
        v10 = S.op("vector", lambda e: e.tensor_mul(t(16), t(7), t(15)),
                   waits=[("sSC", s3)], inc="sVE")
        v11 = S.op("vector", lambda e: e.scalar_tensor_tensor(t(18), t(2), -1.0, t(16),
                                                              ALU.mult, ALU.mult),
                   waits=[("sVE", v10)], inc="sVE")
        def fin2(e):
            r = []
            if save is not None:
                rbuf, bbuf, scol = save
                r += [e.activation(rbuf[:, scol:scol + 1], t(7), AF.Identity),
                      e.activation(bbuf[:, scol:scol + 1], t(8), AF.Identity)]
            if n_out is not None:
                r.append(e.activation(n_out[:, :], xin[:, :], AF.Identity,
                                      scale=t(16), bias=t(18)))
            return r
        return S.op("scalar", fin2, waits=[("sVE", v11)] + list(extra_waits), inc="sSC")

    def rope(self, kraw, cosb, sinb, out_view_e, out_view_o, waits):
        S, sb = self.S, self.sb
        r3 = lambda ap: ap.rearrange("p (c t) -> p c t", t=128)
        return S.op("vector", lambda e, kr=kraw, cb=cosb, sn=sinb, oe=out_view_e, oo=out_view_o: [
            e.tensor_mul(r3(sb["rsc"][:, 0:512]), r3(kr[:, 0:512]), r3(cb[:, :])),
            e.tensor_mul(r3(sb["rsc"][:, 512:1024]), r3(kr[:, 512:1024]), r3(sn[:, :])),
            e.tensor_add(oe, r3(sb["rsc"][:, 0:512]), r3(sb["rsc"][:, 512:1024])),
            e.tensor_mul(r3(sb["rsc"][:, 0:512]), r3(kr[:, 512:1024]), r3(cb[:, :])),
            e.tensor_mul(r3(sb["rsc"][:, 512:1024]), r3(kr[:, 0:512]), r3(sn[:, :])),
            e.tensor_sub(oo, r3(sb["rsc"][:, 0:512]), r3(sb["rsc"][:, 512:1024])),
        ], waits=waits, inc="sVE")

    def quant(self, h, xd_bf, z, zsq, mg, h_ready, par, vq_waits=()):
        S, sb = self.S, self.sb
        b = 96 * par
        qt = sb["qt"]
        r3 = lambda ap: ap.rearrange("p (g k) -> p g k", k=32)
        v1 = S.op("vector", lambda e: e.reduce_max(qt[:, b:b + 32], r3(h[:, :]), axis=AX.X,
                                                   apply_absolute_value=True),
                  waits=[h_ready], inc="sVE")
        v2 = S.op("vector", lambda e: e.tensor_scalar(qt[:, b + 32:b + 64], qt[:, b:b + 32],
                                                      1.0 / 7.5, 1e-30, ALU.mult, ALU.max),
                  waits=[("sVE", v1)], inc="sVE")
        v3 = S.op("vector", lambda e: e.reciprocal(qt[:, b + 64:b + 96], qt[:, b + 32:b + 64]),
                  waits=[("sVE", v2)], inc="sVE")
        vz = S.op("vector", lambda e: e.tensor_mul(
            r3(z[:, :]), r3(h[:, :]), qt[:, b + 64:b + 96, None].broadcast_to([128, 32, 32])),
            waits=[("sVE", v3)], inc="sVE")
        s1 = S.op("scalar", lambda e: e.activation(zsq[:, :], z[:, :], AF.Square),
                  waits=[("sVE", vz)], inc="sSC")
        vab = S.op("vector", lambda e: [
            e.tensor_scalar(mg[:, :], zsq[:, :], 16.0, 2.0, ALU.is_ge, ALU.mult),
            e.scalar_tensor_tensor(mg[:, :], zsq[:, :], 4.0, mg[:, :],
                                   ALU.is_ge, ALU.add),
        ], waits=[("sSC", s1)], inc="sVE")
        s2 = S.op("scalar", lambda e: e.activation(mg[:, :], mg[:, :], AF.Identity,
                                                   scale=1572864.0, bias=sb["mgc"][:, :]),
                  waits=[("sVE", vab)], inc="sSC")
        vq = S.op("vector", lambda e: [
            e.tensor_add(z[:, :], z[:, :], mg[:, :]),
            e.tensor_sub(z[:, :], z[:, :], mg[:, :]),
            e.tensor_mul(r3(xd_bf[:, :]), r3(z[:, :]),
                         qt[:, b + 32:b + 64, None].broadcast_to([128, 32, 32])),
        ], waits=[("sSC", s2)] + list(vq_waits), inc="sVE")
        return vq

    def proj_mm(self, e, psum, x_fm, w_sb, bias_row, extra_rows=()):
        r = []
        for half in range(2):
            tgt = psum[:, half * 512:(half + 1) * 512]
            if bias_row is not None:
                r.append(e.matmul(tgt, self.sb["ones_rb"][0:1, 0:128],
                                  bias_row[0:1, half * 512:(half + 1) * 512],
                                  start=True, stop=False))
            for row in extra_rows:
                r.append(e.matmul(tgt, self.sb["ones_rb"][0:1, 0:128],
                                  row[0:1, half * 512:(half + 1) * 512],
                                  start=False, stop=False))
            for jc in range(8):
                first = jc == 0 and bias_row is None and not extra_rows
                r.append(e.matmul(tgt, x_fm[:, jc * 128:(jc + 1) * 128],
                                  w_sb[:, jc * D + half * 512: jc * D + (half + 1) * 512],
                                  start=first, stop=(jc == 7)))
        return r

    def tr8(self, src_bf, pT, dst_view_or_ap, waits, evac_waits=()):
        S, sb = self.S, self.sb
        t = S.op("tensor", lambda e: [
            e.transpose(pT[:, c * 128:(c + 1) * 128], src_bf[:, c * 128:(c + 1) * 128],
                        sb["ident"][:, :]) for c in range(8)],
            waits=waits, inc="sMM")
        ev = S.op("vector", lambda e: e.tensor_copy(dst_view_or_ap, pT[:, :]),
                  waits=[("sMM", t)] + list(evac_waits), inc="sVE")
        return t, ev

    def prep(self):
        nc, sb, dr, st = self.nc, self.sb, self.dram, self.stack
        self.alloc("wkT", [128, 8 * D], BF16, side="right")
        self.alloc("wvT", [128, 8 * D], BF16, side="right")
        self.begin()
        self.S.op("vector", lambda e: [
            e.memset(sb["epsc"][:, :], EPS),
            e.memset(sb["nepsc"][:, :], -EPS),
            e.memset(sb["mgc"][:, :], 1572864.0),
        ], inc="sVE")
        self.dma("gpsimd", sb["ident"][:, :], dr["ident"][:, :])
        self.dma("gpsimd", sb["ones_rb"][:, :], dr["ones_r"][:, :])
        self.dma("sync", sb["ones_rf"][:, :], dr["ones_r"][:, :])
        self.dma("gpsimd", sb["ones_cb"][:, :], dr["ones_c"][:, :])
        for jc in range(8):
            self.dma("gpsimd", sb["wkT"][:, jc * D:(jc + 1) * D], dr["wkT"][jc * 128:(jc + 1) * 128, :])
            self.dma("gpsimd", sb["wvT"][:, jc * D:(jc + 1) * D], dr["wvT"][jc * 128:(jc + 1) * 128, :])
        self.end()

    def p0a(self):
        nc, sb, dr, st = self.nc, self.sb, self.dram, self.stack
        self.alloc("K_fm", [128, 8 * NSEQ], BF16)
        self.alloc("V_tm", [128, NT_ALL * D], BF16)
        with ExitStack() as ph:
            T = lambda n, s, dt: ph.enter_context(nc.sbuf_tensor(n, s, dt))
            xin = [T(f"a_xin{p}", [128, D], F32) for p in range(2)]
            cosb = [T(f"a_cos{p}", [128, 512], BF16) for p in range(2)]
            sinb = [T(f"a_sin{p}", [128, 512], BF16) for p in range(2)]
            nbf = [T(f"a_nbf{p}", [128, D], BF16) for p in range(3)]
            nfm = [T(f"a_nfm{p}", [128, D], BF16) for p in range(3)]
            ktmb = [T(f"a_ktm{p}", [128, D], BF16) for p in range(2)]
            kraw = [T(f"a_kraw{p}", [128, D], BF16) for p in range(2)]
            pT = ph.enter_context(nc.psum_tensor("a_pT", [128, 1024], BF16))
            pT2 = ph.enter_context(nc.psum_tensor("a_pT2", [128, 1024], BF16))
            pp = [ph.enter_context(nc.psum_tensor(f"a_pp{i}", [128, 1024], F32)) for i in range(3)]
            self.begin()
            S = self.S
            lnapply = [0] * (NT_ALL + 2)
            ropeK = [0] * (NT_ALL + 2)
            evK = [0] * (NT_ALL + 2)
            evV = [0] * (NT_ALL + 2)
            nfm_ev = [0] * (NT_ALL + 2)
            kraw_ev = [0] * (NT_ALL + 2)
            ntr_l = [0] * (NT_ALL + 2)

            Kv = sb["K_fm"][:, :].rearrange("p (c t) -> p c t", t=NSEQ)
            def front(tt):
                p = tt % 2
                pn = tt % 3
                t0 = tt * 128
                x_in = self.dma("sync", xin[p][:, :], dr["x0"][t0:t0 + 128, :],
                                waits=[("sSC", lnapply[tt - 2])])
                cview = dr["costab"][:, :].rearrange("p (c t) -> p c t", t=NSEQ)[:, :, t0:t0 + 128]
                sview = dr["sintab"][:, :].rearrange("p (c t) -> p c t", t=NSEQ)[:, :, t0:t0 + 128]
                self.dma("gpsimd", cosb[p][:, :].rearrange("p (c t) -> p c t", t=128), cview,
                         waits=[("sVE", ropeK[tt - 2])])
                self.dma("gpsimd", sinb[p][:, :].rearrange("p (c t) -> p c t", t=128), sview)
                cos_in[tt] = self.ckpt("gpsimd")
                lnapply[tt] = self.ln_stats_apply(xin[p], nbf[pn], ("sIN", x_in), pn,
                                                  extra_waits=[("sMM", ntr_l[tt - 3])])
                ntr_l[tt] = S.op("tensor", lambda e, i=nbf[pn]: [
                    e.transpose(pT[:, c * 128:(c + 1) * 128], i[:, c * 128:(c + 1) * 128], sb["ident"][:, :])
                    for c in range(8)],
                    waits=[("sSC", lnapply[tt]), ("sVE", nfm_ev[tt - 1])], inc="sMM")
                nfm_ev[tt] = S.op("vector", lambda e, o=nfm[pn]: e.tensor_copy(o[:, :], pT[:, :]),
                                  waits=[("sMM", ntr_l[tt])], inc="sVE")

            def mmK(tt):
                pn = tt % 3
                rK = pp[(2 * tt) % 3]
                kmmK[tt] = S.op("tensor", lambda e, n=nfm[pn], rk=rK: [
                    mm for jc in range(8) for mm in (
                        e.matmul(rk[:, 0:512], n[:, jc * 128:(jc + 1) * 128],
                                 sb["wkT"][:, jc * D:jc * D + 512], start=(jc == 0), stop=(jc == 7)),
                        e.matmul(rk[:, 512:1024], n[:, jc * 128:(jc + 1) * 128],
                                 sb["wkT"][:, jc * D + 512:(jc + 1) * D], start=(jc == 0), stop=(jc == 7)),
                    )], waits=[("sVE", nfm_ev[tt]), ("sSC", evK[tt - 1])], inc="sMM")

            def mmV(tt):
                pn = tt % 3
                rV = pp[(2 * tt + 1) % 3]
                kmmV[tt] = S.op("tensor", lambda e, n=nfm[pn], rv=rV: [
                    mm for jc in range(8) for mm in (
                        e.matmul(rv[:, 0:512], n[:, jc * 128:(jc + 1) * 128],
                                 sb["wvT"][:, jc * D:jc * D + 512], start=(jc == 0), stop=(jc == 7)),
                        e.matmul(rv[:, 512:1024], n[:, jc * 128:(jc + 1) * 128],
                                 sb["wvT"][:, jc * D + 512:(jc + 1) * D], start=(jc == 0), stop=(jc == 7)),
                    )], waits=[("sVE", nfm_ev[tt]), ("sSC", evV[tt - 1])], inc="sMM")

            def backS(tt):
                if tt < 0:
                    return
                p = tt % 2
                rK = pp[(2 * tt) % 3]
                rV = pp[(2 * tt + 1) % 3]
                evK[tt] = S.op("scalar", lambda e, o=ktmb[p], rk=rK, c_=tt: [
                    e.activation(o[:, 0:512], rk[:, 0:512], AF.Copy),
                    e.activation(o[:, 512:1024], rk[:, 512:1024], AF.Copy),
                    e.activation(sb["sq"][:, :], o[:, :], AF.Square,
                                 accum_out=sb["knorm"][:, c_:c_ + 1]),
                ], waits=[("sMM", kmmK[tt])], inc="sSC")
                evV[tt] = S.op("scalar", lambda e, rv=rV, c_=tt: [
                    e.activation(sb["V_tm"][:, c_ * D:c_ * D + 512], rv[:, 0:512], AF.Copy),
                    e.activation(sb["V_tm"][:, c_ * D + 512:(c_ + 1) * D], rv[:, 512:1024], AF.Copy),
                ], waits=[("sMM", kmmV[tt])], inc="sSC")

            def backT(tt):
                if tt < 0:
                    return
                p = tt % 2
                ktr = S.op("tensor", lambda e, i=ktmb[p]: [
                    e.transpose(pT2[:, c * 128:(c + 1) * 128], i[:, c * 128:(c + 1) * 128], sb["ident"][:, :])
                    for c in range(8)],
                    waits=[("sSC", evK[tt]), ("sVE", kraw_ev[tt - 1])], inc="sMM")
                ktr_l[tt] = ktr

            def backV(tt):
                if tt < 0:
                    return
                p = tt % 2
                t0 = tt * 128
                kraw_ev[tt] = S.op("vector", lambda e, o=kraw[p]: e.tensor_copy(o[:, :], pT2[:, :]),
                                   waits=[("sMM", ktr_l[tt])], inc="sVE")
                ropeK[tt] = self.rope(kraw[p], cosb[p], sinb[p],
                                      Kv[:, 0:4, t0:t0 + 128], Kv[:, 4:8, t0:t0 + 128],
                                      waits=[("sVE", kraw_ev[tt]), ("sGD", cos_in[tt])])

            kmmK = [0] * (NT_ALL + 2)
            kmmV = [0] * (NT_ALL + 2)
            ktr_l = [0] * (NT_ALL + 2)
            cos_in = [0] * (NT_ALL + 2)
            front(0)
            for tt in range(NT_ALL):
                mmK(tt)
                backS(tt - 1)
                backT(tt - 1)
                mmV(tt)
                backV(tt - 1)
                if tt + 1 < NT_ALL:
                    front(tt + 1)
            backS(NT_ALL - 1)
            backT(NT_ALL - 1)
            backV(NT_ALL - 1)
            self.end()

    def p0b(self):
        nc, sb, dr, st = self.nc, self.sb, self.dram, self.stack
        with ExitStack() as ph:
            T = lambda n, s, dt: ph.enter_context(nc.sbuf_tensor(n, s, dt))
            xin = [T(f"b_xin{p}", [128, D], F32) for p in range(2)]
            cosb = [T(f"b_cos{p}", [128, 512], BF16) for p in range(2)]
            sinb = [T(f"b_sin{p}", [128, 512], BF16) for p in range(2)]
            nbf = [T(f"b_nbf{p}", [128, D], BF16) for p in range(2)]
            nfm = [T(f"b_nfm{p}", [128, D], BF16) for p in range(2)]
            qtmb = [T(f"b_qtm{p}", [128, D], BF16) for p in range(2)]
            qraw = [T(f"b_qraw{p}", [128, D], BF16) for p in range(2)]
            qfmt = [T(f"b_qfmt{p}", [128, D], BF16) for p in range(2)]
            pTn = [ph.enter_context(nc.psum_tensor(f"b_pTn{i}", [128, 1024], BF16)) for i in range(2)]
            pT2 = ph.enter_context(nc.psum_tensor("b_pT2", [128, 1024], BF16))
            pT = pTn[0]
            pp = [ph.enter_context(nc.psum_tensor(f"b_pp{i}", [128, 1024], F32)) for i in range(2)]
            self.begin()
            S = self.S
            lnapply = [0] * (NT_OWN + 2)
            ropeQ = [0] * (NT_OWN + 2)
            evQ = [0] * (NT_OWN + 2)
            nfm_ev = [0] * (NT_OWN + 2)
            qraw_ev = [0] * (NT_OWN + 2)
            spill = [0] * (NT_OWN + 2)
            ntr_l = [0] * (NT_OWN + 2)
            qmm_l = [0] * (NT_OWN + 2)
            cos_in = [0] * (NT_OWN + 2)

            for s in range(NT_OWN + 1):
                if s < NT_OWN:
                    p = s % 2
                    t0 = s * 128
                    x_in = self.dma("sync", xin[p][:, :], dr["x0own"][t0:t0 + 128, :],
                                    waits=[("sSC", lnapply[s - 2])])
                    cview = dr["costab_own"][:, :].rearrange("p (c t) -> p c t", t=2048)[:, :, t0:t0 + 128]
                    sview = dr["sintab_own"][:, :].rearrange("p (c t) -> p c t", t=2048)[:, :, t0:t0 + 128]
                    self.dma("gpsimd", cosb[p][:, :].rearrange("p (c t) -> p c t", t=128), cview,
                             waits=[("sVE", ropeQ[s - 2])])
                    self.dma("gpsimd", sinb[p][:, :].rearrange("p (c t) -> p c t", t=128), sview)
                    cos_in[s] = self.ckpt("gpsimd")
                    lnapply[s] = self.ln_stats_apply(xin[p], nbf[p], ("sIN", x_in), p,
                                                     extra_waits=[("sMM", ntr_l[s - 2])])
                    ntr_l[s] = ntr = S.op("tensor", lambda e, i=nbf[p], b=pTn[p]: [
                        e.transpose(b[:, c * 128:(c + 1) * 128], i[:, c * 128:(c + 1) * 128], sb["ident"][:, :])
                        for c in range(8)],
                        waits=[("sSC", lnapply[s]), ("sVE", nfm_ev[s - 2])], inc="sMM")
                    nfm_ev[s] = S.op("vector", lambda e, o=nfm[p], b=pTn[p]: e.tensor_copy(o[:, :], b[:, :]),
                                     waits=[("sMM", ntr)], inc="sVE")
                    rQ = pp[s % 2]

                    def mmQ(e, n=nfm[p], rq=rQ):
                        r = []
                        for jc in range(8):
                            st_, sp = jc == 0, jc == 7
                            lhs = n[:, jc * 128:(jc + 1) * 128]
                            r.append(e.matmul(rq[:, 0:512], lhs, sb["wqT"][:, jc * D:jc * D + 512], start=st_, stop=sp))
                            r.append(e.matmul(rq[:, 512:1024], lhs, sb["wqT"][:, jc * D + 512:(jc + 1) * D], start=st_, stop=sp))
                        return r
                    qmm_l[s] = qmm = S.op("tensor", mmQ, waits=[("sVE", nfm_ev[s]), ("sSC", evQ[s - 2]),
                                                                ("sGP", self.wq_loaded)], inc="sMM")
                    evQ[s] = S.op("scalar", lambda e, o=qtmb[p], rq=rQ, c_=s: [
                        e.activation(o[:, 0:512], rq[:, 0:512], AF.Copy),
                        e.activation(o[:, 512:1024], rq[:, 512:1024], AF.Copy),
                        e.activation(sb["sq"][:, :], o[:, :], AF.Square, accum_out=sb["qnorm"][:, c_:c_ + 1]),
                    ], waits=[("sMM", qmm)], inc="sSC")
                if s - 1 >= 0:
                    bs = s - 1
                    pb = bs % 2
                    tb0 = bs * 128
                    qtr = S.op("tensor", lambda e, i=qtmb[pb]: [
                        e.transpose(pT2[:, c * 128:(c + 1) * 128], i[:, c * 128:(c + 1) * 128], sb["ident"][:, :])
                        for c in range(8)],
                        waits=[("sSC", evQ[bs]), ("sVE", qraw_ev[bs - 1])], inc="sMM")
                    qraw_ev[bs] = S.op("vector", lambda e, o=qraw[pb]: e.tensor_copy(o[:, :], pT2[:, :]),
                                       waits=[("sMM", qtr)], inc="sVE")
                    qv = qfmt[pb][:, :].rearrange("p (c t) -> p c t", t=128)
                    ropeQ[bs] = self.rope(qraw[pb], cosb[pb], sinb[pb], qv[:, 0:4, :], qv[:, 4:8, :],
                                          waits=[("sVE", qraw_ev[bs]), ("sGD", cos_in[bs]),
                                                 ("sGP", spill[bs - 2])])
                    qview = dr["qfm_s"][:, :].rearrange("p (c t) -> p c t", t=2048)[:, :, tb0:tb0 + 128]
                    spill[bs] = self.dma("gpsimd", qview,
                                         qfmt[pb][:, :].rearrange("p (c t) -> p c t", t=128),
                                         waits=[("sVE", ropeQ[bs])])
            vmax = S.op("vector", lambda e: [
                e.reduce_max(sb["tiny"][:, 0:1], sb["knorm"][:, :], axis=AX.X),
                e.reduce_max(sb["tiny"][:, 1:2], sb["qnorm"][:, :], axis=AX.X),
            ], waits=[("sSC", evQ[NT_OWN - 1])], inc="sVE")
            scast = S.op("scalar", lambda e: e.activation(sb["rsc"][:, 0:2], sb["tiny"][:, 0:2], AF.Copy),
                         waits=[("sVE", vmax)], inc="sSC")
            ttr = S.op("tensor", lambda e: e.transpose(pT[0:2, 0:128], sb["rsc"][:, 0:2], sb["ident"][:, :]),
                       waits=[("sSC", scast), ("sVE", qraw_ev[NT_OWN - 1])], inc="sMM")
            vred = S.op("vector", lambda e: e.reduce_max(sb["tiny"][0:2, 3:4], pT[0:2, 0:128], axis=AX.X),
                        waits=[("sMM", ttr)], inc="sVE")
            scast2 = S.op("scalar", lambda e: e.activation(sb["rsc"][0:2, 2:3], sb["tiny"][0:2, 3:4], AF.Copy),
                          waits=[("sVE", vred)], inc="sSC")
            ttr2 = S.op("tensor", lambda e: e.transpose(pT[0:1, 128:130], sb["rsc"][0:2, 2:3], sb["ident"][0:2, 0:2]),
                        waits=[("sSC", scast2)], inc="sMM")
            vcp = S.op("vector", lambda e: e.tensor_copy(sb["tiny"][0:1, 9:11], pT[0:1, 128:130]),
                       waits=[("sMM", ttr2)], inc="sVE")
            vprod = S.op("vector", lambda e: e.tensor_mul(
                sb["tiny"][0:1, 3:4], sb["tiny"][0:1, 9:10], sb["tiny"][0:1, 10:11]),
                waits=[("sVE", vcp)], inc="sVE")
            sneg = S.op("scalar", lambda e: [
                e.activation(sb["tiny"][0:1, 4:5], sb["tiny"][0:1, 3:4], AF.Ln),
                e.activation(sb["tiny"][0:1, 5:6], sb["tiny"][0:1, 4:5], AF.Exp, scale=0.5),
            ], waits=[("sVE", vprod)], inc="sSC")
            vneg = S.op("vector", lambda e: e.tensor_scalar_mul(
                sb["tiny"][0:1, 6:7], sb["tiny"][0:1, 5:6], -1.05), waits=[("sSC", sneg)], inc="sVE")
            S.op("scalar", lambda e: e.activation(sb["negm_row"][:, :], sb["ones_rf"][0:1, 0:256],
                                                  AF.Identity, scale=sb["tiny"][0:1, 6:7]),
                 waits=[("sVE", vneg)], inc="sSC")
            if self.dbg == "p0":
                for c in range(8):
                    self.dma("gpsimd", dr["dbg_kfm"][:, c * NSEQ:(c + 1) * NSEQ],
                             sb["K_fm"][:, c * NSEQ:(c + 1) * NSEQ],
                             waits=[("sVE", self.C["sVE"]), ("sSC", self.C["sSC"])])
                    self.dma("gpsimd", dr["dbg_vtm"][:, c * NSEQ:(c + 1) * NSEQ],
                             sb["V_tm"][:, c * NSEQ:(c + 1) * NSEQ])
                self.dma("gpsimd", dr["dbg_negm"][:, :], sb["negm_row"][:, :])
                self.dma("gpsimd", dr["dbg_tiny"][:, :], sb["tiny"][:, :])
                self.dma("gpsimd", dr["dbg_kn"][:, :], sb["knorm"][:, :])
                self.dma("gpsimd", dr["dbg_qn"][:, :], sb["qnorm"][:, :])
            self.end()

    def p1(self):
        nc, sb, dr, st = self.nc, self.sb, self.dram, self.stack
        with ExitStack() as ph:
            T = lambda n, s, dt: ph.enter_context(nc.sbuf_tensor(n, s, dt))
            qg = [T(f"c_qg{p}", [128, 2048], BF16) for p in range(2)]
            maskb = [T(f"c_mask{p}", [128, 1024], F32) for p in range(2)]
            x0t = [T(f"c_x0t{p}", [128, D], F32) for p in range(2)]
            x1t = [T(f"c_x1t{p}", [128, D], F32) for p in range(2)]
            PT = [T(f"c_PT{p}", [128, 256], BF16) for p in range(3)]
            rcp = T("c_rcp", [128, 2], F32)
            pS = [ph.enter_context(nc.psum_tensor(f"c_pS{i}", [128, 512], F32)) for i in range(3)]
            av = [ph.enter_context(nc.psum_tensor(f"c_av{i}", [128, 512], F32)) for i in range(4)]
            pden = ph.enter_context(nc.psum_tensor("c_pden", [128, 512], F32))
            self.begin()
            S = self.S
            Kv = sb["K_fm"]
            n = 0
            expd = {}
            avmm = {}
            qg_in = [0, 0]
            mk_in = [0, 0]
            qg_lastuse = [0, 0]
            mk_lastuse = [0, 0]
            x0_in = [0, 0]
            epi_v = 0
            epi_prev = 0
            x1_store = [0] * (NT_OWN + 2)
            recip_done = 0

            for g in range(8):
                gp = g % 2
                L = 4 * g + 4
                qview = dr["qfm_s"][:, :].rearrange("p (c t) -> p c t", t=2048)[:, :, g * 256:(g + 1) * 256]
                qg_in[gp] = self.dma("sync", qg[gp][:, :].rearrange("p (c t) -> p c t", t=256),
                                     qview, waits=[("sMM", qg_lastuse[gp])])
                mk_in[gp] = self.dma("sync", maskb[gp][:, :], dr["p1mask"][g, :, :],
                                     waits=[("sVE", mk_lastuse[gp])])
                for qc in range(2):
                    slot = 2 * g + qc
                    self.dma("sync", x0t[qc][:, :],
                             dr["x0own"][slot * 128:(slot + 1) * 128, :],
                             waits=[("sVE", epi_prev)])
                grp_rdy = self.ckpt("sync")
                x0_in[0] = x0_in[1] = grp_rdy
                for kt in range(L):
                    p = n % 3
                    def mmS(e, g_=g, kt_=kt, p_=p, qb=qg[gp]):
                        r = []
                        for dch in range(8):
                            r.append(e.matmul(
                                pS[p_][:, 0:256],
                                Kv[:, dch * NSEQ + kt_ * 128: dch * NSEQ + (kt_ + 1) * 128],
                                qb[:, dch * 256:(dch + 1) * 256],
                                start=(dch == 0), stop=False))
                        r.append(e.matmul(pS[p_][:, 0:256], sb["ones_rb"][0:1, 0:128],
                                          sb["negm_row"][0:1, 0:256], start=False, stop=True))
                        return r
                    smm = S.op("tensor", mmS,
                               waits=[("sRD", grp_rdy), ("sSC", expd.get(n - 3, 0))],
                               inc="sMM")
                    qg_lastuse[gp] = smm
                    m = kt - 4 * g
                    if m >= 0:
                        maskd = S.op("vector", lambda e, p_=p, m_=m, gp_=gp: e.tensor_add(
                            pS[p_][:, 0:256], pS[p_][:, 0:256], maskb[gp_][:, m_ * 256:(m_ + 1) * 256]),
                            waits=[("sMM", smm), ("sRD", grp_rdy)], inc="sVE")
                        mk_lastuse[gp] = maskd
                        ew = [("sMM", smm), ("sVE", maskd), ("sMM", avmm.get(n - 3, 0))]
                    else:
                        ew = [("sMM", smm), ("sMM", avmm.get(n - 3, 0))]
                    expd[n] = S.op("scalar", lambda e, p_=p: e.activation(
                        PT[p_][:, :], pS[p_][:, 0:256], AF.Exp, scale=1.0 / 32.0), waits=ew, inc="sSC")
                    def mmAV(e, kt_=kt, p_=p, st_=(kt == 0), sp_=(kt == L - 1)):
                        r = []
                        for qc in range(2):
                            for ec in range(2):
                                r.append(e.matmul(
                                    av[qc * 2 + ec][:, 0:512],
                                    PT[p_][:, qc * 128:(qc + 1) * 128],
                                    sb["V_tm"][:, kt_ * D + ec * 512: kt_ * D + (ec + 1) * 512],
                                    start=st_, stop=sp_))
                        for qc in range(2):
                            r.append(e.matmul(pden[:, qc:qc + 1], PT[p_][:, qc * 128:(qc + 1) * 128],
                                              sb["ones_cb"][:, :],
                                              start=(st_ and qc == 0), stop=(sp_ and qc == 1)))
                        return r
                    w = [("sSC", expd[n])]
                    if kt == 0:
                        w += [("sVE", epi_v), ("sVE", recip_done)]
                    avmm[n] = S.op("tensor", mmAV, waits=w, inc="sMM")
                    n += 1
                recip_done = S.op("vector", lambda e: e.reciprocal(rcp[:, 0:2], pden[:, 0:2]),
                                  waits=[("sMM", avmm[n - 1])], inc="sVE")
                for qc in range(2):
                    slot = 2 * g + qc
                    epi = S.op("vector", lambda e, qc_=qc: [
                        e.scalar_tensor_tensor(x1t[qc_][:, ec * 512:(ec + 1) * 512],
                                               av[qc_ * 2 + ec][:, 0:512], rcp[:, qc_:qc_ + 1],
                                               x0t[qc_][:, ec * 512:(ec + 1) * 512],
                                               ALU.mult, ALU.add)
                        for ec in range(2)],
                        waits=[("sVE", recip_done), ("sRD", x0_in[qc]),
                               ("sGP", x1_store[slot - 2])], inc="sVE")
                    x1_store[slot] = self.dma("gpsimd", dr["x1s"][slot * 128:(slot + 1) * 128, :],
                                              x1t[qc][:, :], waits=[("sVE", epi)])
                    epi_v = epi
                epi_prev = epi_v
            self.end()

    def p2w(self):
        nc, sb, dr = self.nc, self.sb, self.dram
        self.alloc("wdqT", [128, 8 * D], BF16, side="right")
        self.alloc("mxb_bf", [1, D], BF16)
        with ExitStack() as ph:
            T = lambda n, s, dt: ph.enter_context(nc.sbuf_tensor(n, s, dt))
            wbuf = [T(f"w_wbuf{p}", [128, D], F32) for p in range(2)]
            wq_bf = [T(f"w_wq{p}", [128, D], BF16) for p in range(2)]
            z = [T(f"w_z{i}", [128, D], F32) for i in range(2)]
            zsq = [T(f"w_zsq{i}", [128, D], F32) for i in range(2)]
            mg = [T(f"w_mg{i}", [128, D], F32) for i in range(2)]
            mxbf = T("w_mxbf", [1, D], F32)
            pT = ph.enter_context(nc.psum_tensor("w_pT", [128, 1024], BF16))
            self.begin()
            S = self.S
            self.dma("sync", mxbf[:, :], dr["mxb"][:, :])
            rdy0 = self.ckpt("sync")
            S.op("scalar", lambda e: e.activation(sb["mxb_bf"][:, :], mxbf[:, :], AF.Copy),
                 waits=[("sRD", rdy0)], inc="sSC")
            vq_l = [0] * (8 + 2)
            tr_l = [0] * (8 + 2)
            ev_prev = 0
            wv = sb["wdqT"][:, :].rearrange("p (c t) -> p c t", t=D)
            for wt in range(8):
                p = wt % 2
                self.dma("sync", wbuf[p][:, :], dr["mxw"][wt * 128:(wt + 1) * 128, :],
                         waits=[("sVE", vq_l[wt - 2])])
                rdy = self.ckpt("sync")
                vq_l[wt] = self.quant(wbuf[p], wq_bf[p], z[p], zsq[p], mg[p], ("sRD", rdy), p,
                                      vq_waits=[("sMM", tr_l[wt - 2])])
                tr_l[wt], ev_prev = self.tr8(wq_bf[p], pT, wv[:, :, wt * 128:(wt + 1) * 128],
                                             waits=[("sVE", vq_l[wt]), ("sVE", ev_prev)])
            self.end()

    def p2(self):
        nc, sb, dr = self.nc, self.sb, self.dram
        with ExitStack() as ph:
            T = lambda n, s, dt: ph.enter_context(nc.sbuf_tensor(n, s, dt))
            x1t = [T(f"f_x1{p}", [128, D], F32) for p in range(3)]
            ht = [T(f"f_h{p}", [128, D], F32) for p in range(3)]
            xd = [T(f"f_xd{p}", [128, D], BF16) for p in range(2)]
            xdfm = [T(f"f_xdfm{p}", [128, D], BF16) for p in range(2)]
            x2t = [T(f"f_x2{p}", [128, D], F32) for p in range(2)]
            z = [T(f"f_z{i}", [128, D], F32) for i in range(2)]
            zsq = [T(f"f_zsq{i}", [128, D], F32) for i in range(2)]
            mg = [T(f"f_mg{i}", [128, D], F32) for i in range(2)]
            pT = ph.enter_context(nc.psum_tensor("f_pT", [128, 1024], BF16))
            pp = [ph.enter_context(nc.psum_tensor(f"f_pp{i}", [128, 1024], F32)) for i in range(2)]
            self.begin()
            S = self.S
            lnap = [0] * (NT_OWN + 2)
            vq_l = [0] * (NT_OWN + 2)
            trx = [0] * (NT_OWN + 2)
            fmev = [0] * (NT_OWN + 2)
            mmv = [0] * (NT_OWN + 2)
            epi = [0] * (NT_OWN + 2)
            st_l = [0] * (NT_OWN + 2)
            def front(s):
                p3 = s % 3
                p = s % 2
                p4 = s % 4
                self.dma("sync", x1t[p3][:, :], dr["x1s"][s * 128:(s + 1) * 128, :],
                         waits=[("sVE", epi[s - 3])])
                rdy = self.ckpt("sync")
                lnap[s] = self.ln_stats_apply(x1t[p3], ht[p3], ("sRD", rdy), p4,
                                              extra_waits=[("sVE", vq_l[s - 3])],
                                              sum_on_scalar=True)
                vq_l[s] = self.quant(ht[p3], xd[p], z[p], zsq[p], mg[p], ("sSC", lnap[s]), p4,
                                     vq_waits=[("sMM", trx[s - 2])])
                trx[s], fmev[s] = self.tr8(xd[p], pT, xdfm[p][:, :],
                                           waits=[("sVE", vq_l[s]), ("sVE", fmev[s - 1])],
                                           evac_waits=[("sMM", mmv[s - 1])])
                mmv[s] = S.op("tensor", lambda e, xf=xdfm[p], rp=pp[s % 2]: self.proj_mm(
                    e, rp, xf, sb["wdqT"], sb["mxb_bf"]),
                    waits=[("sVE", fmev[s]), ("sVE", epi[s - 2])], inc="sMM")

            def back(s):
                if s < 0:
                    return
                p = s % 2
                p3 = s % 3
                epi[s] = S.op("vector", lambda e, p_=p, p3_=p3, rp=pp[s % 2]: [
                    e.tensor_add(x2t[p_][:, 0:512], rp[:, 0:512], x1t[p3_][:, 0:512]),
                    e.tensor_add(x2t[p_][:, 512:1024], rp[:, 512:1024], x1t[p3_][:, 512:1024]),
                ], waits=[("sMM", mmv[s]), ("sGP", st_l[s - 2])], inc="sVE")
                st_l[s] = self.dma("gpsimd", dr["x2s"][s * 128:(s + 1) * 128, :], x2t[p][:, :],
                                   waits=[("sVE", epi[s])])

            front(0)
            for s in range(NT_OWN):
                if s + 1 < NT_OWN:
                    front(s + 1)
                back(s)
            self.end()

    def p3w(self):
        nc, sb, dr = self.nc, self.sb, self.dram
        self.alloc("bo_bf", [1, D], BF16)
        self.alloc("bf2_bf", [1, D], BF16)
        self.alloc("bd_bf", [1, D], BF16)
        self.alloc("Wff1", [128, 8 * 4 * D], BF16)
        self.alloc("negmk_row", [1, D], BF16)
        self.alloc("bq_r", [1, D], BF16, side="right")
        self.alloc("bk_r", [1, D], BF16, side="right")
        self.alloc("bv_r", [1, D], BF16, side="right")
        self.alloc("wo_bf", [128, 8 * D], BF16, side="right")
        self.alloc("Wq_b", [128, 8 * D], BF16, side="right")
        self.alloc("Wk_b", [128, 8 * D], BF16, side="right")
        self.alloc("Wv_b", [128, 8 * D], BF16, side="right")
        self.alloc("b1row", [1, 4 * D], BF16, side="right")
        with ExitStack() as ph:
            T = lambda n, s, dt: ph.enter_context(nc.sbuf_tensor(n, s, dt))
            stage = [T(f"g_st{p}", [128, D], F32) for p in range(2)]
            raw_bf = [T(f"g_raw{p}", [128, D], BF16) for p in range(2)]
            g1col = T("g_g1", [128, 8], F32)
            g2col = T("g_g2", [128, 8], F32)
            b1bf = T("g_b1bf", [128, 8], BF16)
            b2bf = T("g_b2bf", [128, 8], BF16)
            sqb = T("g_sqb", [128, D], BF16)
            rowf = T("g_rowf", [1, D], F32)
            rowf2 = T("g_rowf2", [1, D], F32)
            self.begin()
            S = self.S
            self.dma("sync", g1col[:, :], dr["g1c"][:, :])
            self.dma("sync", g2col[:, :], dr["g2c"][:, :])
            self.dma("gpsimd", sb["bo_bf"][:, :], dr["bo_r"][:, :])
            self.dma("gpsimd", sb["bf2_bf"][:, :], dr["bf2_r"][:, :])
            self.dma("gpsimd", sb["bd_bf"][:, :], dr["bd_r"][:, :])
            self.dma("gpsimd", b1bf[:, :], dr["b1c"][:, :])
            self.dma("gpsimd", b2bf[:, :], dr["b2c"][:, :])
            bcast_rdy = self.ckpt("gpsimd")
            for jc in range(8):
                self.dma("gpsimd", sb["wo_bf"][:, jc * D:(jc + 1) * D],
                         dr["bwo"][jc * 128:(jc + 1) * 128, :])

            with ExitStack() as pA:
                pRow = [pA.enter_context(nc.psum_tensor(f"g_pR{i}", [1, 512], F32)) for i in range(2)]
                pCS = [pA.enter_context(nc.psum_tensor(f"g_pC{i}", [1, 512], F32)) for i in range(2)]

                def wprep(wname, dest, brow, colsq):
                    last_mm = 0
                    last_raw = [0, 0]
                    cs = 0
                    for jc in range(8):
                        p = jc % 2
                        self.dma("sync", stage[p][:, :], dr[wname][jc * 128:(jc + 1) * 128, :],
                                 waits=[("sSC", last_raw[p])])
                        rd = self.ckpt("sync")
                        rc = S.op("scalar", lambda e, st_=stage[p], rb=raw_bf[p]: e.activation(
                            rb[:, :], st_[:, :], AF.Copy),
                            waits=[("sRD", rd), ("sMM", last_mm)], inc="sSC")
                        last_mm = S.op("tensor", lambda e, jc_=jc, rb=raw_bf[p]: [
                            e.matmul(pRow[h][0:1, 0:512], b1bf[:, jc_:jc_ + 1],
                                     rb[:, h * 512:(h + 1) * 512],
                                     start=(jc_ == 0), stop=(jc_ == 7))
                            for h in range(2)],
                            waits=[("sSC", rc), ("sGD", bcast_rdy)], inc="sMM")
                        gsc = S.op("scalar", lambda e, st_=stage[p], jc_=jc: e.activation(
                            dest[:, jc_ * D:(jc_ + 1) * D], st_[:, :], AF.Identity,
                            scale=g1col[:, jc_:jc_ + 1]),
                            waits=[("sRD", rd)], inc="sSC")
                        last_raw[p] = gsc
                        if colsq:
                            sq2 = S.op("scalar", lambda e, jc_=jc: e.activation(
                                sqb[:, :], dest[:, jc_ * D:(jc_ + 1) * D], AF.Square),
                                waits=[("sSC", gsc)], inc="sSC")
                            cs = S.op("tensor", lambda e, jc_=jc: [
                                e.matmul(pCS[h][0:1, 0:512], sb["ones_cb"][:, :],
                                         sqb[:, h * 512:(h + 1) * 512],
                                         start=(jc_ == 0), stop=(jc_ == 7))
                                for h in range(2)],
                                waits=[("sSC", sq2)], inc="sMM")
                    ev = S.op("vector", lambda e: [
                        e.tensor_copy(brow[0:1, 0:512], pRow[0][0:1, 0:512]),
                        e.tensor_copy(brow[0:1, 512:1024], pRow[1][0:1, 0:512]),
                    ], waits=[("sMM", last_mm)], inc="sVE")
                    return ev, cs

                evq, _ = wprep("bwq", sb["Wq_b"], sb["bq_r"], False)
                evk, cs_done = wprep("bwk", sb["Wk_b"], sb["bk_r"], True)
                evv, _ = wprep("bwv", sb["Wv_b"], sb["bv_r"], False)
                vcs = S.op("vector", lambda e: [
                    e.tensor_copy(rowf2[0:1, 0:512], pCS[0][0:1, 0:512]),
                    e.tensor_copy(rowf2[0:1, 512:1024], pCS[1][0:1, 0:512]),
                ], waits=[("sMM", cs_done)], inc="sVE")
                vm1 = S.op("vector", lambda e: e.reduce_max(sb["tiny"][0:1, 0:1], rowf2[0:1, 0:1024],
                                                            axis=AX.X), waits=[("sVE", vcs)], inc="sVE")
                vm2 = S.op("vector", lambda e: e.reduce_max(sb["tiny"][0:1, 1:2], sb["bk_r"][0:1, :],
                                                            axis=AX.X, apply_absolute_value=True),
                           waits=[("sVE", evk)], inc="sVE")
                sm1 = S.op("scalar", lambda e: [
                    e.activation(sb["tiny"][0:1, 2:3], sb["tiny"][0:1, 0:1], AF.Ln),
                    e.activation(sb["tiny"][0:1, 3:4], sb["tiny"][0:1, 2:3], AF.Exp, scale=0.5),
                ], waits=[("sVE", vm1)], inc="sSC")
                vm3 = S.op("vector", lambda e: e.tensor_scalar_mul(
                    sb["tiny"][0:1, 4:5], sb["tiny"][0:1, 3:4], 32.1),
                    waits=[("sSC", sm1)], inc="sVE")
                vm4 = S.op("vector", lambda e: e.tensor_add(
                    sb["tiny"][0:1, 5:6], sb["tiny"][0:1, 4:5], sb["tiny"][0:1, 1:2]),
                    waits=[("sVE", vm3), ("sVE", vm2)], inc="sVE")
                vm5 = S.op("vector", lambda e: e.tensor_scalar_mul(
                    sb["tiny"][0:1, 6:7], sb["tiny"][0:1, 5:6], -1.02),
                    waits=[("sVE", vm4)], inc="sVE")
                S.op("scalar", lambda e: e.activation(sb["negmk_row"][:, :], sb["ones_rf"][0:1, :],
                                                      AF.Identity, scale=sb["tiny"][0:1, 6:7]),
                     waits=[("sVE", vm5)], inc="sSC")
            with ExitStack() as pB:
                pFF = [pB.enter_context(nc.psum_tensor(f"g_pF{i}", [1, 512], F32)) for i in range(8)]
                last_mm = 0
                last_raw = [0, 0]
                for jc in range(8):
                    for qtr in range(4):
                        p = (jc * 4 + qtr) % 2
                        self.dma("sync", stage[p][:, :],
                                 dr["bff1"][jc * 128:(jc + 1) * 128, qtr * D:(qtr + 1) * D],
                                 waits=[("sSC", last_raw[p])])
                        rd = self.ckpt("sync")
                        rc = S.op("scalar", lambda e, st_=stage[p], rb=raw_bf[p]: e.activation(
                            rb[:, :], st_[:, :], AF.Copy),
                            waits=[("sRD", rd), ("sMM", last_mm)], inc="sSC")
                        last_mm = S.op("tensor", lambda e, jc_=jc, q_=qtr, rb=raw_bf[p]: [
                            e.matmul(pFF[q_ * 2 + h][0:1, 0:512], b2bf[:, jc_:jc_ + 1],
                                     rb[:, h * 512:(h + 1) * 512],
                                     start=(jc_ == 0), stop=(jc_ == 7))
                            for h in range(2)],
                            waits=[("sSC", rc)], inc="sMM")
                        last_raw[p] = S.op("scalar", lambda e, st_=stage[p], jc_=jc, q_=qtr: e.activation(
                            sb["Wff1"][:, jc_ * 4 * D + q_ * D: jc_ * 4 * D + (q_ + 1) * D],
                            st_[:, :], AF.Identity, scale=g2col[:, jc_:jc_ + 1]),
                            waits=[("sRD", rd)], inc="sSC")
                prev_cast = 0
                for i in range(4):
                    self.dma("sync", rowf[0:1, :], dr["bf1_r"][0:1, i * D:(i + 1) * D],
                             waits=[("sSC", prev_cast)])
                    rdf = self.ckpt("sync")
                    vr = S.op("vector", lambda e, i_=i: [
                        e.tensor_copy(rowf2[0:1, 0:512], pFF[2 * i_][0:1, 0:512]),
                        e.tensor_copy(rowf2[0:1, 512:1024], pFF[2 * i_ + 1][0:1, 0:512]),
                    ], waits=[("sMM", last_mm), ("sSC", prev_cast)], inc="sVE")
                    vr2 = S.op("vector", lambda e: e.tensor_add(rowf2[0:1, :], rowf2[0:1, :],
                                                                rowf[0:1, :]),
                               waits=[("sVE", vr), ("sRD", rdf)], inc="sVE")
                    prev_cast = S.op("scalar", lambda e, i_=i: e.activation(
                        sb["b1row"][0:1, i_ * D:(i_ + 1) * D], rowf2[0:1, :], AF.Copy),
                        waits=[("sVE", vr2)], inc="sSC")
                self.dma("gpsimd", dr["b1row_s"][:, :], sb["b1row"][0:1, :],
                         waits=[("sSC", prev_cast)])
            self.end()

    def p3a(self):
        nc, sb, dr = self.nc, self.sb, self.dram
        with ExitStack() as ph:
            T = lambda n, s, dt: ph.enter_context(nc.sbuf_tensor(n, s, dt))
            x2t = [T(f"h_x2{p}", [128, D], F32) for p in range(2)]
            n2b = [T(f"h_n2{p}", [128, D], BF16) for p in range(2)]
            n2fm = [T(f"h_n2fm{p}", [128, D], BF16) for p in range(2)]
            qet = [T(f"h_qe{p}", [128, D], BF16) for p in range(2)]
            qefm = [T(f"h_qefm{p}", [128, D], BF16) for p in range(2)]
            ekt = [T(f"h_ek{p}", [128, D], BF16) for p in range(2)]
            v2b = [T(f"h_v2{p}", [128, D], BF16) for p in range(2)]
            U_sb = T("h_Usb", [128, 2056], F32)
            pT = ph.enter_context(nc.psum_tensor("h_pT", [128, 1024], BF16))
            pP = ph.enter_context(nc.psum_tensor("h_pP", [128, 1024], F32))
            Ub = [ph.enter_context(nc.psum_tensor(f"h_U{i}", [128, 512], F32)) for i in range(4)]
            Ucol = ph.enter_context(nc.psum_tensor("h_Uc", [128, 512], F32))
            self.begin()
            S = self.S
            lnap = [0] * (NT_OWN + 2)
            trn = [0] * (NT_OWN + 2)
            fmev = [0] * (NT_OWN + 2)
            qev = [0] * (NT_OWN + 2)
            qtr_l = [0] * (NT_OWN + 2)
            qfm_ev = [0] * (NT_OWN + 2)
            spl = [0] * (NT_OWN + 2)
            ekev = [0] * (NT_OWN + 2)
            vev = [0] * (NT_OWN + 2)
            umm = [0] * (NT_OWN + 2)
            vrq_l = [0] * (NT_OWN + 2)
            def front(s):
                p = s % 2
                self.dma("sync", x2t[p][:, :], dr["x2s"][s * 128:(s + 1) * 128, :],
                         waits=[("sSC", lnap[s - 2])])
                rdy = self.ckpt("sync")
                lnap[s] = self.ln_stats_apply(x2t[p], n2b[p], ("sRD", rdy), s % 4,
                                              extra_waits=[("sMM", trn[s - 2])],
                                              fused2=True,
                                              save=(sb["ln1_r"], sb["ln1_b"], s))

            def trn_emit(s):
                p = s % 2
                trn[s], fmev[s] = self.tr8(n2b[p], pT, n2fm[p][:, :],
                                           waits=[("sSC", lnap[s]), ("sVE", qfm_ev[s - 1])],
                                           evac_waits=[("sMM", umm[s - 1])])

            def back(s):
                p = s % 2
                mmq = S.op("tensor", lambda e, xf=n2fm[p]: self.proj_mm(
                    e, pP, xf, sb["Wq_b"], sb["bq_r"]),
                    waits=[("sVE", fmev[s]), ("sSC", vev[s - 1]), ("sVE", vrq_l[s - 1])], inc="sMM")
                if s + 1 < NT_OWN:
                    front(s + 1)
                vq1 = S.op("vector", lambda e: [
                    e.reduce_max(sb["qt"][:, hh:hh + 1], pP[:, hh * 256:(hh + 1) * 256], axis=AX.X)
                    for hh in range(4)], waits=[("sMM", mmq)], inc="sVE")
                vq2 = S.op("vector", lambda e: e.tensor_scalar_mul(sb["qt"][:, 4:8], sb["qt"][:, 0:4],
                                                                   -0.25),
                           waits=[("sVE", vq1)], inc="sVE")
                qev[s] = S.op("scalar", lambda e, o=qet[p]: [
                    e.activation(o[:, hh * 256:(hh + 1) * 256], pP[:, hh * 256:(hh + 1) * 256],
                                 AF.Exp, scale=0.25, bias=sb["qt"][:, 4 + hh:5 + hh],
                                 accum_out=sb["qt"][:, 8 + hh:9 + hh])
                    for hh in range(4)], waits=[("sVE", vq2), ("sMM", qtr_l[s - 2])], inc="sSC")
                vrq_l[s] = S.op("vector", lambda e, s_=s: e.reciprocal(
                    sb["rq"][:, 4 * s_:4 * s_ + 4], sb["qt"][:, 8:12]),
                    waits=[("sSC", qev[s])], inc="sVE")
                qtr_l[s], qfm_ev[s] = self.tr8(qet[p], pT, qefm[p][:, :],
                                               waits=[("sSC", qev[s]), ("sVE", fmev[s])],
                                               evac_waits=[("sGP", spl[s - 2])])
                qv = dr["qefm_s"][:, :].rearrange("p (c t) -> p c t", t=2048)[:, :, s * 128:(s + 1) * 128]
                spl[s] = self.dma("gpsimd", qv,
                                  qefm[p][:, :].rearrange("p (c t) -> p c t", t=128),
                                  waits=[("sVE", qfm_ev[s])])
                mmk = S.op("tensor", lambda e, xf=n2fm[p]: self.proj_mm(
                    e, pP, xf, sb["Wk_b"], sb["bk_r"], extra_rows=(sb["negmk_row"],)),
                    waits=[("sSC", qev[s]), ("sVE", vq1)], inc="sMM")
                ekev[s] = S.op("scalar", lambda e, o=ekt[p]: [
                    e.activation(o[:, 0:512], pP[:, 0:512], AF.Exp, scale=0.25),
                    e.activation(o[:, 512:1024], pP[:, 512:1024], AF.Exp, scale=0.25),
                ], waits=[("sMM", mmk), ("sMM", umm[s - 1])], inc="sSC")
                mmv = S.op("tensor", lambda e, xf=n2fm[p]: self.proj_mm(
                    e, pP, xf, sb["Wv_b"], sb["bv_r"]),
                    waits=[("sSC", ekev[s])], inc="sMM")
                vev[s] = S.op("scalar", lambda e, o=v2b[p]: [
                    e.activation(o[:, 0:512], pP[:, 0:512], AF.Copy),
                    e.activation(o[:, 512:1024], pP[:, 512:1024], AF.Copy),
                ], waits=[("sMM", mmv)], inc="sSC")
                def mmU(e, s_=s, ek=ekt[p], v_=v2b[p]):
                    r = []
                    for dc in range(8):
                        hh = dc >> 1
                        r.append(e.matmul(Ub[hh][:, (dc & 1) * 256:((dc & 1) + 1) * 256],
                                          ek[:, dc * 128:(dc + 1) * 128],
                                          v_[:, hh * 256:(hh + 1) * 256],
                                          start=(s_ == 0 and (dc & 1) == 0),
                                          stop=(s_ == NT_OWN - 1 and (dc & 1) == 1)))
                    for dc in range(8):
                        r.append(e.matmul(Ucol[:, dc:dc + 1], ek[:, dc * 128:(dc + 1) * 128],
                                          sb["ones_cb"][:, :],
                                          start=(s_ == 0 and dc == 0),
                                          stop=(s_ == NT_OWN - 1 and dc == 7)))
                    return r
                umm[s] = S.op("tensor", mmU, waits=[("sSC", ekev[s]), ("sSC", vev[s])], inc="sMM")

            front(0)
            trn_emit(0)
            for s in range(NT_OWN):
                back(s)
                if s + 1 < NT_OWN:
                    trn_emit(s + 1)
            uev = S.op("scalar", lambda e: [
                e.activation(U_sb[:, bh * 512:(bh + 1) * 512], Ub[bh][:, 0:512], AF.Copy)
                for bh in range(4)], waits=[("sMM", umm[NT_OWN - 1])], inc="sSC")
            ucv = S.op("vector", lambda e: e.tensor_copy(U_sb[:, 2048:2056], Ucol[:, 0:8]),
                       waits=[("sMM", umm[NT_OWN - 1])], inc="sVE")
            self.dma("gpsimd", dr["cc_in"][:, :], U_sb[:, :],
                     waits=[("sSC", uev), ("sVE", ucv)])
            self.end()

    def p3x(self):
        nc, sb, dr = self.nc, self.sb, self.dram
        self.alloc("C_fm", [128, 2 * D], BF16)
        with ExitStack() as ph:
            T = lambda n, s, dt: ph.enter_context(nc.sbuf_tensor(n, s, dt))
            Ug = T("x_Ug", [128, 2056], F32)
            rcb = T("x_rc", [128, 8], F32)
            self.begin()
            S = self.S
            cc = S.op("gpsimd", lambda e: e.collective_compute(
                "AllReduce", ALU.add, replica_groups=[[0, 1], [2, 3], [4, 5], [6, 7]],
                ins=[dr["cc_in"][:, :]], outs=[dr["cc_out"][:, :]]), inc="sCC")
            self.dma("sync", Ug[:, :], dr["cc_out"][:, :], waits=[("sCC", cc)])
            rdy = self.ckpt("sync")
            vr = S.op("vector", lambda e: e.reciprocal(rcb[:, 0:8], Ug[:, 2048:2056]),
                      waits=[("sRD", rdy)], inc="sVE")
            S.op("scalar", lambda e: [
                e.activation(sb["C_fm"][:, dc * 256:(dc + 1) * 256], Ug[:, dc * 256:(dc + 1) * 256],
                             AF.Identity, scale=rcb[:, dc:dc + 1])
                for dc in range(8)], waits=[("sVE", vr)], inc="sSC")
            self.end()

    def p3e1(self):
        nc, sb, dr = self.nc, self.sb, self.dram
        with ExitStack() as ph:
            T = lambda n, s, dt: ph.enter_context(nc.sbuf_tensor(n, s, dt))
            x2t = [T(f"i_x2{p}", [128, D], F32) for p in range(2)]
            qet = [T(f"i_qe{p}", [128, D], BF16) for p in range(2)]
            atb = [T(f"i_at{p}", [128, D], BF16) for p in range(2)]
            atfm = [T(f"i_atfm{p}", [128, D], BF16) for p in range(2)]
            inpt = [T(f"i_inp{p}", [128, D], F32) for p in range(2)]
            xxt = [T(f"i_xx{p}", [128, D], F32) for p in range(2)]
            pT = ph.enter_context(nc.psum_tensor("i_pT", [128, 1024], BF16))
            pP = ph.enter_context(nc.psum_tensor("i_pP", [128, 1024], F32))
            pW = ph.enter_context(nc.psum_tensor("i_pW", [128, 1024], F32))
            self.begin()
            S = self.S
            atev = [0] * (NT_OWN + 2)
            trat = [0] * (NT_OWN + 2)
            fmev = [0] * (NT_OWN + 2)
            mmw_l = [0] * (NT_OWN + 2)
            mma_l = [0] * (NT_OWN + 2)
            xxv = [0] * (NT_OWN + 2)
            inps = [0] * (NT_OWN + 2)
            stl = [0] * (NT_OWN + 2)
            sq_l = [0] * (NT_OWN + 2)
            for s in range(NT_OWN):
                p = s % 2
                qv = dr["qefm_s"][:, :].rearrange("p (c t) -> p c t", t=2048)[:, :, s * 128:(s + 1) * 128]
                self.dma("sync", qet[p][:, :].rearrange("p (c t) -> p c t", t=128), qv,
                         waits=[("sMM", mma_l[s - 2])])
                self.dma("sync", x2t[p][:, :], dr["x2s"][s * 128:(s + 1) * 128, :],
                         waits=[("sSC", inps[s - 2]), ("sSC", sq_l[s - 2])])
                rdy = self.ckpt("sync")
                def mmA(e, q_=qet[p]):
                    r = []
                    for hh in range(4):
                        for j, dc in enumerate((2 * hh, 2 * hh + 1)):
                            bank_first = (hh % 2 == 0) and j == 0
                            r.append(e.matmul(pP[:, hh * 256:(hh + 1) * 256],
                                              q_[:, dc * 128:(dc + 1) * 128],
                                              sb["C_fm"][:, dc * 256:(dc + 1) * 256],
                                              start=bank_first, stop=(j == 1 and hh % 2 == 1)))
                    return r
                mma_l[s] = S.op("tensor", mmA, waits=[("sRD", rdy), ("sSC", atev[s - 1])], inc="sMM")
                atev[s] = S.op("scalar", lambda e, o=atb[p], s_=s: [
                    e.activation(o[:, hh * 256:(hh + 1) * 256], pP[:, hh * 256:(hh + 1) * 256],
                                 AF.Identity, scale=sb["rq"][:, 4 * s_ + hh:4 * s_ + hh + 1])
                    for hh in range(4)], waits=[("sMM", mma_l[s]), ("sMM", trat[s - 2])], inc="sSC")
                trat[s], fmev[s] = self.tr8(atb[p], pT, atfm[p][:, :],
                                            waits=[("sSC", atev[s]), ("sVE", fmev[s - 1])],
                                            evac_waits=[("sMM", mmw_l[s - 1])])
                mmw_l[s] = S.op("tensor", lambda e, xf=atfm[p]: self.proj_mm(
                    e, pW, xf, sb["wo_bf"], sb["bo_bf"]),
                    waits=[("sVE", fmev[s]), ("sVE", xxv[s - 1])], inc="sMM")
                inps[s] = S.op("scalar", lambda e, a=x2t[p], o=inpt[p], s_=s: e.activation(
                    o[:, :], a[:, :], AF.Identity, scale=sb["ln1_r"][:, s_:s_ + 1],
                    bias=sb["ln1_b"][:, s_:s_ + 1]),
                    waits=[("sRD", rdy), ("sVE", xxv[s - 2])], inc="sSC")
                xxv[s] = S.op("vector", lambda e, p_=p: [
                    e.tensor_add(xxt[p_][:, 0:512], pW[:, 0:512], inpt[p_][:, 0:512]),
                    e.tensor_add(xxt[p_][:, 512:1024], pW[:, 512:1024], inpt[p_][:, 512:1024]),
                ], waits=[("sMM", mmw_l[s]), ("sSC", inps[s]), ("sGP", stl[s - 2])], inc="sVE")
                sq_l[s] = self.ln_stats_apply(xxt[p], None, ("sVE", xxv[s]), p,
                                              save=(sb["ln2_r"], sb["ln2_b"], s))
                stl[s] = self.dma("gpsimd", dr["xxs"][s * 128:(s + 1) * 128, :], xxt[p][:, :],
                                  waits=[("sVE", xxv[s])])
            self.end()

    def p3e2(self):
        nc, sb, dr = self.nc, self.sb, self.dram
        self.alloc("ff2_bf", [128, 32 * D], BF16)
        with ExitStack() as ph:
            T = lambda n, s, dt: ph.enter_context(nc.sbuf_tensor(n, s, dt))
            xxt = [sb["sq"]] + [T(f"j_xx{i}", [128, D], F32) for i in range(3)]
            n3b = sb["rsc"]
            n3g = T("j_n3g", [128, 4 * D], BF16)
            u_g = T("j_ug", [128, 16 * D], BF16)
            b1v = T("j_b1v", [32, 128], BF16)
            b1col = T("j_b1col", [128, 32], BF16)
            pT = ph.enter_context(nc.psum_tensor("j_pT", [128, 1024], BF16))
            pF1 = [ph.enter_context(nc.psum_tensor(f"j_f1{i}", [128, 512], F32)) for i in range(3)]
            pF2 = [ph.enter_context(nc.psum_tensor(f"j_f2{i}", [128, 1024], F32)) for i in range(2)]
            self.begin()
            S = self.S
            self.dma("gpsimd", b1v[:, :],
                     dr["b1row_s"][:, :].rearrange("o (a b) -> (o a) b", b=128))
            for c in range(32):
                self.dma("gpsimd", sb["ff2_bf"][:, c * D:(c + 1) * D],
                         dr["bff2"][c * 128:(c + 1) * 128, :])
            ff2_rdy = self.ckpt("gpsimd")
            b1tr = S.op("tensor", lambda e: e.transpose(pT[:, 0:32], b1v[:, :],
                                                        sb["ident"][0:32, 0:32]),
                        waits=[("sGD", ff2_rdy)], inc="sMM")
            b1ev = S.op("vector", lambda e: e.tensor_copy(b1col[:, :], pT[:, 0:32]),
                        waits=[("sMM", b1tr)], inc="sVE")
            lnap = [0] * (NT_OWN + 4)
            trn = [0] * (NT_OWN + 4)
            fme = [0] * (NT_OWN + 4)
            epi = [0] * (NT_OWN + 4)
            stl = [0] * (NT_OWN + 4)
            mmf1_last = [0] * 6
            mmf2_l = [0] * (NT_OWN + 4)
            gel_d = {}
            gel_last = [0] * 6

            def front(t):
                self.dma("sync", xxt[t % 4][:, :], dr["xxs"][t * 128:(t + 1) * 128, :],
                         waits=[("sGP", stl[t - 4])])
                rdy = self.ckpt("sync")
                lnap[t] = S.op("scalar", lambda e, a=xxt[t % 4], t_=t: e.activation(
                    n3b[:, :], a[:, :], AF.Identity, scale=sb["ln2_r"][:, t_:t_ + 1],
                    bias=sb["ln2_b"][:, t_:t_ + 1]),
                    waits=[("sRD", rdy), ("sMM", trn[t - 1])], inc="sSC")
                trn[t] = S.op("tensor", lambda e: [
                    e.transpose(pT[:, c * 128:(c + 1) * 128], n3b[:, c * 128:(c + 1) * 128],
                                sb["ident"][:, :]) for c in range(8)],
                    waits=[("sSC", lnap[t]), ("sVE", fme[t - 1] if t else b1ev)], inc="sMM")
                dst = n3g[:, :].rearrange("p (c b t) -> p c b t", b=4, t=128)[:, :, t % 4, :]
                fme[t] = S.op("vector", lambda e, d=dst: e.tensor_copy(
                    d, pT[:, :].rearrange("p (c t) -> p c t", t=128)),
                    waits=[("sMM", trn[t]), ("sMM", mmf1_last[t // 4 - 1])], inc="sVE")

            for t in range(4):
                front(t)
            for g in range(4):
                for oc in range(32):
                    gidx = g * 32 + oc
                    q = gidx % 3
                    def mmF1(e, oc_=oc, q_=q):
                        r = []
                        for ic in range(8):
                            r.append(e.matmul(
                                pF1[q_][:, 0:512],
                                sb["Wff1"][:, ic * 4 * D + oc_ * 128: ic * 4 * D + (oc_ + 1) * 128],
                                n3g[:, ic * 512:(ic + 1) * 512],
                                start=(ic == 0), stop=(ic == 7)))
                        return r
                    w = [("sSC", gel_d.get(gidx - 3, 0))]
                    if oc == 0:
                        w += [("sVE", fme[4 * g + 3])]
                    mmf1 = S.op("tensor", mmF1, waits=w, inc="sMM")
                    gel_d[gidx] = S.op("scalar", lambda e, oc_=oc, q_=q: e.activation(
                        u_g[:, oc_ * 512:(oc_ + 1) * 512], pF1[q_][:, 0:512],
                        AF.Gelu_apprx_tanh, bias=b1col[:, oc_:oc_ + 1]),
                        waits=[("sMM", mmf1), ("sVE", b1ev)], inc="sSC")
                mmf1_last[g] = mmf1
                for tb in range(4):
                    t = 4 * g + tb
                    pf = pF2[tb % 2]
                    def mmF2(e, tb_=tb, pf_=pf):
                        r = []
                        for hx in range(2):
                            r.append(e.matmul(pf_[:, hx * 512:(hx + 1) * 512],
                                              sb["ones_rb"][0:1, 0:128],
                                              sb["bf2_bf"][0:1, hx * 512:(hx + 1) * 512],
                                              start=True, stop=False))
                        for oc in range(32):
                            for hx in range(2):
                                r.append(e.matmul(
                                    pf_[:, hx * 512:(hx + 1) * 512],
                                    u_g[:, oc * 512 + tb_ * 128: oc * 512 + tb_ * 128 + 128],
                                    sb["ff2_bf"][:, oc * D + hx * 512: oc * D + (hx + 1) * 512],
                                    start=False, stop=(oc == 31)))
                        return r
                    mmf2_l[t] = S.op("tensor", mmF2,
                                     waits=[("sSC", gel_d[g * 32 + 31]), ("sGD", ff2_rdy),
                                            ("sVE", epi[t - 2])], inc="sMM")
                    epi[t] = S.op("vector", lambda e, pf_=pf, xs=xxt[t % 4]: [
                        e.tensor_add(xs[:, 0:512], pf_[:, 0:512], xs[:, 0:512]),
                        e.tensor_add(xs[:, 512:1024], pf_[:, 512:1024], xs[:, 512:1024]),
                    ], waits=[("sMM", mmf2_l[t])], inc="sVE")
                    stl[t] = self.dma("gpsimd", dr["xx2s"][t * 128:(t + 1) * 128, :],
                                      xxt[t % 4][:, :], waits=[("sVE", epi[t])])
                    if t + 4 < NT_OWN:
                        front(t + 4)
            self.end()

    def p3e3(self):
        nc, sb, dr = self.nc, self.sb, self.dram
        self.alloc("wd_bf", [128, 8 * D], BF16, side="right")
        self.alloc("wdqT", [128, 8 * D], BF16)
        with ExitStack() as ph:
            T = lambda n, s, dt: ph.enter_context(nc.sbuf_tensor(n, s, dt))
            xx2 = [T(f"k_xx2{p}", [128, D], F32) for p in range(2)]
            x2t = [T(f"k_x2{p}", [128, D], F32) for p in range(2)]
            xxb = [T(f"k_xxb{p}", [128, D], BF16) for p in range(2)]
            xxfm = [T(f"k_xxfm{p}", [128, D], BF16) for p in range(2)]
            x3t = [T(f"k_x3{p}", [128, D], F32) for p in range(3)]
            ht = [T(f"k_h{p}", [128, D], F32) for p in range(3)]
            xd = [T(f"k_xd{p}", [128, D], BF16) for p in range(2)]
            xdfm = [T(f"k_xdfm{p}", [128, D], BF16) for p in range(2)]
            x4t = [T(f"k_x4{p}", [128, D], F32) for p in range(2)]
            z = [T(f"k_z{i}", [128, D], F32) for i in range(2)]
            zsq = [T(f"k_zsq{i}", [128, D], F32) for i in range(2)]
            mg = [T(f"k_mg{i}", [128, D], F32) for i in range(2)]
            pT = ph.enter_context(nc.psum_tensor("k_pT", [128, 1024], BF16))
            pD = [ph.enter_context(nc.psum_tensor(f"k_pD{i}", [128, 1024], F32)) for i in range(2)]
            pM = ph.enter_context(nc.psum_tensor("k_pM", [128, 1024], F32))
            self.begin()
            S = self.S
            self.dma("gpsimd", sb["wdqT"][:, :], dr["wdq_s"][:, :])
            for jc in range(8):
                self.dma("gpsimd", sb["wd_bf"][:, jc * D:(jc + 1) * D],
                         dr["bwd"][jc * 128:(jc + 1) * 128, :])
            wd_rdy = self.ckpt("gpsimd")
            cast_l = [0] * (NT_OWN + 2)
            trx = [0] * (NT_OWN + 2)
            fmev = [0] * (NT_OWN + 2)
            mmd = [0] * (NT_OWN + 2)
            x3v = [0] * (NT_OWN + 2)
            lnap = [0] * (NT_OWN + 2)
            vq_l = [0] * (NT_OWN + 2)
            trq = [0] * (NT_OWN + 2)
            fmq = [0] * (NT_OWN + 2)
            mmx = [0] * (NT_OWN + 2)
            x4v = [0] * (NT_OWN + 2)
            stl = [0] * (NT_OWN + 2)
            def stA(s):
                p = s % 2
                self.dma("sync", xx2[p][:, :], dr["xx2s"][s * 128:(s + 1) * 128, :],
                         waits=[("sSC", cast_l[s - 2])])
                self.dma("sync", x2t[p][:, :], dr["x2s"][s * 128:(s + 1) * 128, :],
                         waits=[("sVE", x3v[s - 2])])
                rdy = self.ckpt("sync")
                cast_l[s] = S.op("scalar", lambda e, a=xx2[p], o=xxb[p]: e.activation(
                    o[:, :], a[:, :], AF.Copy),
                    waits=[("sRD", rdy), ("sMM", trx[s - 2])], inc="sSC")
                trx[s], fmev[s] = self.tr8(xxb[p], pT, xxfm[p][:, :],
                                           waits=[("sSC", cast_l[s]), ("sVE", fmq[s - 3])],
                                           evac_waits=[("sMM", mmd[s - 1])])
                mmd[s] = S.op("tensor", lambda e, xf=xxfm[p], rp=pD[s % 2]: self.proj_mm(
                    e, rp, xf, sb["wd_bf"], sb["bd_bf"]),
                    waits=[("sVE", fmev[s]), ("sGD", wd_rdy), ("sVE", x3v[s - 2])], inc="sMM")

            def stB(s):
                if s < 0 or s >= NT_OWN:
                    return
                p = s % 2
                p3 = s % 3
                p4 = s % 4
                x3v[s] = S.op("vector", lambda e, p_=p3, p2_=p, rp=pD[s % 2]: [
                    e.tensor_add(x3t[p_][:, 0:512], rp[:, 0:512], x2t[p2_][:, 0:512]),
                    e.tensor_add(x3t[p_][:, 512:1024], rp[:, 512:1024], x2t[p2_][:, 512:1024]),
                ], waits=[("sMM", mmd[s]), ("sSC", lnap[s - 3]), ("sVE", x4v[s - 3])], inc="sVE")
                lnap[s] = self.ln_stats_apply(x3t[p3], ht[p3], ("sVE", x3v[s]), p4,
                                              extra_waits=[("sVE", vq_l[s - 3])],
                                              sum_on_scalar=True)
                vq_l[s] = self.quant(ht[p3], xd[p], z[p], zsq[p], mg[p], ("sSC", lnap[s]), p4,
                                     vq_waits=[("sMM", trq[s - 2])])

            def stC(s):
                if s < 0 or s >= NT_OWN:
                    return
                p = s % 2
                p3 = s % 3
                trq[s], fmq[s] = self.tr8(xd[p], pT, xdfm[p][:, :],
                                          waits=[("sVE", vq_l[s]),
                                                 ("sVE", fmev[s + 2] if s + 2 < NT_OWN else fmev[NT_OWN - 1])],
                                          evac_waits=[("sMM", mmx[s - 1])])
                mmx[s] = S.op("tensor", lambda e, xf=xdfm[p]: self.proj_mm(
                    e, pM, xf, sb["wdqT"], sb["mxb_bf"]),
                    waits=[("sVE", fmq[s]), ("sVE", x4v[s - 1]), ("sGD", wd_rdy)], inc="sMM")
                x4v[s] = S.op("vector", lambda e, p_=p, p3_=p3: [
                    e.tensor_add(x4t[p_][:, 0:512], pM[:, 0:512], x3t[p3_][:, 0:512]),
                    e.tensor_add(x4t[p_][:, 512:1024], pM[:, 512:1024], x3t[p3_][:, 512:1024]),
                ], waits=[("sMM", mmx[s]), ("sGP", stl[s - 2])], inc="sVE")
                stl[s] = self.dma("gpsimd", dr["out"][s * 128:(s + 1) * 128, :], x4t[p][:, :],
                                  waits=[("sVE", x4v[s])])

            for i in range(NT_OWN + 2):
                if i < NT_OWN:
                    stA(i)
                stB(i - 1)
                stC(i - 2)
            self.end()

    def build(self):
        ph = self.phases
        if "prep" in ph:
            self.prep()
        if "p0a" in ph:
            self.p0a()
            self.free("wkT", "wvT")
        if "p0b" in ph:
            self.alloc("wqT", [128, 8 * D], BF16, side="right")
            self.begin()
            for jc in range(8):
                self.wq_loaded = self.dma("gpsimd", self.sb["wqT"][:, jc * D:(jc + 1) * D],
                                          self.dram["wqT"][jc * 128:(jc + 1) * 128, :])
            self.end()
            self.p0b()
            self.free("wqT")
        if "p1" in ph:
            self.p1()
            self.free("V_tm", "K_fm")
        if "p2w" in ph:
            self.p2w()
        if "p2" in ph:
            self.p2()
        if "p2w" in ph:
            self.begin()
            self.dma("gpsimd", self.dram["wdq_s"][:, :], self.sb["wdqT"][:, :])
            self.end()
            self.free("wdqT")
        if "p3w" in ph:
            self.p3w()
            self.free("b1row")
        if "p3a" in ph:
            self.p3a()
            self.free("Wv_b", "Wk_b", "Wq_b", "negmk_row")
        if "p3x" in ph:
            self.p3x()
        if "p3e1" in ph:
            self.p3e1()
            self.free("wo_bf", "C_fm", "bv_r", "bk_r", "bq_r")
        if "p3e2" in ph:
            self.p3e2()
            self.free("ff2_bf", "Wff1")
        if "p3e3" in ph:
            self.p3e3()
        if self._cms:
            self.free(*list(self._cms.keys()))
        self.stack.close()
        return self.nc


def _host_tables():
    i = np.arange(512, dtype=np.float64)
    phi = 10000.0 ** (-2.0 * (i - 1.0) / D)
    pos = np.arange(NSEQ, dtype=np.float64)
    ang = phi[:, None] * pos[None, :]
    cos = np.cos(ang).astype(np.float32)
    sin = np.sin(ang).astype(np.float32)
    cos_fm = np.ascontiguousarray(cos.reshape(4, 128, NSEQ).transpose(1, 0, 2).reshape(128, 4 * NSEQ))
    sin_fm = np.ascontiguousarray(sin.reshape(4, 128, NSEQ).transpose(1, 0, 2).reshape(128, 4 * NSEQ))
    return cos_fm, sin_fm


def _own_slices(tab_fm, h):
    t = tab_fm.reshape(128, 4, NT_ALL, 128)
    own = t[:, :, h::2, :]
    return np.ascontiguousarray(own.reshape(128, 4 * 2048))


def _p1_masks(h):
    M = np.zeros((8, 128, 4, 256), np.float32)
    for g in range(8):
        for m in range(4):
            kt = 4 * g + m
            kg = kt * 128 + np.arange(128)
            tA = (4 * g + h) * 128
            tB = (4 * g + 2 + h) * 128
            qg = np.concatenate([tA + np.arange(128), tB + np.arange(128)])
            M[g, :, m, :] = np.where(qg[None, :] >= kg[:, None], 0.0, NEG)
    return np.ascontiguousarray(M.reshape(8, 128, 1024))


def prepare_inputs(inputs):
    x = np.ascontiguousarray(inputs["L0_attn_a_x"], dtype=np.float32)
    perm = np.concatenate([np.arange(0, D, 2), np.arange(1, D, 2)])
    wqT = np.ascontiguousarray(np.asarray(inputs["L0_attn_a_wq"], np.float32).T[:, perm])
    wkT = np.ascontiguousarray(np.asarray(inputs["L0_attn_a_wk"], np.float32).T[:, perm])
    wvT = np.ascontiguousarray(np.asarray(inputs["L0_attn_a_wv"], np.float32).T)
    cos_fm, sin_fm = _host_tables()
    colsify = lambda v: np.ascontiguousarray(np.asarray(v, np.float32).reshape(8, 128).T)
    common = dict(
        wqT=wqT, wkT=wkT, wvT=wvT, costab=cos_fm, sintab=sin_fm,
        mxw=np.ascontiguousarray(np.asarray(inputs["L0_ffn_weight"], np.float32)),
        mxb=np.asarray(inputs["L0_ffn_bias"], np.float32).reshape(1, D),
        bwq=np.ascontiguousarray(np.asarray(inputs["L0_attn_b_wq"], np.float32)),
        bwk=np.ascontiguousarray(np.asarray(inputs["L0_attn_b_wk"], np.float32)),
        bwv=np.ascontiguousarray(np.asarray(inputs["L0_attn_b_wv"], np.float32)),
        bwo=np.ascontiguousarray(np.asarray(inputs["L0_attn_b_wo"], np.float32)),
        bwd=np.ascontiguousarray(np.asarray(inputs["L0_attn_b_w_d"], np.float32)),
        bff1=np.ascontiguousarray(np.asarray(inputs["L0_attn_b_w_ff1"], np.float32)),
        bff2=np.ascontiguousarray(np.asarray(inputs["L0_attn_b_w_ff2"], np.float32)),
        g1c=colsify(inputs["L0_attn_b_ln1_g"]), b1c=colsify(inputs["L0_attn_b_ln1_b"]),
        g2c=colsify(inputs["L0_attn_b_ln2_g"]), b2c=colsify(inputs["L0_attn_b_ln2_b"]),
        bo_r=np.asarray(inputs["L0_attn_b_bo"], np.float32).reshape(1, D),
        bf1_r=np.asarray(inputs["L0_attn_b_b_ff1"], np.float32).reshape(1, 4 * D),
        bf2_r=np.asarray(inputs["L0_attn_b_b_ff2"], np.float32).reshape(1, D),
        bd_r=np.asarray(inputs["L0_attn_b_b_d"], np.float32).reshape(1, D),
        ident=np.eye(128, dtype=np.float32),
        ones_r=np.ones((1, 1024), np.float32),
        ones_c=np.ones((128, 1), np.float32),
    )
    masks = [_p1_masks(0), _p1_masks(1)]
    cos_own = [_own_slices(cos_fm, 0), _own_slices(cos_fm, 1)]
    sin_own = [_own_slices(sin_fm, 0), _own_slices(sin_fm, 1)]
    in_maps = []
    for c in range(8):
        b, h = c >> 1, c & 1
        m = dict(common)
        m["x0"] = np.ascontiguousarray(x[b])
        m["x0own"] = np.ascontiguousarray(
            x[b].reshape(NT_ALL, 128, D)[h::2].reshape(NT_OWN * 128, D))
        m["p1mask"] = masks[h]
        m["costab_own"] = cos_own[h]
        m["sintab_own"] = sin_own[h]
        in_maps.append(m)

    def assemble(results):
        out = np.empty((4, NSEQ, D), np.float32)
        for c in range(8):
            b, h = c >> 1, c & 1
            o = np.asarray(results[c]["out"], np.float32)
            for s in range(NT_OWN):
                t = 2 * s + h
                out[b, t * 128:(t + 1) * 128, :] = o[s * 128:(s + 1) * 128, :]
        return out
    return in_maps, assemble


def build_program():
    return Builder(PHASES, DBG).build()


def kernel(**inputs):
    in_maps, assemble = prepare_inputs(inputs)
    nc = build_program()
    res = run_bass_kernel_spmd(nc, in_maps, core_ids=list(range(8)))
    return assemble(res.results)


# version 11
# speedup vs baseline: 1.1090x; 1.0238x over previous
import numpy as np
import ml_dtypes
from contextlib import ExitStack
import concourse.bass as bass
import concourse.mybir as mybir
from concourse.bass_utils import run_bass_kernel_spmd

F32 = mybir.dt.float32
BF16 = mybir.dt.bfloat16
AF = mybir.ActivationFunctionType
ALU = mybir.AluOpType
AX = mybir.AxisListType

D = 1024
NSEQ = 4096
NT_ALL = 32
NT_OWN = 16
EPS = 1e-5
NEG = -3.0e38
MAGIC = 12582912.0
ENGS = ("sync", "gpsimd", "tensor", "scalar", "vector")

PHASES = ["prep", "p0a", "p0b", "p1", "p2w", "p2", "p3w", "p3a", "p3x", "p3e1", "p3e2", "p3e3"]
DBG = None
STRIP = 0
SERIAL = False


class Sched:
    def __init__(self, counts, last_wait):
        self.ev = {e: [] for e in ENGS}
        self.C = counts
        self.lastw = last_wait

    def op(self, eng, fn, waits=(), inc=None, n_dma=1):
        if SERIAL:
            waits = list(waits) + [(s, v) for s, v in self.C.items()
                                   if s not in ("sPH",) and v > 0]
        w = []
        for s, v in waits:
            if v > 0 and self.lastw.get((eng, s), -1) < v:
                w.append((s, v))
                self.lastw[(eng, s)] = v
        dma = inc in ("sIN", "sGP")
        self.ev[eng].append((fn, w, inc, dma))
        if inc is not None:
            self.C[inc] += 16 * n_dma if dma else 1
        return self.C[inc] if inc is not None else None


def emit_block(nc, sched, sems):
    with nc.Block() as blk:
        def mk(eng_name):
            def closure(eng):
                for fn, waits, inc, dma in sched.ev[eng_name]:
                    for s, v in waits:
                        eng.wait_ge(sems[s], v)
                    r = fn(eng)
                    if inc is None:
                        continue
                    instrs = r if isinstance(r, (list, tuple)) else [r]
                    if dma:
                        for i in instrs:
                            i.then_inc(sems[inc], 16)
                    else:
                        instrs[-1].then_inc(sems[inc], 1)
            return closure
        blk.sync(mk("sync"))
        blk.gpsimd(mk("gpsimd"))
        blk.tensor(mk("tensor"))
        blk.scalar(mk("scalar"))
        blk.vector(mk("vector"))


class Builder:
    def __init__(self, phases, dbg=None):
        self.phases = phases
        self.dbg = dbg
        nc = bass.Bass("TRN2", target_bir_lowering=False)
        self.nc = nc
        self.C = {k: 0 for k in ("sIN", "sGP", "sMM", "sSC", "sVE", "sPH", "sCC", "sRD", "sGD")}
        self.lastw = {}
        self.stack = ExitStack()
        self.sems = {k: self.stack.enter_context(nc.semaphore(k)) for k in self.C}
        self.phase_idx = 0
        self.dram = {}
        self.sb = {}
        self._cms = {}
        self._seq = {}
        self._declare_dram()
        self._alloc_persistent()

    def _declare_dram(self):
        nc = self.nc
        inp = lambda name, shape, dt=F32: self.dram.__setitem__(
            name, nc.dram_tensor(name, shape, dt, kind="ExternalInput"))
        inp("x0", [NSEQ, D])
        inp("x0own", [NT_OWN * 128, D])
        inp("p1mask", [8, 128, 1024])
        inp("wqT", [D, D]); inp("wkT", [D, D]); inp("wvT", [D, D])
        inp("costab", [128, 4 * NSEQ]); inp("sintab", [128, 4 * NSEQ])
        inp("costab_own", [128, 4 * 2048]); inp("sintab_own", [128, 4 * 2048])
        inp("mxw", [D, D]); inp("mxb", [1, D])
        inp("bwq", [D, D]); inp("bwk", [D, D]); inp("bwv", [D, D])
        inp("bwo", [D, D]); inp("bwd", [D, D])
        inp("bff1", [D, 4 * D]); inp("bff2", [4 * D, D])
        inp("g1c", [128, 8]); inp("b1c", [128, 8]); inp("g2c", [128, 8]); inp("b2c", [128, 8])
        inp("bo_r", [1, D]); inp("bf1_r", [1, 4 * D]); inp("bf2_r", [1, D]); inp("bd_r", [1, D])
        inp("ident", [128, 128]); inp("ones_r", [1, 1024]); inp("ones_c", [128, 1])
        self.dram["out"] = nc.dram_tensor("out", [NT_OWN * 128, D], F32, kind="ExternalOutput")
        d = self.dbg
        scr = lambda name, shape, dt=F32, dbg_out=False: self.dram.__setitem__(
            name, nc.dram_tensor(name, shape, dt, kind="ExternalOutput" if dbg_out else "Internal"))
        scr("x1s", [NT_OWN * 128, D], F32, d in ("p1", "all"))
        scr("x2s", [NT_OWN * 128, D], F32, d in ("p2", "all"))
        scr("xxs", [NT_OWN * 128, D], F32, d in ("p3e1", "all"))
        scr("xx2s", [NT_OWN * 128, D], F32, d in ("p3e2", "all"))
        scr("qfm_s", [128, 8 * 2048], BF16, d == "p0")
        scr("qefm_s", [128, 8 * 2048], BF16, d == "p3a")
        scr("cc_in", [128, 2056], F32)
        scr("cc_out", [128, 2056], F32)
        scr("wdq_s", [128, 8 * D], BF16, d == "all")
        scr("b1row_s", [1, 4 * D], BF16, d == "all")
        if d == "all":
            scr("dbg_wff1", [128, 4 * D], BF16, True)
            scr("dbg_u", [128, 8 * D], BF16, True)
        if d == "p0":
            scr("dbg_kfm", [128, 8 * NSEQ], BF16, True)
            scr("dbg_vtm", [128, NT_ALL * D], BF16, True)
            scr("dbg_negm", [1, 256], F32, True)
            scr("dbg_tiny", [128, 16], F32, True)

            scr("dbg_kn", [128, NT_ALL], F32, True)
            scr("dbg_qn", [128, NT_OWN], F32, True)

    def _alloc_persistent(self):
        nc, st = self.nc, self.stack
        T = lambda n, s, dt: st.enter_context(nc.sbuf_tensor(n, s, dt))
        self.sb["ident"] = T("c_ident", [128, 128], BF16)
        self.sb["ones_rb"] = T("c_ones_rb", [1, 1024], BF16)
        self.sb["ones_rf"] = T("c_ones_rf", [1, 1024], F32)
        self.sb["ones_cb"] = T("c_ones_cb", [128, 1], BF16)
        self.sb["negm_row"] = T("negm_row", [1, 256], BF16)
        self.sb["sq"] = T("sq_scratch", [128, 1024], F32)
        self.sb["rsc"] = T("rope_scratch", [128, 1024], BF16)
        self.sb["tiny"] = T("tiny", [128, 80], F32)
        self.sb["epsc"] = T("epsc", [128, 1], F32)
        self.sb["nepsc"] = T("nepsc", [128, 1], F32)
        self.sb["mgc"] = T("mgc", [128, 1], F32)
        self.sb["knorm"] = T("knorm", [128, NT_ALL], F32)
        self.sb["qnorm"] = T("qnorm", [128, NT_OWN], F32)
        self.sb["ln1_r"] = T("ln1_r", [128, NT_OWN], F32)
        self.sb["ln1_b"] = T("ln1_b", [128, NT_OWN], F32)
        self.sb["ln2_r"] = T("ln2_r", [128, NT_OWN], F32)
        self.sb["ln2_b"] = T("ln2_b", [128, NT_OWN], F32)
        self.sb["rq"] = T("rq", [128, NT_OWN * 4], F32)
        self.sb["qt"] = T("qt", [128, 384], F32)


    def barrier(self):
        S = self.S
        for e in ENGS:
            w = [("sIN", self.C["sIN"])] if e == "sync" else (
                [("sGP", self.C["sGP"])] if e == "gpsimd" else [])
            S.op(e, lambda eng: eng.nop(), waits=w, inc="sPH", n_dma=0)
        self.phase_idx += 1

    def begin(self):
        self.S = Sched(self.C, self.lastw)
        tgt = 5 * self.phase_idx
        if tgt:
            for e in ENGS:
                self.S.lastw.pop((e, "sPH"), None)
                self.S.ev[e].append(((lambda eng: None), [("sPH", tgt)], None, False))

    def end(self):
        self.barrier()
        emit_block(self.nc, self.S, self.sems)
        self.S = None

    def alloc(self, name, shape, dt, side=None):
        self._aseq = getattr(self, "_aseq", 0) + 1
        cm = self.nc.sbuf_tensor(f"s{self._aseq}_{name}", shape, dt, side=side)
        h = cm.__enter__()
        self._cms[name] = cm
        self._seq[name] = self._seqno = getattr(self, "_seqno", 0) + 1
        self.sb[name] = h
        return h

    def free(self, *names):
        for n in sorted(names, key=lambda n: -self._seq[n]):
            self._cms.pop(n).__exit__(None, None, None)
            self.sb.pop(n)
            self._seq.pop(n)

    def ckpt(self, eng):
        if eng == "sync":
            return self.S.op("sync", lambda e: e.nop(), waits=[("sIN", self.C["sIN"])],
                             inc="sRD", n_dma=0)
        return self.S.op("gpsimd", lambda e: e.nop(), waits=[("sGP", self.C["sGP"])],
                         inc="sGD", n_dma=0)

    def dma(self, eng, out, in_, waits=()):
        return self.S.op(eng, lambda e, o=out, i=in_: e.dma_start(out=o, in_=i),
                         waits=waits, inc="sIN" if eng == "sync" else "sGP")

    def ln_stats_apply(self, xin, n_out, x_ready, par, extra_waits=(), fused2=None, save=None,
                       sum_on_scalar=False):
        S, sb = self.S, self.sb
        b = 20 * par
        t = lambda i: sb["tiny"][:, b + i:b + i + 1]
        sq = S.op("scalar", lambda e, a=xin: e.activation(sb["sq"][:, :], a[:, :], AF.Square,
                                                          accum_out=t(1)),
                  waits=[x_ready], inc="sSC")
        if sum_on_scalar:
            v1 = 0
            sq = S.op("scalar", lambda e, a=xin: e.activation(sb["sq"][:, :], a[:, :], AF.Copy,
                                                              accum_out=t(0)),
                      waits=[x_ready], inc="sSC")
        else:
            v1 = S.op("vector", lambda e, a=xin: e.reduce_sum(t(0), a[:, :], axis=AX.X),
                      waits=[x_ready], inc="sVE")
        v2 = S.op("vector", lambda e: [
            e.tensor_scalar_mul(t(2), t(0), 1.0 / D),
            e.tensor_scalar(t(3), t(1), 1.0 / D, EPS, ALU.mult, ALU.add),
        ], waits=[("sVE", v1), ("sSC", sq)], inc="sVE")
        v3 = S.op("vector", lambda e: e.tensor_mul(t(4), t(2), t(2)), waits=[("sVE", v2)], inc="sVE")
        v4 = S.op("vector", lambda e: e.scalar_tensor_tensor(t(5), t(4), -1.0, t(3),
                                                             ALU.mult, ALU.add),
                  waits=[("sVE", v3)], inc="sVE")
        s2 = S.op("scalar", lambda e: [
            e.activation(t(6), t(5), AF.Ln),
            e.activation(t(7), t(6), AF.Exp, scale=-0.5),
        ], waits=[("sVE", v4)], inc="sSC")
        if not fused2:
            v5 = S.op("vector", lambda e: e.scalar_tensor_tensor(t(8), t(2), -1.0, t(7),
                                                                 ALU.mult, ALU.mult),
                      waits=[("sSC", s2)], inc="sVE")
            def fin(e):
                r = []
                if save is not None:
                    rbuf, bbuf, scol = save
                    r += [e.activation(rbuf[:, scol:scol + 1], t(7), AF.Identity),
                          e.activation(bbuf[:, scol:scol + 1], t(8), AF.Identity)]
                if n_out is not None:
                    r.append(e.activation(n_out[:, :], xin[:, :], AF.Identity,
                                          scale=t(7), bias=t(8)))
                return r
            return S.op("scalar", fin, waits=[("sVE", v5)] + list(extra_waits), inc="sSC")
        v5 = S.op("vector", lambda e: [
            e.scalar_tensor_tensor(t(8), t(2), -1.0, t(7), ALU.mult, ALU.mult),
            e.tensor_mul(t(10), t(7), t(7)),
            e.tensor_scalar_add(t(11), t(5), -EPS),
        ], waits=[("sSC", s2)], inc="sVE")
        v8 = S.op("vector", lambda e: e.scalar_tensor_tensor(
            t(13), t(10), t(11), sb["epsc"][:, :], ALU.mult, ALU.add),
            waits=[("sVE", v5)], inc="sVE")
        s3 = S.op("scalar", lambda e: [
            e.activation(t(14), t(13), AF.Ln),
            e.activation(t(15), t(14), AF.Exp, scale=-0.5),
        ], waits=[("sVE", v8)], inc="sSC")
        v10 = S.op("vector", lambda e: e.tensor_mul(t(16), t(7), t(15)),
                   waits=[("sSC", s3)], inc="sVE")
        v11 = S.op("vector", lambda e: e.scalar_tensor_tensor(t(18), t(2), -1.0, t(16),
                                                              ALU.mult, ALU.mult),
                   waits=[("sVE", v10)], inc="sVE")
        def fin2(e):
            r = []
            if save is not None:
                rbuf, bbuf, scol = save
                r += [e.activation(rbuf[:, scol:scol + 1], t(7), AF.Identity),
                      e.activation(bbuf[:, scol:scol + 1], t(8), AF.Identity)]
            if n_out is not None:
                r.append(e.activation(n_out[:, :], xin[:, :], AF.Identity,
                                      scale=t(16), bias=t(18)))
            return r
        return S.op("scalar", fin2, waits=[("sVE", v11)] + list(extra_waits), inc="sSC")

    def rope(self, kraw, cosb, sinb, out_view_e, out_view_o, waits):
        S, sb = self.S, self.sb
        r3 = lambda ap: ap.rearrange("p (c t) -> p c t", t=128)
        return S.op("vector", lambda e, kr=kraw, cb=cosb, sn=sinb, oe=out_view_e, oo=out_view_o: [
            e.tensor_mul(r3(sb["rsc"][:, 0:512]), r3(kr[:, 0:512]), r3(cb[:, :])),
            e.tensor_mul(r3(sb["rsc"][:, 512:1024]), r3(kr[:, 512:1024]), r3(sn[:, :])),
            e.tensor_add(oe, r3(sb["rsc"][:, 0:512]), r3(sb["rsc"][:, 512:1024])),
            e.tensor_mul(r3(sb["rsc"][:, 0:512]), r3(kr[:, 512:1024]), r3(cb[:, :])),
            e.tensor_mul(r3(sb["rsc"][:, 512:1024]), r3(kr[:, 0:512]), r3(sn[:, :])),
            e.tensor_sub(oo, r3(sb["rsc"][:, 0:512]), r3(sb["rsc"][:, 512:1024])),
        ], waits=waits, inc="sVE")

    def quant(self, h, xd_bf, z, zsq, mg, h_ready, par, vq_waits=()):
        S, sb = self.S, self.sb
        b = 96 * par
        qt = sb["qt"]
        r3 = lambda ap: ap.rearrange("p (g k) -> p g k", k=32)
        v1 = S.op("vector", lambda e: e.reduce_max(qt[:, b:b + 32], r3(h[:, :]), axis=AX.X,
                                                   apply_absolute_value=True),
                  waits=[h_ready], inc="sVE")
        v2 = S.op("vector", lambda e: e.tensor_scalar(qt[:, b + 32:b + 64], qt[:, b:b + 32],
                                                      1.0 / 7.5, 1e-30, ALU.mult, ALU.max),
                  waits=[("sVE", v1)], inc="sVE")
        v3 = S.op("vector", lambda e: e.reciprocal(qt[:, b + 64:b + 96], qt[:, b + 32:b + 64]),
                  waits=[("sVE", v2)], inc="sVE")
        vz = S.op("vector", lambda e: e.tensor_mul(
            r3(z[:, :]), r3(h[:, :]), qt[:, b + 64:b + 96, None].broadcast_to([128, 32, 32])),
            waits=[("sVE", v3)], inc="sVE")
        s1 = S.op("scalar", lambda e: e.activation(zsq[:, :], z[:, :], AF.Square),
                  waits=[("sVE", vz)], inc="sSC")
        vab = S.op("vector", lambda e: [
            e.tensor_scalar(mg[:, :], zsq[:, :], 16.0, 2.0, ALU.is_ge, ALU.mult),
            e.scalar_tensor_tensor(mg[:, :], zsq[:, :], 4.0, mg[:, :],
                                   ALU.is_ge, ALU.add),
        ], waits=[("sSC", s1)], inc="sVE")
        s2 = S.op("scalar", lambda e: e.activation(mg[:, :], mg[:, :], AF.Identity,
                                                   scale=1572864.0, bias=sb["mgc"][:, :]),
                  waits=[("sVE", vab)], inc="sSC")
        vq = S.op("vector", lambda e: [
            e.tensor_add(z[:, :], z[:, :], mg[:, :]),
            e.tensor_sub(z[:, :], z[:, :], mg[:, :]),
            e.tensor_mul(r3(xd_bf[:, :]), r3(z[:, :]),
                         qt[:, b + 32:b + 64, None].broadcast_to([128, 32, 32])),
        ], waits=[("sSC", s2)] + list(vq_waits), inc="sVE")
        return vq

    def proj_mm(self, e, psum, x_fm, w_sb, bias_row, extra_rows=()):
        r = []
        for half in range(2):
            tgt = psum[:, half * 512:(half + 1) * 512]
            if bias_row is not None:
                r.append(e.matmul(tgt, self.sb["ones_rb"][0:1, 0:128],
                                  bias_row[0:1, half * 512:(half + 1) * 512],
                                  start=True, stop=False))
            for row in extra_rows:
                r.append(e.matmul(tgt, self.sb["ones_rb"][0:1, 0:128],
                                  row[0:1, half * 512:(half + 1) * 512],
                                  start=False, stop=False))
            for jc in range(8):
                first = jc == 0 and bias_row is None and not extra_rows
                r.append(e.matmul(tgt, x_fm[:, jc * 128:(jc + 1) * 128],
                                  w_sb[:, jc * D + half * 512: jc * D + (half + 1) * 512],
                                  start=first, stop=(jc == 7)))
        return r

    def tr8(self, src_bf, pT, dst_view_or_ap, waits, evac_waits=()):
        S, sb = self.S, self.sb
        t = S.op("tensor", lambda e: [
            e.transpose(pT[:, c * 128:(c + 1) * 128], src_bf[:, c * 128:(c + 1) * 128],
                        sb["ident"][:, :]) for c in range(8)],
            waits=waits, inc="sMM")
        ev = S.op("vector", lambda e: e.tensor_copy(dst_view_or_ap, pT[:, :]),
                  waits=[("sMM", t)] + list(evac_waits), inc="sVE")
        return t, ev

    def prep(self):
        nc, sb, dr, st = self.nc, self.sb, self.dram, self.stack
        self.alloc("wkT", [128, 8 * D], BF16, side="right")
        self.alloc("wvT", [128, 8 * D], BF16, side="right")
        self.begin()
        self.S.op("vector", lambda e: [
            e.memset(sb["epsc"][:, :], EPS),
            e.memset(sb["nepsc"][:, :], -EPS),
            e.memset(sb["mgc"][:, :], 1572864.0),
        ], inc="sVE")
        self.dma("gpsimd", sb["ident"][:, :], dr["ident"][:, :])
        self.dma("gpsimd", sb["ones_rb"][:, :], dr["ones_r"][:, :])
        self.dma("sync", sb["ones_rf"][:, :], dr["ones_r"][:, :])
        self.dma("gpsimd", sb["ones_cb"][:, :], dr["ones_c"][:, :])
        for jc in range(8):
            self.dma("gpsimd", sb["wkT"][:, jc * D:(jc + 1) * D], dr["wkT"][jc * 128:(jc + 1) * 128, :])
            self.dma("gpsimd", sb["wvT"][:, jc * D:(jc + 1) * D], dr["wvT"][jc * 128:(jc + 1) * 128, :])
        self.end()

    def p0a(self):
        nc, sb, dr, st = self.nc, self.sb, self.dram, self.stack
        self.alloc("K_fm", [128, 8 * NSEQ], BF16)
        self.alloc("V_tm", [128, NT_ALL * D], BF16)
        with ExitStack() as ph:
            T = lambda n, s, dt: ph.enter_context(nc.sbuf_tensor(n, s, dt))
            xin = [T(f"a_xin{p}", [128, D], F32) for p in range(2)]
            cosb = [T(f"a_cos{p}", [128, 512], BF16) for p in range(2)]
            sinb = [T(f"a_sin{p}", [128, 512], BF16) for p in range(2)]
            nbf = [T(f"a_nbf{p}", [128, D], BF16) for p in range(3)]
            nfm = [T(f"a_nfm{p}", [128, D], BF16) for p in range(3)]
            ktmb = [T(f"a_ktm{p}", [128, D], BF16) for p in range(2)]
            kraw = [T(f"a_kraw{p}", [128, D], BF16) for p in range(2)]
            pT = ph.enter_context(nc.psum_tensor("a_pT", [128, 1024], BF16))
            pT2 = ph.enter_context(nc.psum_tensor("a_pT2", [128, 1024], BF16))
            pp = [ph.enter_context(nc.psum_tensor(f"a_pp{i}", [128, 1024], F32)) for i in range(3)]
            self.begin()
            S = self.S
            lnapply = [0] * (NT_ALL + 2)
            ropeK = [0] * (NT_ALL + 2)
            evK = [0] * (NT_ALL + 2)
            evV = [0] * (NT_ALL + 2)
            nfm_ev = [0] * (NT_ALL + 2)
            kraw_ev = [0] * (NT_ALL + 2)
            ntr_l = [0] * (NT_ALL + 2)

            Kv = sb["K_fm"][:, :].rearrange("p (c t) -> p c t", t=NSEQ)
            def front(tt):
                p = tt % 2
                pn = tt % 3
                t0 = tt * 128
                x_in = self.dma("sync", xin[p][:, :], dr["x0"][t0:t0 + 128, :],
                                waits=[("sSC", lnapply[tt - 2])])
                cview = dr["costab"][:, :].rearrange("p (c t) -> p c t", t=NSEQ)[:, :, t0:t0 + 128]
                sview = dr["sintab"][:, :].rearrange("p (c t) -> p c t", t=NSEQ)[:, :, t0:t0 + 128]
                self.dma("gpsimd", cosb[p][:, :].rearrange("p (c t) -> p c t", t=128), cview,
                         waits=[("sVE", ropeK[tt - 2])])
                self.dma("gpsimd", sinb[p][:, :].rearrange("p (c t) -> p c t", t=128), sview)
                cos_in[tt] = self.ckpt("gpsimd")
                lnapply[tt] = self.ln_stats_apply(xin[p], nbf[pn], ("sIN", x_in), pn,
                                                  extra_waits=[("sMM", ntr_l[tt - 3])])
                ntr_l[tt] = S.op("tensor", lambda e, i=nbf[pn]: [
                    e.transpose(pT[:, c * 128:(c + 1) * 128], i[:, c * 128:(c + 1) * 128], sb["ident"][:, :])
                    for c in range(8)],
                    waits=[("sSC", lnapply[tt]), ("sVE", nfm_ev[tt - 1])], inc="sMM")
                nfm_ev[tt] = S.op("vector", lambda e, o=nfm[pn]: e.tensor_copy(o[:, :], pT[:, :]),
                                  waits=[("sMM", ntr_l[tt])], inc="sVE")

            def mmK(tt):
                pn = tt % 3
                rK = pp[(2 * tt) % 3]
                kmmK[tt] = S.op("tensor", lambda e, n=nfm[pn], rk=rK: [
                    mm for jc in range(8) for mm in (
                        e.matmul(rk[:, 0:512], n[:, jc * 128:(jc + 1) * 128],
                                 sb["wkT"][:, jc * D:jc * D + 512], start=(jc == 0), stop=(jc == 7)),
                        e.matmul(rk[:, 512:1024], n[:, jc * 128:(jc + 1) * 128],
                                 sb["wkT"][:, jc * D + 512:(jc + 1) * D], start=(jc == 0), stop=(jc == 7)),
                    )], waits=[("sVE", nfm_ev[tt]), ("sSC", evK[tt - 1])], inc="sMM")

            def mmV(tt):
                pn = tt % 3
                rV = pp[(2 * tt + 1) % 3]
                kmmV[tt] = S.op("tensor", lambda e, n=nfm[pn], rv=rV: [
                    mm for jc in range(8) for mm in (
                        e.matmul(rv[:, 0:512], n[:, jc * 128:(jc + 1) * 128],
                                 sb["wvT"][:, jc * D:jc * D + 512], start=(jc == 0), stop=(jc == 7)),
                        e.matmul(rv[:, 512:1024], n[:, jc * 128:(jc + 1) * 128],
                                 sb["wvT"][:, jc * D + 512:(jc + 1) * D], start=(jc == 0), stop=(jc == 7)),
                    )], waits=[("sVE", nfm_ev[tt]), ("sSC", evV[tt - 1])], inc="sMM")

            def backS(tt):
                if tt < 0:
                    return
                p = tt % 2
                rK = pp[(2 * tt) % 3]
                rV = pp[(2 * tt + 1) % 3]
                evK[tt] = S.op("scalar", lambda e, o=ktmb[p], rk=rK, c_=tt: [
                    e.activation(o[:, 0:512], rk[:, 0:512], AF.Copy),
                    e.activation(o[:, 512:1024], rk[:, 512:1024], AF.Copy),
                    e.activation(sb["sq"][:, :], o[:, :], AF.Square,
                                 accum_out=sb["knorm"][:, c_:c_ + 1]),
                ], waits=[("sMM", kmmK[tt])], inc="sSC")
                evV[tt] = S.op("scalar", lambda e, rv=rV, c_=tt: [
                    e.activation(sb["V_tm"][:, c_ * D:c_ * D + 512], rv[:, 0:512], AF.Copy),
                    e.activation(sb["V_tm"][:, c_ * D + 512:(c_ + 1) * D], rv[:, 512:1024], AF.Copy),
                ], waits=[("sMM", kmmV[tt])], inc="sSC")

            def backT(tt):
                if tt < 0:
                    return
                p = tt % 2
                ktr = S.op("tensor", lambda e, i=ktmb[p]: [
                    e.transpose(pT2[:, c * 128:(c + 1) * 128], i[:, c * 128:(c + 1) * 128], sb["ident"][:, :])
                    for c in range(8)],
                    waits=[("sSC", evK[tt]), ("sVE", kraw_ev[tt - 1])], inc="sMM")
                ktr_l[tt] = ktr

            def backV(tt):
                if tt < 0:
                    return
                p = tt % 2
                t0 = tt * 128
                kraw_ev[tt] = S.op("vector", lambda e, o=kraw[p]: e.tensor_copy(o[:, :], pT2[:, :]),
                                   waits=[("sMM", ktr_l[tt])], inc="sVE")
                ropeK[tt] = self.rope(kraw[p], cosb[p], sinb[p],
                                      Kv[:, 0:4, t0:t0 + 128], Kv[:, 4:8, t0:t0 + 128],
                                      waits=[("sVE", kraw_ev[tt]), ("sGD", cos_in[tt])])

            kmmK = [0] * (NT_ALL + 2)
            kmmV = [0] * (NT_ALL + 2)
            ktr_l = [0] * (NT_ALL + 2)
            cos_in = [0] * (NT_ALL + 2)
            front(0)
            for tt in range(NT_ALL):
                mmK(tt)
                backS(tt - 1)
                backT(tt - 1)
                mmV(tt)
                backV(tt - 1)
                if tt + 1 < NT_ALL:
                    front(tt + 1)
            backS(NT_ALL - 1)
            backT(NT_ALL - 1)
            backV(NT_ALL - 1)
            self.end()

    def p0b(self):
        nc, sb, dr, st = self.nc, self.sb, self.dram, self.stack
        with ExitStack() as ph:
            T = lambda n, s, dt: ph.enter_context(nc.sbuf_tensor(n, s, dt))
            xin = [T(f"b_xin{p}", [128, D], F32) for p in range(2)]
            cosb = [T(f"b_cos{p}", [128, 512], BF16) for p in range(2)]
            sinb = [T(f"b_sin{p}", [128, 512], BF16) for p in range(2)]
            nbf = [T(f"b_nbf{p}", [128, D], BF16) for p in range(2)]
            nfm = [T(f"b_nfm{p}", [128, D], BF16) for p in range(2)]
            qtmb = [T(f"b_qtm{p}", [128, D], BF16) for p in range(2)]
            qraw = [T(f"b_qraw{p}", [128, D], BF16) for p in range(2)]
            qfmt = [T(f"b_qfmt{p}", [128, D], BF16) for p in range(2)]
            pTn = [ph.enter_context(nc.psum_tensor(f"b_pTn{i}", [128, 1024], BF16)) for i in range(2)]
            pT2 = ph.enter_context(nc.psum_tensor("b_pT2", [128, 1024], BF16))
            pT = pTn[0]
            pp = [ph.enter_context(nc.psum_tensor(f"b_pp{i}", [128, 1024], F32)) for i in range(2)]
            self.begin()
            S = self.S
            lnapply = [0] * (NT_OWN + 2)
            ropeQ = [0] * (NT_OWN + 2)
            evQ = [0] * (NT_OWN + 2)
            nfm_ev = [0] * (NT_OWN + 2)
            qraw_ev = [0] * (NT_OWN + 2)
            spill = [0] * (NT_OWN + 2)
            ntr_l = [0] * (NT_OWN + 2)
            qmm_l = [0] * (NT_OWN + 2)
            cos_in = [0] * (NT_OWN + 2)

            for s in range(NT_OWN + 1):
                if s < NT_OWN:
                    p = s % 2
                    t0 = s * 128
                    x_in = self.dma("sync", xin[p][:, :], dr["x0own"][t0:t0 + 128, :],
                                    waits=[("sSC", lnapply[s - 2])])
                    cview = dr["costab_own"][:, :].rearrange("p (c t) -> p c t", t=2048)[:, :, t0:t0 + 128]
                    sview = dr["sintab_own"][:, :].rearrange("p (c t) -> p c t", t=2048)[:, :, t0:t0 + 128]
                    self.dma("gpsimd", cosb[p][:, :].rearrange("p (c t) -> p c t", t=128), cview,
                             waits=[("sVE", ropeQ[s - 2])])
                    self.dma("gpsimd", sinb[p][:, :].rearrange("p (c t) -> p c t", t=128), sview)
                    cos_in[s] = self.ckpt("gpsimd")
                    lnapply[s] = self.ln_stats_apply(xin[p], nbf[p], ("sIN", x_in), p,
                                                     extra_waits=[("sMM", ntr_l[s - 2])])
                    ntr_l[s] = ntr = S.op("tensor", lambda e, i=nbf[p], b=pTn[p]: [
                        e.transpose(b[:, c * 128:(c + 1) * 128], i[:, c * 128:(c + 1) * 128], sb["ident"][:, :])
                        for c in range(8)],
                        waits=[("sSC", lnapply[s]), ("sVE", nfm_ev[s - 2])], inc="sMM")
                    nfm_ev[s] = S.op("vector", lambda e, o=nfm[p], b=pTn[p]: e.tensor_copy(o[:, :], b[:, :]),
                                     waits=[("sMM", ntr)], inc="sVE")
                    rQ = pp[s % 2]

                    def mmQ(e, n=nfm[p], rq=rQ):
                        r = []
                        for jc in range(8):
                            st_, sp = jc == 0, jc == 7
                            lhs = n[:, jc * 128:(jc + 1) * 128]
                            r.append(e.matmul(rq[:, 0:512], lhs, sb["wqT"][:, jc * D:jc * D + 512], start=st_, stop=sp))
                            r.append(e.matmul(rq[:, 512:1024], lhs, sb["wqT"][:, jc * D + 512:(jc + 1) * D], start=st_, stop=sp))
                        return r
                    qmm_l[s] = qmm = S.op("tensor", mmQ, waits=[("sVE", nfm_ev[s]), ("sSC", evQ[s - 2]),
                                                                ("sGP", self.wq_loaded)], inc="sMM")
                    evQ[s] = S.op("scalar", lambda e, o=qtmb[p], rq=rQ, c_=s: [
                        e.activation(o[:, 0:512], rq[:, 0:512], AF.Copy),
                        e.activation(o[:, 512:1024], rq[:, 512:1024], AF.Copy),
                        e.activation(sb["sq"][:, :], o[:, :], AF.Square, accum_out=sb["qnorm"][:, c_:c_ + 1]),
                    ], waits=[("sMM", qmm)], inc="sSC")
                if s - 1 >= 0:
                    bs = s - 1
                    pb = bs % 2
                    tb0 = bs * 128
                    qtr = S.op("tensor", lambda e, i=qtmb[pb]: [
                        e.transpose(pT2[:, c * 128:(c + 1) * 128], i[:, c * 128:(c + 1) * 128], sb["ident"][:, :])
                        for c in range(8)],
                        waits=[("sSC", evQ[bs]), ("sVE", qraw_ev[bs - 1])], inc="sMM")
                    qraw_ev[bs] = S.op("vector", lambda e, o=qraw[pb]: e.tensor_copy(o[:, :], pT2[:, :]),
                                       waits=[("sMM", qtr)], inc="sVE")
                    qv = qfmt[pb][:, :].rearrange("p (c t) -> p c t", t=128)
                    ropeQ[bs] = self.rope(qraw[pb], cosb[pb], sinb[pb], qv[:, 0:4, :], qv[:, 4:8, :],
                                          waits=[("sVE", qraw_ev[bs]), ("sGD", cos_in[bs]),
                                                 ("sGP", spill[bs - 2])])
                    qview = dr["qfm_s"][:, :].rearrange("p (c t) -> p c t", t=2048)[:, :, tb0:tb0 + 128]
                    spill[bs] = self.dma("gpsimd", qview,
                                         qfmt[pb][:, :].rearrange("p (c t) -> p c t", t=128),
                                         waits=[("sVE", ropeQ[bs])])
            vmax = S.op("vector", lambda e: [
                e.reduce_max(sb["tiny"][:, 0:1], sb["knorm"][:, :], axis=AX.X),
                e.reduce_max(sb["tiny"][:, 1:2], sb["qnorm"][:, :], axis=AX.X),
            ], waits=[("sSC", evQ[NT_OWN - 1])], inc="sVE")
            scast = S.op("scalar", lambda e: e.activation(sb["rsc"][:, 0:2], sb["tiny"][:, 0:2], AF.Copy),
                         waits=[("sVE", vmax)], inc="sSC")
            ttr = S.op("tensor", lambda e: e.transpose(pT[0:2, 0:128], sb["rsc"][:, 0:2], sb["ident"][:, :]),
                       waits=[("sSC", scast), ("sVE", qraw_ev[NT_OWN - 1])], inc="sMM")
            vred = S.op("vector", lambda e: e.reduce_max(sb["tiny"][0:2, 3:4], pT[0:2, 0:128], axis=AX.X),
                        waits=[("sMM", ttr)], inc="sVE")
            scast2 = S.op("scalar", lambda e: e.activation(sb["rsc"][0:2, 2:3], sb["tiny"][0:2, 3:4], AF.Copy),
                          waits=[("sVE", vred)], inc="sSC")
            ttr2 = S.op("tensor", lambda e: e.transpose(pT[0:1, 128:130], sb["rsc"][0:2, 2:3], sb["ident"][0:2, 0:2]),
                        waits=[("sSC", scast2)], inc="sMM")
            vcp = S.op("vector", lambda e: e.tensor_copy(sb["tiny"][0:1, 9:11], pT[0:1, 128:130]),
                       waits=[("sMM", ttr2)], inc="sVE")
            vprod = S.op("vector", lambda e: e.tensor_mul(
                sb["tiny"][0:1, 3:4], sb["tiny"][0:1, 9:10], sb["tiny"][0:1, 10:11]),
                waits=[("sVE", vcp)], inc="sVE")
            sneg = S.op("scalar", lambda e: [
                e.activation(sb["tiny"][0:1, 4:5], sb["tiny"][0:1, 3:4], AF.Ln),
                e.activation(sb["tiny"][0:1, 5:6], sb["tiny"][0:1, 4:5], AF.Exp, scale=0.5),
            ], waits=[("sVE", vprod)], inc="sSC")
            vneg = S.op("vector", lambda e: e.tensor_scalar_mul(
                sb["tiny"][0:1, 6:7], sb["tiny"][0:1, 5:6], -1.05), waits=[("sSC", sneg)], inc="sVE")
            S.op("scalar", lambda e: e.activation(sb["negm_row"][:, :], sb["ones_rf"][0:1, 0:256],
                                                  AF.Identity, scale=sb["tiny"][0:1, 6:7]),
                 waits=[("sVE", vneg)], inc="sSC")
            if self.dbg == "p0":
                for c in range(8):
                    self.dma("gpsimd", dr["dbg_kfm"][:, c * NSEQ:(c + 1) * NSEQ],
                             sb["K_fm"][:, c * NSEQ:(c + 1) * NSEQ],
                             waits=[("sVE", self.C["sVE"]), ("sSC", self.C["sSC"])])
                    self.dma("gpsimd", dr["dbg_vtm"][:, c * NSEQ:(c + 1) * NSEQ],
                             sb["V_tm"][:, c * NSEQ:(c + 1) * NSEQ])
                self.dma("gpsimd", dr["dbg_negm"][:, :], sb["negm_row"][:, :])
                self.dma("gpsimd", dr["dbg_tiny"][:, :], sb["tiny"][:, :])
                self.dma("gpsimd", dr["dbg_kn"][:, :], sb["knorm"][:, :])
                self.dma("gpsimd", dr["dbg_qn"][:, :], sb["qnorm"][:, :])
            self.end()

    def p1(self):
        nc, sb, dr, st = self.nc, self.sb, self.dram, self.stack
        with ExitStack() as ph:
            T = lambda n, s, dt: ph.enter_context(nc.sbuf_tensor(n, s, dt))
            qg = [T(f"c_qg{p}", [128, 2048], BF16) for p in range(3)]
            maskb = [T(f"c_mask{p}", [128, 1024], F32) for p in range(3)]
            x0t = [T(f"c_x0t{p}", [128, D], F32) for p in range(2)]
            x1t = [T(f"c_x1t{p}", [128, D], F32) for p in range(2)]
            PT = [T(f"c_PT{p}", [128, 256], BF16) for p in range(3)]
            rcp = T("c_rcp", [128, 2], F32)
            pS = [ph.enter_context(nc.psum_tensor(f"c_pS{i}", [128, 512], F32)) for i in range(3)]
            av = [ph.enter_context(nc.psum_tensor(f"c_av{i}", [128, 512], F32)) for i in range(4)]
            pden = ph.enter_context(nc.psum_tensor("c_pden", [128, 512], F32))
            self.begin()
            S = self.S
            Kv = sb["K_fm"]
            n = 0
            expd = {}
            avmm = {}
            qg_lastuse = [0, 0, 0]
            mk_lastuse = [0, 0, 0]
            x0_in = [0, 0]
            epi_v = 0
            epi_prev = 0
            x1_store = [0] * (NT_OWN + 2)
            recip_done = 0
            prefetched = {}
            pend_epi = [None]

            def emit_group_dmas(g_):
                gp_ = g_ % 3
                qview_ = dr["qfm_s"][:, :].rearrange("p (c t) -> p c t", t=2048)[:, :, g_ * 256:(g_ + 1) * 256]
                self.dma("sync", qg[gp_][:, :].rearrange("p (c t) -> p c t", t=256),
                         qview_, waits=[("sMM", qg_lastuse[gp_])])
                self.dma("sync", maskb[gp_][:, :], dr["p1mask"][g_, :, :],
                         waits=[("sVE", mk_lastuse[gp_])])
                prefetched[g_] = self.ckpt("sync")

            for g in range(8):
                gp = g % 3
                L = 4 * g + 4
                if g not in prefetched:
                    emit_group_dmas(g)
                grp_rdy = prefetched[g]
                for kt in range(L):
                    p = n % 3
                    def mmS(e, g_=g, kt_=kt, p_=p, qb=qg[gp]):
                        r = []
                        for dch in range(8):
                            r.append(e.matmul(
                                pS[p_][:, 0:256],
                                Kv[:, dch * NSEQ + kt_ * 128: dch * NSEQ + (kt_ + 1) * 128],
                                qb[:, dch * 256:(dch + 1) * 256],
                                start=(dch == 0), stop=False))
                        r.append(e.matmul(pS[p_][:, 0:256], sb["ones_rb"][0:1, 0:128],
                                          sb["negm_row"][0:1, 0:256], start=False, stop=True))
                        return r
                    smm = S.op("tensor", mmS,
                               waits=[("sRD", grp_rdy), ("sSC", expd.get(n - 3, 0))],
                               inc="sMM")
                    qg_lastuse[gp] = smm
                    m = kt - 4 * g
                    if m >= 0:
                        maskd = S.op("vector", lambda e, p_=p, m_=m, gp_=gp: e.tensor_add(
                            pS[p_][:, 0:256], pS[p_][:, 0:256], maskb[gp_][:, m_ * 256:(m_ + 1) * 256]),
                            waits=[("sMM", smm), ("sRD", grp_rdy)], inc="sVE")
                        mk_lastuse[gp] = maskd
                        ew = [("sMM", smm), ("sVE", maskd), ("sMM", avmm.get(n - 3, 0))]
                    else:
                        ew = [("sMM", smm), ("sMM", avmm.get(n - 3, 0))]
                    expd[n] = S.op("scalar", lambda e, p_=p: e.activation(
                        PT[p_][:, :], pS[p_][:, 0:256], AF.Exp, scale=1.0 / 32.0), waits=ew, inc="sSC")
                    if kt == 0 and pend_epi[0] is not None:
                        pend_epi[0]()
                        pend_epi[0] = None
                    if kt == 1:
                        if g + 1 < 8:
                            emit_group_dmas(g + 1)
                        for qc in range(2):
                            slot = 2 * g + qc
                            self.dma("sync", x0t[qc][:, :],
                                     dr["x0own"][slot * 128:(slot + 1) * 128, :],
                                     waits=[("sVE", epi_prev)])
                        x0_in[0] = x0_in[1] = self.ckpt("sync")
                    def mmAV(e, kt_=kt, p_=p, st_=(kt == 0), sp_=(kt == L - 1)):
                        r = []
                        for qc in range(2):
                            for ec in range(2):
                                r.append(e.matmul(
                                    av[qc * 2 + ec][:, 0:512],
                                    PT[p_][:, qc * 128:(qc + 1) * 128],
                                    sb["V_tm"][:, kt_ * D + ec * 512: kt_ * D + (ec + 1) * 512],
                                    start=st_, stop=sp_))
                        for qc in range(2):
                            r.append(e.matmul(pden[:, qc:qc + 1], PT[p_][:, qc * 128:(qc + 1) * 128],
                                              sb["ones_cb"][:, :],
                                              start=(st_ and qc == 0), stop=(sp_ and qc == 1)))
                        return r
                    w = [("sSC", expd[n])]
                    if kt == 0:
                        w += [("sVE", epi_v), ("sVE", recip_done)]
                    avmm[n] = S.op("tensor", mmAV, waits=w, inc="sMM")
                    n += 1
                def mk_epi(g_=g, last_av=avmm[n - 1], x0r=x0_in[0]):
                    nonlocal epi_v, epi_prev, recip_done
                    recip_done = S.op("vector", lambda e: e.reciprocal(rcp[:, 0:2], pden[:, 0:2]),
                                      waits=[("sMM", last_av)], inc="sVE")
                    for qc in range(2):
                        slot = 2 * g_ + qc
                        epi = S.op("vector", lambda e, qc_=qc: [
                            e.scalar_tensor_tensor(x1t[qc_][:, ec * 512:(ec + 1) * 512],
                                                   av[qc_ * 2 + ec][:, 0:512], rcp[:, qc_:qc_ + 1],
                                                   x0t[qc_][:, ec * 512:(ec + 1) * 512],
                                                   ALU.mult, ALU.add)
                            for ec in range(2)],
                            waits=[("sVE", recip_done), ("sRD", x0r),
                                   ("sGP", x1_store[slot - 2])], inc="sVE")
                        x1_store[slot] = self.dma("gpsimd", dr["x1s"][slot * 128:(slot + 1) * 128, :],
                                                  x1t[qc][:, :], waits=[("sVE", epi)])
                        epi_v = epi
                    epi_prev = epi_v
                pend_epi[0] = mk_epi
            pend_epi[0]()
            self.end()

    def p2w(self):
        nc, sb, dr = self.nc, self.sb, self.dram
        self.alloc("wdqT", [128, 8 * D], BF16, side="right")
        self.alloc("mxb_bf", [1, D], BF16)
        with ExitStack() as ph:
            T = lambda n, s, dt: ph.enter_context(nc.sbuf_tensor(n, s, dt))
            wbuf = [T(f"w_wbuf{p}", [128, D], F32) for p in range(2)]
            wq_bf = [T(f"w_wq{p}", [128, D], BF16) for p in range(2)]
            z = [T(f"w_z{i}", [128, D], F32) for i in range(2)]
            zsq = [T(f"w_zsq{i}", [128, D], F32) for i in range(2)]
            mg = [T(f"w_mg{i}", [128, D], F32) for i in range(2)]
            mxbf = T("w_mxbf", [1, D], F32)
            pT = ph.enter_context(nc.psum_tensor("w_pT", [128, 1024], BF16))
            self.begin()
            S = self.S
            self.dma("sync", mxbf[:, :], dr["mxb"][:, :])
            rdy0 = self.ckpt("sync")
            S.op("scalar", lambda e: e.activation(sb["mxb_bf"][:, :], mxbf[:, :], AF.Copy),
                 waits=[("sRD", rdy0)], inc="sSC")
            vq_l = [0] * (8 + 2)
            tr_l = [0] * (8 + 2)
            ev_prev = 0
            wv = sb["wdqT"][:, :].rearrange("p (c t) -> p c t", t=D)
            for wt in range(8):
                p = wt % 2
                self.dma("sync", wbuf[p][:, :], dr["mxw"][wt * 128:(wt + 1) * 128, :],
                         waits=[("sVE", vq_l[wt - 2])])
                rdy = self.ckpt("sync")
                vq_l[wt] = self.quant(wbuf[p], wq_bf[p], z[p], zsq[p], mg[p], ("sRD", rdy), p,
                                      vq_waits=[("sMM", tr_l[wt - 2])])
                tr_l[wt], ev_prev = self.tr8(wq_bf[p], pT, wv[:, :, wt * 128:(wt + 1) * 128],
                                             waits=[("sVE", vq_l[wt]), ("sVE", ev_prev)])
            self.end()

    def p2(self):
        nc, sb, dr = self.nc, self.sb, self.dram
        with ExitStack() as ph:
            T = lambda n, s, dt: ph.enter_context(nc.sbuf_tensor(n, s, dt))
            x1t = [T(f"f_x1{p}", [128, D], F32) for p in range(3)]
            ht = [T(f"f_h{p}", [128, D], F32) for p in range(3)]
            xd = [T(f"f_xd{p}", [128, D], BF16) for p in range(2)]
            xdfm = [T(f"f_xdfm{p}", [128, D], BF16) for p in range(2)]
            x2t = [T(f"f_x2{p}", [128, D], F32) for p in range(2)]
            z = [T(f"f_z{i}", [128, D], F32) for i in range(2)]
            zsq = [T(f"f_zsq{i}", [128, D], F32) for i in range(2)]
            mg = [T(f"f_mg{i}", [128, D], F32) for i in range(2)]
            pT = ph.enter_context(nc.psum_tensor("f_pT", [128, 1024], BF16))
            pp = [ph.enter_context(nc.psum_tensor(f"f_pp{i}", [128, 1024], F32)) for i in range(2)]
            self.begin()
            S = self.S
            lnap = [0] * (NT_OWN + 2)
            vq_l = [0] * (NT_OWN + 2)
            trx = [0] * (NT_OWN + 2)
            fmev = [0] * (NT_OWN + 2)
            mmv = [0] * (NT_OWN + 2)
            epi = [0] * (NT_OWN + 2)
            st_l = [0] * (NT_OWN + 2)
            def front(s):
                p3 = s % 3
                p = s % 2
                p4 = s % 4
                self.dma("sync", x1t[p3][:, :], dr["x1s"][s * 128:(s + 1) * 128, :],
                         waits=[("sVE", epi[s - 3])])
                rdy = self.ckpt("sync")
                lnap[s] = self.ln_stats_apply(x1t[p3], ht[p3], ("sRD", rdy), p4,
                                              extra_waits=[("sVE", vq_l[s - 3])],
                                              sum_on_scalar=True)
                vq_l[s] = self.quant(ht[p3], xd[p], z[p], zsq[p], mg[p], ("sSC", lnap[s]), p4,
                                     vq_waits=[("sMM", trx[s - 2])])
                trx[s], fmev[s] = self.tr8(xd[p], pT, xdfm[p][:, :],
                                           waits=[("sVE", vq_l[s]), ("sVE", fmev[s - 1])],
                                           evac_waits=[("sMM", mmv[s - 1])])
                mmv[s] = S.op("tensor", lambda e, xf=xdfm[p], rp=pp[s % 2]: self.proj_mm(
                    e, rp, xf, sb["wdqT"], sb["mxb_bf"]),
                    waits=[("sVE", fmev[s]), ("sVE", epi[s - 2])], inc="sMM")

            def back(s):
                if s < 0:
                    return
                p = s % 2
                p3 = s % 3
                epi[s] = S.op("vector", lambda e, p_=p, p3_=p3, rp=pp[s % 2]: [
                    e.tensor_add(x2t[p_][:, 0:512], rp[:, 0:512], x1t[p3_][:, 0:512]),
                    e.tensor_add(x2t[p_][:, 512:1024], rp[:, 512:1024], x1t[p3_][:, 512:1024]),
                ], waits=[("sMM", mmv[s]), ("sGP", st_l[s - 2])], inc="sVE")
                st_l[s] = self.dma("gpsimd", dr["x2s"][s * 128:(s + 1) * 128, :], x2t[p][:, :],
                                   waits=[("sVE", epi[s])])

            front(0)
            for s in range(NT_OWN):
                if s + 1 < NT_OWN:
                    front(s + 1)
                back(s)
            self.end()

    def p3w(self):
        nc, sb, dr = self.nc, self.sb, self.dram
        self.alloc("bo_bf", [1, D], BF16)
        self.alloc("bf2_bf", [1, D], BF16)
        self.alloc("bd_bf", [1, D], BF16)
        self.alloc("Wff1", [128, 8 * 4 * D], BF16)
        self.alloc("negmk_row", [1, D], BF16)
        self.alloc("bq_r", [1, D], BF16, side="right")
        self.alloc("bk_r", [1, D], BF16, side="right")
        self.alloc("bv_r", [1, D], BF16, side="right")
        self.alloc("wo_bf", [128, 8 * D], BF16, side="right")
        self.alloc("Wq_b", [128, 8 * D], BF16, side="right")
        self.alloc("Wk_b", [128, 8 * D], BF16, side="right")
        self.alloc("Wv_b", [128, 8 * D], BF16, side="right")
        self.alloc("b1row", [1, 4 * D], BF16, side="right")
        with ExitStack() as ph:
            T = lambda n, s, dt: ph.enter_context(nc.sbuf_tensor(n, s, dt))
            stage = [T(f"g_st{p}", [128, D], F32) for p in range(2)]
            raw_bf = [T(f"g_raw{p}", [128, D], BF16) for p in range(2)]
            g1col = T("g_g1", [128, 8], F32)
            g2col = T("g_g2", [128, 8], F32)
            b1bf = T("g_b1bf", [128, 8], BF16)
            b2bf = T("g_b2bf", [128, 8], BF16)
            sqb = T("g_sqb", [128, D], BF16)
            rowf = T("g_rowf", [1, D], F32)
            rowf2 = T("g_rowf2", [1, D], F32)
            self.begin()
            S = self.S
            self.dma("sync", g1col[:, :], dr["g1c"][:, :])
            self.dma("sync", g2col[:, :], dr["g2c"][:, :])
            self.dma("gpsimd", sb["bo_bf"][:, :], dr["bo_r"][:, :])
            self.dma("gpsimd", sb["bf2_bf"][:, :], dr["bf2_r"][:, :])
            self.dma("gpsimd", sb["bd_bf"][:, :], dr["bd_r"][:, :])
            self.dma("gpsimd", b1bf[:, :], dr["b1c"][:, :])
            self.dma("gpsimd", b2bf[:, :], dr["b2c"][:, :])
            bcast_rdy = self.ckpt("gpsimd")
            for jc in range(8):
                self.dma("gpsimd", sb["wo_bf"][:, jc * D:(jc + 1) * D],
                         dr["bwo"][jc * 128:(jc + 1) * 128, :])

            with ExitStack() as pA:
                pRow = [pA.enter_context(nc.psum_tensor(f"g_pR{i}", [1, 512], F32)) for i in range(2)]
                pCS = [pA.enter_context(nc.psum_tensor(f"g_pC{i}", [1, 512], F32)) for i in range(2)]

                def wprep(wname, dest, brow, colsq):
                    last_mm = 0
                    last_raw = [0, 0]
                    cs = 0
                    for jc in range(8):
                        p = jc % 2
                        self.dma("sync", stage[p][:, :], dr[wname][jc * 128:(jc + 1) * 128, :],
                                 waits=[("sSC", last_raw[p])])
                        rd = self.ckpt("sync")
                        rc = S.op("scalar", lambda e, st_=stage[p], rb=raw_bf[p]: e.activation(
                            rb[:, :], st_[:, :], AF.Copy),
                            waits=[("sRD", rd), ("sMM", last_mm)], inc="sSC")
                        last_mm = S.op("tensor", lambda e, jc_=jc, rb=raw_bf[p]: [
                            e.matmul(pRow[h][0:1, 0:512], b1bf[:, jc_:jc_ + 1],
                                     rb[:, h * 512:(h + 1) * 512],
                                     start=(jc_ == 0), stop=(jc_ == 7))
                            for h in range(2)],
                            waits=[("sSC", rc), ("sGD", bcast_rdy)], inc="sMM")
                        gsc = S.op("scalar", lambda e, st_=stage[p], jc_=jc: e.activation(
                            dest[:, jc_ * D:(jc_ + 1) * D], st_[:, :], AF.Identity,
                            scale=g1col[:, jc_:jc_ + 1]),
                            waits=[("sRD", rd)], inc="sSC")
                        last_raw[p] = gsc
                        if colsq:
                            sq2 = S.op("scalar", lambda e, jc_=jc: e.activation(
                                sqb[:, :], dest[:, jc_ * D:(jc_ + 1) * D], AF.Square),
                                waits=[("sSC", gsc)], inc="sSC")
                            cs = S.op("tensor", lambda e, jc_=jc: [
                                e.matmul(pCS[h][0:1, 0:512], sb["ones_cb"][:, :],
                                         sqb[:, h * 512:(h + 1) * 512],
                                         start=(jc_ == 0), stop=(jc_ == 7))
                                for h in range(2)],
                                waits=[("sSC", sq2)], inc="sMM")
                    ev = S.op("vector", lambda e: [
                        e.tensor_copy(brow[0:1, 0:512], pRow[0][0:1, 0:512]),
                        e.tensor_copy(brow[0:1, 512:1024], pRow[1][0:1, 0:512]),
                    ], waits=[("sMM", last_mm)], inc="sVE")
                    return ev, cs

                evq, _ = wprep("bwq", sb["Wq_b"], sb["bq_r"], False)
                evk, cs_done = wprep("bwk", sb["Wk_b"], sb["bk_r"], True)
                evv, _ = wprep("bwv", sb["Wv_b"], sb["bv_r"], False)
                vcs = S.op("vector", lambda e: [
                    e.tensor_copy(rowf2[0:1, 0:512], pCS[0][0:1, 0:512]),
                    e.tensor_copy(rowf2[0:1, 512:1024], pCS[1][0:1, 0:512]),
                ], waits=[("sMM", cs_done)], inc="sVE")
                vm1 = S.op("vector", lambda e: e.reduce_max(sb["tiny"][0:1, 0:1], rowf2[0:1, 0:1024],
                                                            axis=AX.X), waits=[("sVE", vcs)], inc="sVE")
                vm2 = S.op("vector", lambda e: e.reduce_max(sb["tiny"][0:1, 1:2], sb["bk_r"][0:1, :],
                                                            axis=AX.X, apply_absolute_value=True),
                           waits=[("sVE", evk)], inc="sVE")
                sm1 = S.op("scalar", lambda e: [
                    e.activation(sb["tiny"][0:1, 2:3], sb["tiny"][0:1, 0:1], AF.Ln),
                    e.activation(sb["tiny"][0:1, 3:4], sb["tiny"][0:1, 2:3], AF.Exp, scale=0.5),
                ], waits=[("sVE", vm1)], inc="sSC")
                vm3 = S.op("vector", lambda e: e.tensor_scalar_mul(
                    sb["tiny"][0:1, 4:5], sb["tiny"][0:1, 3:4], 32.1),
                    waits=[("sSC", sm1)], inc="sVE")
                vm4 = S.op("vector", lambda e: e.tensor_add(
                    sb["tiny"][0:1, 5:6], sb["tiny"][0:1, 4:5], sb["tiny"][0:1, 1:2]),
                    waits=[("sVE", vm3), ("sVE", vm2)], inc="sVE")
                vm5 = S.op("vector", lambda e: e.tensor_scalar_mul(
                    sb["tiny"][0:1, 6:7], sb["tiny"][0:1, 5:6], -1.02),
                    waits=[("sVE", vm4)], inc="sVE")
                S.op("scalar", lambda e: e.activation(sb["negmk_row"][:, :], sb["ones_rf"][0:1, :],
                                                      AF.Identity, scale=sb["tiny"][0:1, 6:7]),
                     waits=[("sVE", vm5)], inc="sSC")
            with ExitStack() as pB:
                pFF = [pB.enter_context(nc.psum_tensor(f"g_pF{i}", [1, 512], F32)) for i in range(8)]
                last_mm = 0
                last_raw = [0, 0]
                for jc in range(8):
                    for qtr in range(4):
                        p = (jc * 4 + qtr) % 2
                        self.dma("sync", stage[p][:, :],
                                 dr["bff1"][jc * 128:(jc + 1) * 128, qtr * D:(qtr + 1) * D],
                                 waits=[("sSC", last_raw[p])])
                        rd = self.ckpt("sync")
                        rc = S.op("scalar", lambda e, st_=stage[p], rb=raw_bf[p]: e.activation(
                            rb[:, :], st_[:, :], AF.Copy),
                            waits=[("sRD", rd), ("sMM", last_mm)], inc="sSC")
                        last_mm = S.op("tensor", lambda e, jc_=jc, q_=qtr, rb=raw_bf[p]: [
                            e.matmul(pFF[q_ * 2 + h][0:1, 0:512], b2bf[:, jc_:jc_ + 1],
                                     rb[:, h * 512:(h + 1) * 512],
                                     start=(jc_ == 0), stop=(jc_ == 7))
                            for h in range(2)],
                            waits=[("sSC", rc)], inc="sMM")
                        last_raw[p] = S.op("scalar", lambda e, st_=stage[p], jc_=jc, q_=qtr: e.activation(
                            sb["Wff1"][:, jc_ * 4 * D + q_ * D: jc_ * 4 * D + (q_ + 1) * D],
                            st_[:, :], AF.Identity, scale=g2col[:, jc_:jc_ + 1]),
                            waits=[("sRD", rd)], inc="sSC")
                prev_cast = 0
                for i in range(4):
                    self.dma("sync", rowf[0:1, :], dr["bf1_r"][0:1, i * D:(i + 1) * D],
                             waits=[("sSC", prev_cast)])
                    rdf = self.ckpt("sync")
                    vr = S.op("vector", lambda e, i_=i: [
                        e.tensor_copy(rowf2[0:1, 0:512], pFF[2 * i_][0:1, 0:512]),
                        e.tensor_copy(rowf2[0:1, 512:1024], pFF[2 * i_ + 1][0:1, 0:512]),
                    ], waits=[("sMM", last_mm), ("sSC", prev_cast)], inc="sVE")
                    vr2 = S.op("vector", lambda e: e.tensor_add(rowf2[0:1, :], rowf2[0:1, :],
                                                                rowf[0:1, :]),
                               waits=[("sVE", vr), ("sRD", rdf)], inc="sVE")
                    prev_cast = S.op("scalar", lambda e, i_=i: e.activation(
                        sb["b1row"][0:1, i_ * D:(i_ + 1) * D], rowf2[0:1, :], AF.Copy),
                        waits=[("sVE", vr2)], inc="sSC")
                self.dma("gpsimd", dr["b1row_s"][:, :], sb["b1row"][0:1, :],
                         waits=[("sSC", prev_cast)])
            self.end()

    def p3a(self):
        nc, sb, dr = self.nc, self.sb, self.dram
        with ExitStack() as ph:
            T = lambda n, s, dt: ph.enter_context(nc.sbuf_tensor(n, s, dt))
            x2t = [T(f"h_x2{p}", [128, D], F32) for p in range(2)]
            n2b = [T(f"h_n2{p}", [128, D], BF16) for p in range(2)]
            n2fm = [T(f"h_n2fm{p}", [128, D], BF16) for p in range(2)]
            qet = [T(f"h_qe{p}", [128, D], BF16) for p in range(2)]
            qefm = [T(f"h_qefm{p}", [128, D], BF16) for p in range(2)]
            ekt = [T(f"h_ek{p}", [128, D], BF16) for p in range(2)]
            v2b = [T(f"h_v2{p}", [128, D], BF16) for p in range(2)]
            U_sb = T("h_Usb", [128, 2056], F32)
            pT = ph.enter_context(nc.psum_tensor("h_pT", [128, 1024], BF16))
            pP = ph.enter_context(nc.psum_tensor("h_pP", [128, 1024], F32))
            Ub = [ph.enter_context(nc.psum_tensor(f"h_U{i}", [128, 512], F32)) for i in range(4)]
            Ucol = ph.enter_context(nc.psum_tensor("h_Uc", [128, 512], F32))
            self.begin()
            S = self.S
            lnap = [0] * (NT_OWN + 2)
            trn = [0] * (NT_OWN + 2)
            fmev = [0] * (NT_OWN + 2)
            qev = [0] * (NT_OWN + 2)
            qtr_l = [0] * (NT_OWN + 2)
            qfm_ev = [0] * (NT_OWN + 2)
            spl = [0] * (NT_OWN + 2)
            ekev = [0] * (NT_OWN + 2)
            vev = [0] * (NT_OWN + 2)
            umm = [0] * (NT_OWN + 2)
            vrq_l = [0] * (NT_OWN + 2)
            def front(s):
                p = s % 2
                self.dma("sync", x2t[p][:, :], dr["x2s"][s * 128:(s + 1) * 128, :],
                         waits=[("sSC", lnap[s - 2])])
                rdy = self.ckpt("sync")
                lnap[s] = self.ln_stats_apply(x2t[p], n2b[p], ("sRD", rdy), s % 4,
                                              extra_waits=[("sMM", trn[s - 2])],
                                              fused2=True,
                                              save=(sb["ln1_r"], sb["ln1_b"], s))

            def trn_emit(s):
                p = s % 2
                trn[s], fmev[s] = self.tr8(n2b[p], pT, n2fm[p][:, :],
                                           waits=[("sSC", lnap[s]), ("sVE", qfm_ev[s - 1])],
                                           evac_waits=[("sMM", umm[s - 1])])

            def back(s):
                p = s % 2
                mmq = S.op("tensor", lambda e, xf=n2fm[p]: self.proj_mm(
                    e, pP, xf, sb["Wq_b"], sb["bq_r"]),
                    waits=[("sVE", fmev[s]), ("sSC", vev[s - 1]), ("sVE", vrq_l[s - 1])], inc="sMM")
                if s + 1 < NT_OWN:
                    front(s + 1)
                vq1 = S.op("vector", lambda e: [
                    e.reduce_max(sb["qt"][:, hh:hh + 1], pP[:, hh * 256:(hh + 1) * 256], axis=AX.X)
                    for hh in range(4)], waits=[("sMM", mmq)], inc="sVE")
                vq2 = S.op("vector", lambda e: e.tensor_scalar_mul(sb["qt"][:, 4:8], sb["qt"][:, 0:4],
                                                                   -0.25),
                           waits=[("sVE", vq1)], inc="sVE")
                qev[s] = S.op("scalar", lambda e, o=qet[p]: [
                    e.activation(o[:, hh * 256:(hh + 1) * 256], pP[:, hh * 256:(hh + 1) * 256],
                                 AF.Exp, scale=0.25, bias=sb["qt"][:, 4 + hh:5 + hh],
                                 accum_out=sb["qt"][:, 8 + hh:9 + hh])
                    for hh in range(4)], waits=[("sVE", vq2), ("sMM", qtr_l[s - 2])], inc="sSC")
                vrq_l[s] = S.op("vector", lambda e, s_=s: e.reciprocal(
                    sb["rq"][:, 4 * s_:4 * s_ + 4], sb["qt"][:, 8:12]),
                    waits=[("sSC", qev[s])], inc="sVE")
                qtr_l[s], qfm_ev[s] = self.tr8(qet[p], pT, qefm[p][:, :],
                                               waits=[("sSC", qev[s]), ("sVE", fmev[s])],
                                               evac_waits=[("sGP", spl[s - 2])])
                qv = dr["qefm_s"][:, :].rearrange("p (c t) -> p c t", t=2048)[:, :, s * 128:(s + 1) * 128]
                spl[s] = self.dma("gpsimd", qv,
                                  qefm[p][:, :].rearrange("p (c t) -> p c t", t=128),
                                  waits=[("sVE", qfm_ev[s])])
                mmk = S.op("tensor", lambda e, xf=n2fm[p]: self.proj_mm(
                    e, pP, xf, sb["Wk_b"], sb["bk_r"], extra_rows=(sb["negmk_row"],)),
                    waits=[("sSC", qev[s]), ("sVE", vq1)], inc="sMM")
                ekev[s] = S.op("scalar", lambda e, o=ekt[p]: [
                    e.activation(o[:, 0:512], pP[:, 0:512], AF.Exp, scale=0.25),
                    e.activation(o[:, 512:1024], pP[:, 512:1024], AF.Exp, scale=0.25),
                ], waits=[("sMM", mmk), ("sMM", umm[s - 1])], inc="sSC")
                mmv = S.op("tensor", lambda e, xf=n2fm[p]: self.proj_mm(
                    e, pP, xf, sb["Wv_b"], sb["bv_r"]),
                    waits=[("sSC", ekev[s])], inc="sMM")
                vev[s] = S.op("scalar", lambda e, o=v2b[p]: [
                    e.activation(o[:, 0:512], pP[:, 0:512], AF.Copy),
                    e.activation(o[:, 512:1024], pP[:, 512:1024], AF.Copy),
                ], waits=[("sMM", mmv)], inc="sSC")
                def mmU(e, s_=s, ek=ekt[p], v_=v2b[p]):
                    r = []
                    for dc in range(8):
                        hh = dc >> 1
                        r.append(e.matmul(Ub[hh][:, (dc & 1) * 256:((dc & 1) + 1) * 256],
                                          ek[:, dc * 128:(dc + 1) * 128],
                                          v_[:, hh * 256:(hh + 1) * 256],
                                          start=(s_ == 0 and (dc & 1) == 0),
                                          stop=(s_ == NT_OWN - 1 and (dc & 1) == 1)))
                    for dc in range(8):
                        r.append(e.matmul(Ucol[:, dc:dc + 1], ek[:, dc * 128:(dc + 1) * 128],
                                          sb["ones_cb"][:, :],
                                          start=(s_ == 0 and dc == 0),
                                          stop=(s_ == NT_OWN - 1 and dc == 7)))
                    return r
                umm[s] = S.op("tensor", mmU, waits=[("sSC", ekev[s]), ("sSC", vev[s])], inc="sMM")

            front(0)
            trn_emit(0)
            for s in range(NT_OWN):
                back(s)
                if s + 1 < NT_OWN:
                    trn_emit(s + 1)
            uev = S.op("scalar", lambda e: [
                e.activation(U_sb[:, bh * 512:(bh + 1) * 512], Ub[bh][:, 0:512], AF.Copy)
                for bh in range(4)], waits=[("sMM", umm[NT_OWN - 1])], inc="sSC")
            ucv = S.op("vector", lambda e: e.tensor_copy(U_sb[:, 2048:2056], Ucol[:, 0:8]),
                       waits=[("sMM", umm[NT_OWN - 1])], inc="sVE")
            self.dma("gpsimd", dr["cc_in"][:, :], U_sb[:, :],
                     waits=[("sSC", uev), ("sVE", ucv)])
            self.end()

    def p3x(self):
        nc, sb, dr = self.nc, self.sb, self.dram
        self.alloc("C_fm", [128, 2 * D], BF16)
        with ExitStack() as ph:
            T = lambda n, s, dt: ph.enter_context(nc.sbuf_tensor(n, s, dt))
            Ug = T("x_Ug", [128, 2056], F32)
            rcb = T("x_rc", [128, 8], F32)
            self.begin()
            S = self.S
            cc = S.op("gpsimd", lambda e: e.collective_compute(
                "AllReduce", ALU.add, replica_groups=[[0, 1], [2, 3], [4, 5], [6, 7]],
                ins=[dr["cc_in"][:, :]], outs=[dr["cc_out"][:, :]]), inc="sCC")
            self.dma("sync", Ug[:, :], dr["cc_out"][:, :], waits=[("sCC", cc)])
            rdy = self.ckpt("sync")
            vr = S.op("vector", lambda e: e.reciprocal(rcb[:, 0:8], Ug[:, 2048:2056]),
                      waits=[("sRD", rdy)], inc="sVE")
            S.op("scalar", lambda e: [
                e.activation(sb["C_fm"][:, dc * 256:(dc + 1) * 256], Ug[:, dc * 256:(dc + 1) * 256],
                             AF.Identity, scale=rcb[:, dc:dc + 1])
                for dc in range(8)], waits=[("sVE", vr)], inc="sSC")
            self.end()

    def p3e1(self):
        nc, sb, dr = self.nc, self.sb, self.dram
        with ExitStack() as ph:
            T = lambda n, s, dt: ph.enter_context(nc.sbuf_tensor(n, s, dt))
            x2t = [T(f"i_x2{p}", [128, D], F32) for p in range(2)]
            qet = [T(f"i_qe{p}", [128, D], BF16) for p in range(2)]
            atb = [T(f"i_at{p}", [128, D], BF16) for p in range(2)]
            atfm = [T(f"i_atfm{p}", [128, D], BF16) for p in range(2)]
            inpt = [T(f"i_inp{p}", [128, D], F32) for p in range(2)]
            xxt = [T(f"i_xx{p}", [128, D], F32) for p in range(2)]
            pT = ph.enter_context(nc.psum_tensor("i_pT", [128, 1024], BF16))
            pP = ph.enter_context(nc.psum_tensor("i_pP", [128, 1024], F32))
            pW = ph.enter_context(nc.psum_tensor("i_pW", [128, 1024], F32))
            self.begin()
            S = self.S
            atev = [0] * (NT_OWN + 2)
            trat = [0] * (NT_OWN + 2)
            fmev = [0] * (NT_OWN + 2)
            mmw_l = [0] * (NT_OWN + 2)
            mma_l = [0] * (NT_OWN + 2)
            xxv = [0] * (NT_OWN + 2)
            inps = [0] * (NT_OWN + 2)
            stl = [0] * (NT_OWN + 2)
            sq_l = [0] * (NT_OWN + 2)
            for s in range(NT_OWN):
                p = s % 2
                qv = dr["qefm_s"][:, :].rearrange("p (c t) -> p c t", t=2048)[:, :, s * 128:(s + 1) * 128]
                self.dma("sync", qet[p][:, :].rearrange("p (c t) -> p c t", t=128), qv,
                         waits=[("sMM", mma_l[s - 2])])
                self.dma("sync", x2t[p][:, :], dr["x2s"][s * 128:(s + 1) * 128, :],
                         waits=[("sSC", inps[s - 2]), ("sSC", sq_l[s - 2])])
                rdy = self.ckpt("sync")
                def mmA(e, q_=qet[p]):
                    r = []
                    for hh in range(4):
                        for j, dc in enumerate((2 * hh, 2 * hh + 1)):
                            bank_first = (hh % 2 == 0) and j == 0
                            r.append(e.matmul(pP[:, hh * 256:(hh + 1) * 256],
                                              q_[:, dc * 128:(dc + 1) * 128],
                                              sb["C_fm"][:, dc * 256:(dc + 1) * 256],
                                              start=bank_first, stop=(j == 1 and hh % 2 == 1)))
                    return r
                mma_l[s] = S.op("tensor", mmA, waits=[("sRD", rdy), ("sSC", atev[s - 1])], inc="sMM")
                atev[s] = S.op("scalar", lambda e, o=atb[p], s_=s: [
                    e.activation(o[:, hh * 256:(hh + 1) * 256], pP[:, hh * 256:(hh + 1) * 256],
                                 AF.Identity, scale=sb["rq"][:, 4 * s_ + hh:4 * s_ + hh + 1])
                    for hh in range(4)], waits=[("sMM", mma_l[s]), ("sMM", trat[s - 2])], inc="sSC")
                trat[s], fmev[s] = self.tr8(atb[p], pT, atfm[p][:, :],
                                            waits=[("sSC", atev[s]), ("sVE", fmev[s - 1])],
                                            evac_waits=[("sMM", mmw_l[s - 1])])
                mmw_l[s] = S.op("tensor", lambda e, xf=atfm[p]: self.proj_mm(
                    e, pW, xf, sb["wo_bf"], sb["bo_bf"]),
                    waits=[("sVE", fmev[s]), ("sVE", xxv[s - 1])], inc="sMM")
                inps[s] = S.op("scalar", lambda e, a=x2t[p], o=inpt[p], s_=s: e.activation(
                    o[:, :], a[:, :], AF.Identity, scale=sb["ln1_r"][:, s_:s_ + 1],
                    bias=sb["ln1_b"][:, s_:s_ + 1]),
                    waits=[("sRD", rdy), ("sVE", xxv[s - 2])], inc="sSC")
                xxv[s] = S.op("vector", lambda e, p_=p: [
                    e.tensor_add(xxt[p_][:, 0:512], pW[:, 0:512], inpt[p_][:, 0:512]),
                    e.tensor_add(xxt[p_][:, 512:1024], pW[:, 512:1024], inpt[p_][:, 512:1024]),
                ], waits=[("sMM", mmw_l[s]), ("sSC", inps[s]), ("sGP", stl[s - 2])], inc="sVE")
                sq_l[s] = self.ln_stats_apply(xxt[p], None, ("sVE", xxv[s]), p,
                                              save=(sb["ln2_r"], sb["ln2_b"], s))
                stl[s] = self.dma("gpsimd", dr["xxs"][s * 128:(s + 1) * 128, :], xxt[p][:, :],
                                  waits=[("sVE", xxv[s])])
            self.end()

    def p3e2(self):
        nc, sb, dr = self.nc, self.sb, self.dram
        self.alloc("ff2_bf", [128, 32 * D], BF16)
        with ExitStack() as ph:
            T = lambda n, s, dt: ph.enter_context(nc.sbuf_tensor(n, s, dt))
            xxt = [sb["sq"]] + [T(f"j_xx{i}", [128, D], F32) for i in range(3)]
            n3b = sb["rsc"]
            n3g = T("j_n3g", [128, 4 * D], BF16)
            u_g = T("j_ug", [128, 16 * D], BF16)
            b1v = T("j_b1v", [32, 128], BF16)
            b1col = T("j_b1col", [128, 32], BF16)
            pT = ph.enter_context(nc.psum_tensor("j_pT", [128, 1024], BF16))
            pF1 = [ph.enter_context(nc.psum_tensor(f"j_f1{i}", [128, 512], F32)) for i in range(3)]
            pF2 = [ph.enter_context(nc.psum_tensor(f"j_f2{i}", [128, 1024], F32)) for i in range(2)]
            self.begin()
            S = self.S
            self.dma("gpsimd", b1v[:, :],
                     dr["b1row_s"][:, :].rearrange("o (a b) -> (o a) b", b=128))
            for c in range(32):
                self.dma("gpsimd", sb["ff2_bf"][:, c * D:(c + 1) * D],
                         dr["bff2"][c * 128:(c + 1) * 128, :])
            ff2_rdy = self.ckpt("gpsimd")
            b1tr = S.op("tensor", lambda e: e.transpose(pT[:, 0:32], b1v[:, :],
                                                        sb["ident"][0:32, 0:32]),
                        waits=[("sGD", ff2_rdy)], inc="sMM")
            b1ev = S.op("vector", lambda e: e.tensor_copy(b1col[:, :], pT[:, 0:32]),
                        waits=[("sMM", b1tr)], inc="sVE")
            lnap = [0] * (NT_OWN + 4)
            trn = [0] * (NT_OWN + 4)
            fme = [0] * (NT_OWN + 4)
            epi = [0] * (NT_OWN + 4)
            stl = [0] * (NT_OWN + 4)
            mmf1_last = [0] * 6
            mmf2_l = [0] * (NT_OWN + 4)
            gel_d = {}
            gel_last = [0] * 6

            def front(t):
                self.dma("sync", xxt[t % 4][:, :], dr["xxs"][t * 128:(t + 1) * 128, :],
                         waits=[("sGP", stl[t - 4])])
                rdy = self.ckpt("sync")
                lnap[t] = S.op("scalar", lambda e, a=xxt[t % 4], t_=t: e.activation(
                    n3b[:, :], a[:, :], AF.Identity, scale=sb["ln2_r"][:, t_:t_ + 1],
                    bias=sb["ln2_b"][:, t_:t_ + 1]),
                    waits=[("sRD", rdy), ("sMM", trn[t - 1])], inc="sSC")
                trn[t] = S.op("tensor", lambda e: [
                    e.transpose(pT[:, c * 128:(c + 1) * 128], n3b[:, c * 128:(c + 1) * 128],
                                sb["ident"][:, :]) for c in range(8)],
                    waits=[("sSC", lnap[t]), ("sVE", fme[t - 1] if t else b1ev)], inc="sMM")
                dst = n3g[:, :].rearrange("p (c b t) -> p c b t", b=4, t=128)[:, :, t % 4, :]
                fme[t] = S.op("vector", lambda e, d=dst: e.tensor_copy(
                    d, pT[:, :].rearrange("p (c t) -> p c t", t=128)),
                    waits=[("sMM", trn[t]), ("sMM", mmf1_last[t // 4 - 1])], inc="sVE")

            for t in range(4):
                front(t)
            for g in range(4):
                for oc in range(32):
                    gidx = g * 32 + oc
                    q = gidx % 3
                    def mmF1(e, oc_=oc, q_=q):
                        r = []
                        for ic in range(8):
                            r.append(e.matmul(
                                pF1[q_][:, 0:512],
                                sb["Wff1"][:, ic * 4 * D + oc_ * 128: ic * 4 * D + (oc_ + 1) * 128],
                                n3g[:, ic * 512:(ic + 1) * 512],
                                start=(ic == 0), stop=(ic == 7)))
                        return r
                    w = [("sSC", gel_d.get(gidx - 3, 0))]
                    if oc == 0:
                        w += [("sVE", fme[4 * g + 3])]
                    mmf1 = S.op("tensor", mmF1, waits=w, inc="sMM")
                    gel_d[gidx] = S.op("scalar", lambda e, oc_=oc, q_=q: e.activation(
                        u_g[:, oc_ * 512:(oc_ + 1) * 512], pF1[q_][:, 0:512],
                        AF.Gelu_apprx_tanh, bias=b1col[:, oc_:oc_ + 1]),
                        waits=[("sMM", mmf1), ("sVE", b1ev)], inc="sSC")
                mmf1_last[g] = mmf1
                for tb in range(4):
                    t = 4 * g + tb
                    pf = pF2[tb % 2]
                    def mmF2(e, tb_=tb, pf_=pf):
                        r = []
                        for hx in range(2):
                            r.append(e.matmul(pf_[:, hx * 512:(hx + 1) * 512],
                                              sb["ones_rb"][0:1, 0:128],
                                              sb["bf2_bf"][0:1, hx * 512:(hx + 1) * 512],
                                              start=True, stop=False))
                        for oc in range(32):
                            for hx in range(2):
                                r.append(e.matmul(
                                    pf_[:, hx * 512:(hx + 1) * 512],
                                    u_g[:, oc * 512 + tb_ * 128: oc * 512 + tb_ * 128 + 128],
                                    sb["ff2_bf"][:, oc * D + hx * 512: oc * D + (hx + 1) * 512],
                                    start=False, stop=(oc == 31)))
                        return r
                    mmf2_l[t] = S.op("tensor", mmF2,
                                     waits=[("sSC", gel_d[g * 32 + 31]), ("sGD", ff2_rdy),
                                            ("sVE", epi[t - 2])], inc="sMM")
                    epi[t] = S.op("vector", lambda e, pf_=pf, xs=xxt[t % 4]: [
                        e.tensor_add(xs[:, 0:512], pf_[:, 0:512], xs[:, 0:512]),
                        e.tensor_add(xs[:, 512:1024], pf_[:, 512:1024], xs[:, 512:1024]),
                    ], waits=[("sMM", mmf2_l[t])], inc="sVE")
                    stl[t] = self.dma("gpsimd", dr["xx2s"][t * 128:(t + 1) * 128, :],
                                      xxt[t % 4][:, :], waits=[("sVE", epi[t])])
                    if t + 4 < NT_OWN:
                        front(t + 4)
            self.end()

    def p3e3(self):
        nc, sb, dr = self.nc, self.sb, self.dram
        self.alloc("wd_bf", [128, 8 * D], BF16, side="right")
        self.alloc("wdqT", [128, 8 * D], BF16)
        with ExitStack() as ph:
            T = lambda n, s, dt: ph.enter_context(nc.sbuf_tensor(n, s, dt))
            xx2 = [T(f"k_xx2{p}", [128, D], F32) for p in range(2)]
            x2t = [T(f"k_x2{p}", [128, D], F32) for p in range(2)]
            xxb = [T(f"k_xxb{p}", [128, D], BF16) for p in range(2)]
            xxfm = [T(f"k_xxfm{p}", [128, D], BF16) for p in range(2)]
            x3t = [T(f"k_x3{p}", [128, D], F32) for p in range(3)]
            ht = [T(f"k_h{p}", [128, D], F32) for p in range(3)]
            xd = [T(f"k_xd{p}", [128, D], BF16) for p in range(2)]
            xdfm = [T(f"k_xdfm{p}", [128, D], BF16) for p in range(2)]
            x4t = [T(f"k_x4{p}", [128, D], F32) for p in range(2)]
            z = [T(f"k_z{i}", [128, D], F32) for i in range(2)]
            zsq = [T(f"k_zsq{i}", [128, D], F32) for i in range(2)]
            mg = [T(f"k_mg{i}", [128, D], F32) for i in range(2)]
            pT = ph.enter_context(nc.psum_tensor("k_pT", [128, 1024], BF16))
            pD = [ph.enter_context(nc.psum_tensor(f"k_pD{i}", [128, 1024], F32)) for i in range(2)]
            pM = ph.enter_context(nc.psum_tensor("k_pM", [128, 1024], F32))
            self.begin()
            S = self.S
            self.dma("gpsimd", sb["wdqT"][:, :], dr["wdq_s"][:, :])
            for jc in range(8):
                self.dma("gpsimd", sb["wd_bf"][:, jc * D:(jc + 1) * D],
                         dr["bwd"][jc * 128:(jc + 1) * 128, :])
            wd_rdy = self.ckpt("gpsimd")
            cast_l = [0] * (NT_OWN + 2)
            trx = [0] * (NT_OWN + 2)
            fmev = [0] * (NT_OWN + 2)
            mmd = [0] * (NT_OWN + 2)
            x3v = [0] * (NT_OWN + 2)
            lnap = [0] * (NT_OWN + 2)
            vq_l = [0] * (NT_OWN + 2)
            trq = [0] * (NT_OWN + 2)
            fmq = [0] * (NT_OWN + 2)
            mmx = [0] * (NT_OWN + 2)
            x4v = [0] * (NT_OWN + 2)
            stl = [0] * (NT_OWN + 2)
            def stA(s):
                p = s % 2
                self.dma("sync", xx2[p][:, :], dr["xx2s"][s * 128:(s + 1) * 128, :],
                         waits=[("sSC", cast_l[s - 2])])
                self.dma("sync", x2t[p][:, :], dr["x2s"][s * 128:(s + 1) * 128, :],
                         waits=[("sVE", x3v[s - 2])])
                rdy = self.ckpt("sync")
                cast_l[s] = S.op("scalar", lambda e, a=xx2[p], o=xxb[p]: e.activation(
                    o[:, :], a[:, :], AF.Copy),
                    waits=[("sRD", rdy), ("sMM", trx[s - 2])], inc="sSC")
                trx[s], fmev[s] = self.tr8(xxb[p], pT, xxfm[p][:, :],
                                           waits=[("sSC", cast_l[s]), ("sVE", fmq[s - 3])],
                                           evac_waits=[("sMM", mmd[s - 1])])
                mmd[s] = S.op("tensor", lambda e, xf=xxfm[p], rp=pD[s % 2]: self.proj_mm(
                    e, rp, xf, sb["wd_bf"], sb["bd_bf"]),
                    waits=[("sVE", fmev[s]), ("sGD", wd_rdy), ("sVE", x3v[s - 2])], inc="sMM")

            def stB(s):
                if s < 0 or s >= NT_OWN:
                    return
                p = s % 2
                p3 = s % 3
                p4 = s % 4
                x3v[s] = S.op("vector", lambda e, p_=p3, p2_=p, rp=pD[s % 2]: [
                    e.tensor_add(x3t[p_][:, 0:512], rp[:, 0:512], x2t[p2_][:, 0:512]),
                    e.tensor_add(x3t[p_][:, 512:1024], rp[:, 512:1024], x2t[p2_][:, 512:1024]),
                ], waits=[("sMM", mmd[s]), ("sSC", lnap[s - 3]), ("sVE", x4v[s - 3])], inc="sVE")
                lnap[s] = self.ln_stats_apply(x3t[p3], ht[p3], ("sVE", x3v[s]), p4,
                                              extra_waits=[("sVE", vq_l[s - 3])],
                                              sum_on_scalar=True)
                vq_l[s] = self.quant(ht[p3], xd[p], z[p], zsq[p], mg[p], ("sSC", lnap[s]), p4,
                                     vq_waits=[("sMM", trq[s - 2])])

            def stC(s):
                if s < 0 or s >= NT_OWN:
                    return
                p = s % 2
                p3 = s % 3
                trq[s], fmq[s] = self.tr8(xd[p], pT, xdfm[p][:, :],
                                          waits=[("sVE", vq_l[s]),
                                                 ("sVE", fmev[s + 2] if s + 2 < NT_OWN else fmev[NT_OWN - 1])],
                                          evac_waits=[("sMM", mmx[s - 1])])
                mmx[s] = S.op("tensor", lambda e, xf=xdfm[p]: self.proj_mm(
                    e, pM, xf, sb["wdqT"], sb["mxb_bf"]),
                    waits=[("sVE", fmq[s]), ("sVE", x4v[s - 1]), ("sGD", wd_rdy)], inc="sMM")
                x4v[s] = S.op("vector", lambda e, p_=p, p3_=p3: [
                    e.tensor_add(x4t[p_][:, 0:512], pM[:, 0:512], x3t[p3_][:, 0:512]),
                    e.tensor_add(x4t[p_][:, 512:1024], pM[:, 512:1024], x3t[p3_][:, 512:1024]),
                ], waits=[("sMM", mmx[s]), ("sGP", stl[s - 2])], inc="sVE")
                stl[s] = self.dma("gpsimd", dr["out"][s * 128:(s + 1) * 128, :], x4t[p][:, :],
                                  waits=[("sVE", x4v[s])])

            for i in range(NT_OWN + 2):
                if i < NT_OWN:
                    stA(i)
                stB(i - 1)
                stC(i - 2)
            self.end()

    def build(self):
        ph = self.phases
        if "prep" in ph:
            self.prep()
        if "p0a" in ph:
            self.p0a()
            self.free("wkT", "wvT")
        if "p0b" in ph:
            self.alloc("wqT", [128, 8 * D], BF16, side="right")
            self.begin()
            for jc in range(8):
                self.wq_loaded = self.dma("gpsimd", self.sb["wqT"][:, jc * D:(jc + 1) * D],
                                          self.dram["wqT"][jc * 128:(jc + 1) * 128, :])
            self.end()
            self.p0b()
            self.free("wqT")
        if "p1" in ph:
            self.p1()
            self.free("V_tm", "K_fm")
        if "p2w" in ph:
            self.p2w()
        if "p2" in ph:
            self.p2()
        if "p2w" in ph:
            self.begin()
            self.dma("gpsimd", self.dram["wdq_s"][:, :], self.sb["wdqT"][:, :])
            self.end()
            self.free("wdqT")
        if "p3w" in ph:
            self.p3w()
            self.free("b1row")
        if "p3a" in ph:
            self.p3a()
            self.free("Wv_b", "Wk_b", "Wq_b", "negmk_row")
        if "p3x" in ph:
            self.p3x()
        if "p3e1" in ph:
            self.p3e1()
            self.free("wo_bf", "C_fm", "bv_r", "bk_r", "bq_r")
        if "p3e2" in ph:
            self.p3e2()
            self.free("ff2_bf", "Wff1")
        if "p3e3" in ph:
            self.p3e3()
        if self._cms:
            self.free(*list(self._cms.keys()))
        self.stack.close()
        return self.nc


def _host_tables():
    i = np.arange(512, dtype=np.float64)
    phi = 10000.0 ** (-2.0 * (i - 1.0) / D)
    pos = np.arange(NSEQ, dtype=np.float64)
    ang = phi[:, None] * pos[None, :]
    cos = np.cos(ang).astype(np.float32)
    sin = np.sin(ang).astype(np.float32)
    cos_fm = np.ascontiguousarray(cos.reshape(4, 128, NSEQ).transpose(1, 0, 2).reshape(128, 4 * NSEQ))
    sin_fm = np.ascontiguousarray(sin.reshape(4, 128, NSEQ).transpose(1, 0, 2).reshape(128, 4 * NSEQ))
    return cos_fm, sin_fm


def _own_slices(tab_fm, h):
    t = tab_fm.reshape(128, 4, NT_ALL, 128)
    own = t[:, :, h::2, :]
    return np.ascontiguousarray(own.reshape(128, 4 * 2048))


def _p1_masks(h):
    M = np.zeros((8, 128, 4, 256), np.float32)
    for g in range(8):
        for m in range(4):
            kt = 4 * g + m
            kg = kt * 128 + np.arange(128)
            tA = (4 * g + h) * 128
            tB = (4 * g + 2 + h) * 128
            qg = np.concatenate([tA + np.arange(128), tB + np.arange(128)])
            M[g, :, m, :] = np.where(qg[None, :] >= kg[:, None], 0.0, NEG)
    return np.ascontiguousarray(M.reshape(8, 128, 1024))


def prepare_inputs(inputs):
    x = np.ascontiguousarray(inputs["L0_attn_a_x"], dtype=np.float32)
    perm = np.concatenate([np.arange(0, D, 2), np.arange(1, D, 2)])
    wqT = np.ascontiguousarray(np.asarray(inputs["L0_attn_a_wq"], np.float32).T[:, perm])
    wkT = np.ascontiguousarray(np.asarray(inputs["L0_attn_a_wk"], np.float32).T[:, perm])
    wvT = np.ascontiguousarray(np.asarray(inputs["L0_attn_a_wv"], np.float32).T)
    cos_fm, sin_fm = _host_tables()
    colsify = lambda v: np.ascontiguousarray(np.asarray(v, np.float32).reshape(8, 128).T)
    common = dict(
        wqT=wqT, wkT=wkT, wvT=wvT, costab=cos_fm, sintab=sin_fm,
        mxw=np.ascontiguousarray(np.asarray(inputs["L0_ffn_weight"], np.float32)),
        mxb=np.asarray(inputs["L0_ffn_bias"], np.float32).reshape(1, D),
        bwq=np.ascontiguousarray(np.asarray(inputs["L0_attn_b_wq"], np.float32)),
        bwk=np.ascontiguousarray(np.asarray(inputs["L0_attn_b_wk"], np.float32)),
        bwv=np.ascontiguousarray(np.asarray(inputs["L0_attn_b_wv"], np.float32)),
        bwo=np.ascontiguousarray(np.asarray(inputs["L0_attn_b_wo"], np.float32)),
        bwd=np.ascontiguousarray(np.asarray(inputs["L0_attn_b_w_d"], np.float32)),
        bff1=np.ascontiguousarray(np.asarray(inputs["L0_attn_b_w_ff1"], np.float32)),
        bff2=np.ascontiguousarray(np.asarray(inputs["L0_attn_b_w_ff2"], np.float32)),
        g1c=colsify(inputs["L0_attn_b_ln1_g"]), b1c=colsify(inputs["L0_attn_b_ln1_b"]),
        g2c=colsify(inputs["L0_attn_b_ln2_g"]), b2c=colsify(inputs["L0_attn_b_ln2_b"]),
        bo_r=np.asarray(inputs["L0_attn_b_bo"], np.float32).reshape(1, D),
        bf1_r=np.asarray(inputs["L0_attn_b_b_ff1"], np.float32).reshape(1, 4 * D),
        bf2_r=np.asarray(inputs["L0_attn_b_b_ff2"], np.float32).reshape(1, D),
        bd_r=np.asarray(inputs["L0_attn_b_b_d"], np.float32).reshape(1, D),
        ident=np.eye(128, dtype=np.float32),
        ones_r=np.ones((1, 1024), np.float32),
        ones_c=np.ones((128, 1), np.float32),
    )
    masks = [_p1_masks(0), _p1_masks(1)]
    cos_own = [_own_slices(cos_fm, 0), _own_slices(cos_fm, 1)]
    sin_own = [_own_slices(sin_fm, 0), _own_slices(sin_fm, 1)]
    in_maps = []
    for c in range(8):
        b, h = c >> 1, c & 1
        m = dict(common)
        m["x0"] = np.ascontiguousarray(x[b])
        m["x0own"] = np.ascontiguousarray(
            x[b].reshape(NT_ALL, 128, D)[h::2].reshape(NT_OWN * 128, D))
        m["p1mask"] = masks[h]
        m["costab_own"] = cos_own[h]
        m["sintab_own"] = sin_own[h]
        in_maps.append(m)

    def assemble(results):
        out = np.empty((4, NSEQ, D), np.float32)
        for c in range(8):
            b, h = c >> 1, c & 1
            o = np.asarray(results[c]["out"], np.float32)
            for s in range(NT_OWN):
                t = 2 * s + h
                out[b, t * 128:(t + 1) * 128, :] = o[s * 128:(s + 1) * 128, :]
        return out
    return in_maps, assemble


def build_program():
    return Builder(PHASES, DBG).build()


def kernel(**inputs):
    in_maps, assemble = prepare_inputs(inputs)
    nc = build_program()
    res = run_bass_kernel_spmd(nc, in_maps, core_ids=list(range(8)))
    return assemble(res.results)


# version 12 (final)
# speedup vs baseline: 1.1154x; 1.0058x over previous
import numpy as np
import ml_dtypes
from contextlib import ExitStack
import concourse.bass as bass
import concourse.mybir as mybir
from concourse.bass_utils import run_bass_kernel_spmd

F32 = mybir.dt.float32
BF16 = mybir.dt.bfloat16
AF = mybir.ActivationFunctionType
ALU = mybir.AluOpType
AX = mybir.AxisListType

D = 1024
NSEQ = 4096
NT_ALL = 32
NT_OWN = 16
EPS = 1e-5
NEG = -3.0e38
MAGIC = 12582912.0
ENGS = ("sync", "gpsimd", "tensor", "scalar", "vector")

PHASES = ["prep", "p0a", "p0b", "p1", "p2w", "p2", "p3w", "p3a", "p3x", "p3e1", "p3e2", "p3e3"]
DBG = None
STRIP = 0
SERIAL = False


class Sched:
    def __init__(self, counts, last_wait):
        self.ev = {e: [] for e in ENGS}
        self.C = counts
        self.lastw = last_wait

    def op(self, eng, fn, waits=(), inc=None, n_dma=1):
        if SERIAL:
            waits = list(waits) + [(s, v) for s, v in self.C.items()
                                   if s not in ("sPH",) and v > 0]
        w = []
        for s, v in waits:
            if v > 0 and self.lastw.get((eng, s), -1) < v:
                w.append((s, v))
                self.lastw[(eng, s)] = v
        dma = inc in ("sIN", "sGP")
        self.ev[eng].append((fn, w, inc, dma))
        if inc is not None:
            self.C[inc] += 16 * n_dma if dma else 1
        return self.C[inc] if inc is not None else None


def emit_block(nc, sched, sems):
    with nc.Block() as blk:
        def mk(eng_name):
            def closure(eng):
                for fn, waits, inc, dma in sched.ev[eng_name]:
                    for s, v in waits:
                        eng.wait_ge(sems[s], v)
                    r = fn(eng)
                    if inc is None:
                        continue
                    instrs = r if isinstance(r, (list, tuple)) else [r]
                    if dma:
                        for i in instrs:
                            i.then_inc(sems[inc], 16)
                    else:
                        instrs[-1].then_inc(sems[inc], 1)
            return closure
        blk.sync(mk("sync"))
        blk.gpsimd(mk("gpsimd"))
        blk.tensor(mk("tensor"))
        blk.scalar(mk("scalar"))
        blk.vector(mk("vector"))


class Builder:
    def __init__(self, phases, dbg=None):
        self.phases = phases
        self.dbg = dbg
        nc = bass.Bass("TRN2", target_bir_lowering=False)
        self.nc = nc
        self.C = {k: 0 for k in ("sIN", "sGP", "sMM", "sSC", "sVE", "sPH", "sCC", "sRD", "sGD")}
        self.lastw = {}
        self.stack = ExitStack()
        self.sems = {k: self.stack.enter_context(nc.semaphore(k)) for k in self.C}
        self.phase_idx = 0
        self.dram = {}
        self.sb = {}
        self._cms = {}
        self._seq = {}
        self._declare_dram()
        self._alloc_persistent()

    def _declare_dram(self):
        nc = self.nc
        inp = lambda name, shape, dt=F32: self.dram.__setitem__(
            name, nc.dram_tensor(name, shape, dt, kind="ExternalInput"))
        inp("x0", [NSEQ, D])
        inp("x0own", [NT_OWN * 128, D])
        inp("p1mask", [8, 128, 1024])
        inp("wqT", [D, D]); inp("wkT", [D, D]); inp("wvT", [D, D])
        inp("costab", [128, 4 * NSEQ]); inp("sintab", [128, 4 * NSEQ])
        inp("costab_own", [128, 4 * 2048]); inp("sintab_own", [128, 4 * 2048])
        inp("mxw", [D, D]); inp("mxb", [1, D])
        inp("bwq", [D, D]); inp("bwk", [D, D]); inp("bwv", [D, D])
        inp("bwo", [D, D]); inp("bwd", [D, D])
        inp("bff1", [D, 4 * D]); inp("bff2", [4 * D, D])
        inp("g1c", [128, 8]); inp("b1c", [128, 8]); inp("g2c", [128, 8]); inp("b2c", [128, 8])
        inp("bo_r", [1, D]); inp("bf1_r", [1, 4 * D]); inp("bf2_r", [1, D]); inp("bd_r", [1, D])
        inp("ident", [128, 128]); inp("ones_r", [1, 1024]); inp("ones_c", [128, 1])
        self.dram["out"] = nc.dram_tensor("out", [NT_OWN * 128, D], F32, kind="ExternalOutput")
        d = self.dbg
        scr = lambda name, shape, dt=F32, dbg_out=False: self.dram.__setitem__(
            name, nc.dram_tensor(name, shape, dt, kind="ExternalOutput" if dbg_out else "Internal"))
        scr("x1s", [NT_OWN * 128, D], F32, d in ("p1", "all"))
        scr("x2s", [NT_OWN * 128, D], F32, d in ("p2", "all"))
        scr("xxs", [NT_OWN * 128, D], F32, d in ("p3e1", "all"))
        scr("xx2s", [NT_OWN * 128, D], F32, d in ("p3e2", "all"))
        scr("qfm_s", [128, 8 * 2048], BF16, d == "p0")
        scr("qefm_s", [128, 8 * 2048], BF16, d == "p3a")
        scr("cc_in", [128, 2056], F32)
        scr("cc_out", [128, 2056], F32)
        scr("wdq_s", [128, 8 * D], BF16, d == "all")
        scr("b1row_s", [1, 4 * D], BF16, d == "all")
        if d == "all":
            scr("dbg_wff1", [128, 4 * D], BF16, True)
            scr("dbg_u", [128, 8 * D], BF16, True)
        if d == "p0":
            scr("dbg_kfm", [128, 8 * NSEQ], BF16, True)
            scr("dbg_vtm", [128, NT_ALL * D], BF16, True)
            scr("dbg_negm", [1, 256], F32, True)
            scr("dbg_tiny", [128, 16], F32, True)

            scr("dbg_kn", [128, NT_ALL], F32, True)
            scr("dbg_qn", [128, NT_OWN], F32, True)

    def _alloc_persistent(self):
        nc, st = self.nc, self.stack
        T = lambda n, s, dt: st.enter_context(nc.sbuf_tensor(n, s, dt))
        self.sb["ident"] = T("c_ident", [128, 128], BF16)
        self.sb["ones_rb"] = T("c_ones_rb", [1, 1024], BF16)
        self.sb["ones_rf"] = T("c_ones_rf", [1, 1024], F32)
        self.sb["ones_cb"] = T("c_ones_cb", [128, 1], BF16)
        self.sb["negm_row"] = T("negm_row", [1, 256], BF16)
        self.sb["sq"] = T("sq_scratch", [128, 1024], F32)
        self.sb["rsc"] = T("rope_scratch", [128, 1024], BF16)
        self.sb["tiny"] = T("tiny", [128, 80], F32)
        self.sb["epsc"] = T("epsc", [128, 1], F32)
        self.sb["nepsc"] = T("nepsc", [128, 1], F32)
        self.sb["mgc"] = T("mgc", [128, 1], F32)
        self.sb["knorm"] = T("knorm", [128, NT_ALL], F32)
        self.sb["qnorm"] = T("qnorm", [128, NT_OWN], F32)
        self.sb["ln1_r"] = T("ln1_r", [128, NT_OWN], F32)
        self.sb["ln1_b"] = T("ln1_b", [128, NT_OWN], F32)
        self.sb["ln2_r"] = T("ln2_r", [128, NT_OWN], F32)
        self.sb["ln2_b"] = T("ln2_b", [128, NT_OWN], F32)
        self.sb["rq"] = T("rq", [128, NT_OWN * 4], F32)
        self.sb["qt"] = T("qt", [128, 384], F32)


    def barrier(self):
        S = self.S
        for e in ENGS:
            w = [("sIN", self.C["sIN"])] if e == "sync" else (
                [("sGP", self.C["sGP"])] if e == "gpsimd" else [])
            S.op(e, lambda eng: eng.nop(), waits=w, inc="sPH", n_dma=0)
        self.phase_idx += 1

    def begin(self):
        self.S = Sched(self.C, self.lastw)
        tgt = 5 * self.phase_idx
        if tgt:
            for e in ENGS:
                self.S.lastw.pop((e, "sPH"), None)
                self.S.ev[e].append(((lambda eng: None), [("sPH", tgt)], None, False))

    def end(self):
        self.barrier()
        emit_block(self.nc, self.S, self.sems)
        self.S = None

    def alloc(self, name, shape, dt, side=None):
        self._aseq = getattr(self, "_aseq", 0) + 1
        cm = self.nc.sbuf_tensor(f"s{self._aseq}_{name}", shape, dt, side=side)
        h = cm.__enter__()
        self._cms[name] = cm
        self._seq[name] = self._seqno = getattr(self, "_seqno", 0) + 1
        self.sb[name] = h
        return h

    def free(self, *names):
        for n in sorted(names, key=lambda n: -self._seq[n]):
            self._cms.pop(n).__exit__(None, None, None)
            self.sb.pop(n)
            self._seq.pop(n)

    def ckpt(self, eng):
        if eng == "sync":
            return self.S.op("sync", lambda e: e.nop(), waits=[("sIN", self.C["sIN"])],
                             inc="sRD", n_dma=0)
        return self.S.op("gpsimd", lambda e: e.nop(), waits=[("sGP", self.C["sGP"])],
                         inc="sGD", n_dma=0)

    def dma(self, eng, out, in_, waits=()):
        return self.S.op(eng, lambda e, o=out, i=in_: e.dma_start(out=o, in_=i),
                         waits=waits, inc="sIN" if eng == "sync" else "sGP")

    def ln_stats_apply(self, xin, n_out, x_ready, par, extra_waits=(), fused2=None, save=None,
                       sum_on_scalar=False):
        S, sb = self.S, self.sb
        b = 20 * par
        t = lambda i: sb["tiny"][:, b + i:b + i + 1]
        sq = S.op("scalar", lambda e, a=xin: e.activation(sb["sq"][:, :], a[:, :], AF.Square,
                                                          accum_out=t(1)),
                  waits=[x_ready], inc="sSC")
        if sum_on_scalar:
            v1 = 0
            sq = S.op("scalar", lambda e, a=xin: e.activation(sb["sq"][:, :], a[:, :], AF.Copy,
                                                              accum_out=t(0)),
                      waits=[x_ready], inc="sSC")
        else:
            v1 = S.op("vector", lambda e, a=xin: e.reduce_sum(t(0), a[:, :], axis=AX.X),
                      waits=[x_ready], inc="sVE")
        v2 = S.op("vector", lambda e: [
            e.tensor_scalar_mul(t(2), t(0), 1.0 / D),
            e.tensor_scalar(t(3), t(1), 1.0 / D, EPS, ALU.mult, ALU.add),
        ], waits=[("sVE", v1), ("sSC", sq)], inc="sVE")
        v3 = S.op("vector", lambda e: e.tensor_mul(t(4), t(2), t(2)), waits=[("sVE", v2)], inc="sVE")
        v4 = S.op("vector", lambda e: e.scalar_tensor_tensor(t(5), t(4), -1.0, t(3),
                                                             ALU.mult, ALU.add),
                  waits=[("sVE", v3)], inc="sVE")
        s2 = S.op("scalar", lambda e: [
            e.activation(t(6), t(5), AF.Ln),
            e.activation(t(7), t(6), AF.Exp, scale=-0.5),
        ], waits=[("sVE", v4)], inc="sSC")
        if not fused2:
            v5 = S.op("vector", lambda e: e.scalar_tensor_tensor(t(8), t(2), -1.0, t(7),
                                                                 ALU.mult, ALU.mult),
                      waits=[("sSC", s2)], inc="sVE")
            def fin(e):
                r = []
                if save is not None:
                    rbuf, bbuf, scol = save
                    r += [e.activation(rbuf[:, scol:scol + 1], t(7), AF.Identity),
                          e.activation(bbuf[:, scol:scol + 1], t(8), AF.Identity)]
                if n_out is not None:
                    r.append(e.activation(n_out[:, :], xin[:, :], AF.Identity,
                                          scale=t(7), bias=t(8)))
                return r
            return S.op("scalar", fin, waits=[("sVE", v5)] + list(extra_waits), inc="sSC")
        v5 = S.op("vector", lambda e: [
            e.scalar_tensor_tensor(t(8), t(2), -1.0, t(7), ALU.mult, ALU.mult),
            e.tensor_mul(t(10), t(7), t(7)),
            e.tensor_scalar_add(t(11), t(5), -EPS),
        ], waits=[("sSC", s2)], inc="sVE")
        v8 = S.op("vector", lambda e: e.scalar_tensor_tensor(
            t(13), t(10), t(11), sb["epsc"][:, :], ALU.mult, ALU.add),
            waits=[("sVE", v5)], inc="sVE")
        s3 = S.op("scalar", lambda e: [
            e.activation(t(14), t(13), AF.Ln),
            e.activation(t(15), t(14), AF.Exp, scale=-0.5),
        ], waits=[("sVE", v8)], inc="sSC")
        v10 = S.op("vector", lambda e: e.tensor_mul(t(16), t(7), t(15)),
                   waits=[("sSC", s3)], inc="sVE")
        v11 = S.op("vector", lambda e: e.scalar_tensor_tensor(t(18), t(2), -1.0, t(16),
                                                              ALU.mult, ALU.mult),
                   waits=[("sVE", v10)], inc="sVE")
        def fin2(e):
            r = []
            if save is not None:
                rbuf, bbuf, scol = save
                r += [e.activation(rbuf[:, scol:scol + 1], t(7), AF.Identity),
                      e.activation(bbuf[:, scol:scol + 1], t(8), AF.Identity)]
            if n_out is not None:
                r.append(e.activation(n_out[:, :], xin[:, :], AF.Identity,
                                      scale=t(16), bias=t(18)))
            return r
        return S.op("scalar", fin2, waits=[("sVE", v11)] + list(extra_waits), inc="sSC")

    def rope(self, kraw, cosb, sinb, out_view_e, out_view_o, waits):
        S, sb = self.S, self.sb
        r3 = lambda ap: ap.rearrange("p (c t) -> p c t", t=128)
        return S.op("vector", lambda e, kr=kraw, cb=cosb, sn=sinb, oe=out_view_e, oo=out_view_o: [
            e.tensor_mul(r3(sb["rsc"][:, 0:512]), r3(kr[:, 0:512]), r3(cb[:, :])),
            e.tensor_mul(r3(sb["rsc"][:, 512:1024]), r3(kr[:, 512:1024]), r3(sn[:, :])),
            e.tensor_add(oe, r3(sb["rsc"][:, 0:512]), r3(sb["rsc"][:, 512:1024])),
            e.tensor_mul(r3(sb["rsc"][:, 0:512]), r3(kr[:, 512:1024]), r3(cb[:, :])),
            e.tensor_mul(r3(sb["rsc"][:, 512:1024]), r3(kr[:, 0:512]), r3(sn[:, :])),
            e.tensor_sub(oo, r3(sb["rsc"][:, 0:512]), r3(sb["rsc"][:, 512:1024])),
        ], waits=waits, inc="sVE")

    def quant(self, h, xd_bf, z, zsq, mg, h_ready, par, vq_waits=()):
        S, sb = self.S, self.sb
        b = 96 * par
        qt = sb["qt"]
        r3 = lambda ap: ap.rearrange("p (g k) -> p g k", k=32)
        v1 = S.op("vector", lambda e: e.reduce_max(qt[:, b:b + 32], r3(h[:, :]), axis=AX.X,
                                                   apply_absolute_value=True),
                  waits=[h_ready], inc="sVE")
        v2 = S.op("vector", lambda e: e.tensor_scalar(qt[:, b + 32:b + 64], qt[:, b:b + 32],
                                                      1.0 / 7.5, 1e-30, ALU.mult, ALU.max),
                  waits=[("sVE", v1)], inc="sVE")
        v3 = S.op("vector", lambda e: e.reciprocal(qt[:, b + 64:b + 96], qt[:, b + 32:b + 64]),
                  waits=[("sVE", v2)], inc="sVE")
        vz = S.op("vector", lambda e: e.tensor_mul(
            r3(z[:, :]), r3(h[:, :]), qt[:, b + 64:b + 96, None].broadcast_to([128, 32, 32])),
            waits=[("sVE", v3)], inc="sVE")
        s1 = S.op("scalar", lambda e: e.activation(zsq[:, :], z[:, :], AF.Square),
                  waits=[("sVE", vz)], inc="sSC")
        vab = S.op("vector", lambda e: [
            e.tensor_scalar(mg[:, :], zsq[:, :], 16.0, 2.0, ALU.is_ge, ALU.mult),
            e.scalar_tensor_tensor(mg[:, :], zsq[:, :], 4.0, mg[:, :],
                                   ALU.is_ge, ALU.add),
        ], waits=[("sSC", s1)], inc="sVE")
        s2 = S.op("scalar", lambda e: e.activation(mg[:, :], mg[:, :], AF.Identity,
                                                   scale=1572864.0, bias=sb["mgc"][:, :]),
                  waits=[("sVE", vab)], inc="sSC")
        vq = S.op("vector", lambda e: [
            e.tensor_add(z[:, :], z[:, :], mg[:, :]),
            e.tensor_sub(z[:, :], z[:, :], mg[:, :]),
            e.tensor_mul(r3(xd_bf[:, :]), r3(z[:, :]),
                         qt[:, b + 32:b + 64, None].broadcast_to([128, 32, 32])),
        ], waits=[("sSC", s2)] + list(vq_waits), inc="sVE")
        return vq

    def proj_mm(self, e, psum, x_fm, w_sb, bias_row, extra_rows=()):
        r = []
        for half in range(2):
            tgt = psum[:, half * 512:(half + 1) * 512]
            if bias_row is not None:
                r.append(e.matmul(tgt, self.sb["ones_rb"][0:1, 0:128],
                                  bias_row[0:1, half * 512:(half + 1) * 512],
                                  start=True, stop=False))
            for row in extra_rows:
                r.append(e.matmul(tgt, self.sb["ones_rb"][0:1, 0:128],
                                  row[0:1, half * 512:(half + 1) * 512],
                                  start=False, stop=False))
            for jc in range(8):
                first = jc == 0 and bias_row is None and not extra_rows
                r.append(e.matmul(tgt, x_fm[:, jc * 128:(jc + 1) * 128],
                                  w_sb[:, jc * D + half * 512: jc * D + (half + 1) * 512],
                                  start=first, stop=(jc == 7)))
        return r

    def tr8(self, src_bf, pT, dst_view_or_ap, waits, evac_waits=()):
        S, sb = self.S, self.sb
        t = S.op("tensor", lambda e: [
            e.transpose(pT[:, c * 128:(c + 1) * 128], src_bf[:, c * 128:(c + 1) * 128],
                        sb["ident"][:, :]) for c in range(8)],
            waits=waits, inc="sMM")
        ev = S.op("vector", lambda e: e.tensor_copy(dst_view_or_ap, pT[:, :]),
                  waits=[("sMM", t)] + list(evac_waits), inc="sVE")
        return t, ev

    def prep(self):
        nc, sb, dr, st = self.nc, self.sb, self.dram, self.stack
        self.alloc("wkT", [128, 8 * D], BF16, side="right")
        self.alloc("wvT", [128, 8 * D], BF16, side="right")
        self.begin()
        self.S.op("vector", lambda e: [
            e.memset(sb["epsc"][:, :], EPS),
            e.memset(sb["nepsc"][:, :], -EPS),
            e.memset(sb["mgc"][:, :], 1572864.0),
        ], inc="sVE")
        self.dma("gpsimd", sb["ident"][:, :], dr["ident"][:, :])
        self.dma("gpsimd", sb["ones_rb"][:, :], dr["ones_r"][:, :])
        self.dma("sync", sb["ones_rf"][:, :], dr["ones_r"][:, :])
        self.dma("gpsimd", sb["ones_cb"][:, :], dr["ones_c"][:, :])
        for jc in range(8):
            self.dma("gpsimd", sb["wkT"][:, jc * D:(jc + 1) * D], dr["wkT"][jc * 128:(jc + 1) * 128, :])
            self.dma("gpsimd", sb["wvT"][:, jc * D:(jc + 1) * D], dr["wvT"][jc * 128:(jc + 1) * 128, :])
        self.end()

    def p0a(self):
        nc, sb, dr, st = self.nc, self.sb, self.dram, self.stack
        self.alloc("K_fm", [128, 8 * NSEQ], BF16)
        self.alloc("V_tm", [128, NT_ALL * D], BF16)
        with ExitStack() as ph:
            T = lambda n, s, dt: ph.enter_context(nc.sbuf_tensor(n, s, dt))
            xin = [T(f"a_xin{p}", [128, D], F32) for p in range(2)]
            cosb = [T(f"a_cos{p}", [128, 512], BF16) for p in range(2)]
            sinb = [T(f"a_sin{p}", [128, 512], BF16) for p in range(2)]
            nbf = [T(f"a_nbf{p}", [128, D], BF16) for p in range(3)]
            nfm = [T(f"a_nfm{p}", [128, D], BF16) for p in range(3)]
            ktmb = [T(f"a_ktm{p}", [128, D], BF16) for p in range(2)]
            kraw = [T(f"a_kraw{p}", [128, D], BF16) for p in range(2)]
            pT = ph.enter_context(nc.psum_tensor("a_pT", [128, 1024], BF16))
            pT2 = ph.enter_context(nc.psum_tensor("a_pT2", [128, 1024], BF16))
            pp = [ph.enter_context(nc.psum_tensor(f"a_pp{i}", [128, 1024], F32)) for i in range(3)]
            self.begin()
            S = self.S
            lnapply = [0] * (NT_ALL + 2)
            ropeK = [0] * (NT_ALL + 2)
            evK = [0] * (NT_ALL + 2)
            evV = [0] * (NT_ALL + 2)
            nfm_ev = [0] * (NT_ALL + 2)
            kraw_ev = [0] * (NT_ALL + 2)
            ntr_l = [0] * (NT_ALL + 2)

            Kv = sb["K_fm"][:, :].rearrange("p (c t) -> p c t", t=NSEQ)
            def front(tt):
                p = tt % 2
                pn = tt % 3
                t0 = tt * 128
                x_in = self.dma("sync", xin[p][:, :], dr["x0"][t0:t0 + 128, :],
                                waits=[("sSC", lnapply[tt - 2])])
                cview = dr["costab"][:, :].rearrange("p (c t) -> p c t", t=NSEQ)[:, :, t0:t0 + 128]
                sview = dr["sintab"][:, :].rearrange("p (c t) -> p c t", t=NSEQ)[:, :, t0:t0 + 128]
                self.dma("gpsimd", cosb[p][:, :].rearrange("p (c t) -> p c t", t=128), cview,
                         waits=[("sVE", ropeK[tt - 2])])
                self.dma("gpsimd", sinb[p][:, :].rearrange("p (c t) -> p c t", t=128), sview)
                cos_in[tt] = self.ckpt("gpsimd")
                lnapply[tt] = self.ln_stats_apply(xin[p], nbf[pn], ("sIN", x_in), pn,
                                                  extra_waits=[("sMM", ntr_l[tt - 3])])
                ntr_l[tt] = S.op("tensor", lambda e, i=nbf[pn]: [
                    e.transpose(pT[:, c * 128:(c + 1) * 128], i[:, c * 128:(c + 1) * 128], sb["ident"][:, :])
                    for c in range(8)],
                    waits=[("sSC", lnapply[tt]), ("sVE", nfm_ev[tt - 1])], inc="sMM")
                nfm_ev[tt] = S.op("vector", lambda e, o=nfm[pn]: e.tensor_copy(o[:, :], pT[:, :]),
                                  waits=[("sMM", ntr_l[tt])], inc="sVE")

            def mmK(tt):
                pn = tt % 3
                rK = pp[(2 * tt) % 3]
                kmmK[tt] = S.op("tensor", lambda e, n=nfm[pn], rk=rK: [
                    mm for jc in range(8) for mm in (
                        e.matmul(rk[:, 0:512], n[:, jc * 128:(jc + 1) * 128],
                                 sb["wkT"][:, jc * D:jc * D + 512], start=(jc == 0), stop=(jc == 7)),
                        e.matmul(rk[:, 512:1024], n[:, jc * 128:(jc + 1) * 128],
                                 sb["wkT"][:, jc * D + 512:(jc + 1) * D], start=(jc == 0), stop=(jc == 7)),
                    )], waits=[("sVE", nfm_ev[tt]), ("sSC", evK[tt - 1])], inc="sMM")

            def mmV(tt):
                pn = tt % 3
                rV = pp[(2 * tt + 1) % 3]
                kmmV[tt] = S.op("tensor", lambda e, n=nfm[pn], rv=rV: [
                    mm for jc in range(8) for mm in (
                        e.matmul(rv[:, 0:512], n[:, jc * 128:(jc + 1) * 128],
                                 sb["wvT"][:, jc * D:jc * D + 512], start=(jc == 0), stop=(jc == 7)),
                        e.matmul(rv[:, 512:1024], n[:, jc * 128:(jc + 1) * 128],
                                 sb["wvT"][:, jc * D + 512:(jc + 1) * D], start=(jc == 0), stop=(jc == 7)),
                    )], waits=[("sVE", nfm_ev[tt]), ("sSC", evV[tt - 1])], inc="sMM")

            def backS(tt):
                if tt < 0:
                    return
                p = tt % 2
                rK = pp[(2 * tt) % 3]
                rV = pp[(2 * tt + 1) % 3]
                evK[tt] = S.op("scalar", lambda e, o=ktmb[p], rk=rK, c_=tt: [
                    e.activation(o[:, 0:512], rk[:, 0:512], AF.Copy),
                    e.activation(o[:, 512:1024], rk[:, 512:1024], AF.Copy),
                    e.activation(sb["sq"][:, :], o[:, :], AF.Square,
                                 accum_out=sb["knorm"][:, c_:c_ + 1]),
                ], waits=[("sMM", kmmK[tt])], inc="sSC")
                evV[tt] = S.op("scalar", lambda e, rv=rV, c_=tt: [
                    e.activation(sb["V_tm"][:, c_ * D:c_ * D + 512], rv[:, 0:512], AF.Copy),
                    e.activation(sb["V_tm"][:, c_ * D + 512:(c_ + 1) * D], rv[:, 512:1024], AF.Copy),
                ], waits=[("sMM", kmmV[tt])], inc="sSC")

            def backT(tt):
                if tt < 0:
                    return
                p = tt % 2
                ktr = S.op("tensor", lambda e, i=ktmb[p]: [
                    e.transpose(pT2[:, c * 128:(c + 1) * 128], i[:, c * 128:(c + 1) * 128], sb["ident"][:, :])
                    for c in range(8)],
                    waits=[("sSC", evK[tt]), ("sVE", kraw_ev[tt - 1])], inc="sMM")
                ktr_l[tt] = ktr

            def backV(tt):
                if tt < 0:
                    return
                p = tt % 2
                t0 = tt * 128
                kraw_ev[tt] = S.op("vector", lambda e, o=kraw[p]: e.tensor_copy(o[:, :], pT2[:, :]),
                                   waits=[("sMM", ktr_l[tt])], inc="sVE")
                ropeK[tt] = self.rope(kraw[p], cosb[p], sinb[p],
                                      Kv[:, 0:4, t0:t0 + 128], Kv[:, 4:8, t0:t0 + 128],
                                      waits=[("sVE", kraw_ev[tt]), ("sGD", cos_in[tt])])

            kmmK = [0] * (NT_ALL + 2)
            kmmV = [0] * (NT_ALL + 2)
            ktr_l = [0] * (NT_ALL + 2)
            cos_in = [0] * (NT_ALL + 2)
            front(0)
            for tt in range(NT_ALL):
                mmK(tt)
                backS(tt - 1)
                backT(tt - 1)
                mmV(tt)
                backV(tt - 1)
                if tt + 1 < NT_ALL:
                    front(tt + 1)
            backS(NT_ALL - 1)
            backT(NT_ALL - 1)
            backV(NT_ALL - 1)
            self.end()

    def p0b(self):
        nc, sb, dr, st = self.nc, self.sb, self.dram, self.stack
        with ExitStack() as ph:
            T = lambda n, s, dt: ph.enter_context(nc.sbuf_tensor(n, s, dt))
            xin = [T(f"b_xin{p}", [128, D], F32) for p in range(2)]
            cosb = [T(f"b_cos{p}", [128, 512], BF16) for p in range(2)]
            sinb = [T(f"b_sin{p}", [128, 512], BF16) for p in range(2)]
            nbf = [T(f"b_nbf{p}", [128, D], BF16) for p in range(2)]
            nfm = [T(f"b_nfm{p}", [128, D], BF16) for p in range(2)]
            qtmb = [T(f"b_qtm{p}", [128, D], BF16) for p in range(2)]
            qraw = [T(f"b_qraw{p}", [128, D], BF16) for p in range(2)]
            qfmt = [T(f"b_qfmt{p}", [128, D], BF16) for p in range(2)]
            pTn = [ph.enter_context(nc.psum_tensor(f"b_pTn{i}", [128, 1024], BF16)) for i in range(2)]
            pT2 = ph.enter_context(nc.psum_tensor("b_pT2", [128, 1024], BF16))
            pT = pTn[0]
            pp = [ph.enter_context(nc.psum_tensor(f"b_pp{i}", [128, 1024], F32)) for i in range(2)]
            self.begin()
            S = self.S
            lnapply = [0] * (NT_OWN + 2)
            ropeQ = [0] * (NT_OWN + 2)
            evQ = [0] * (NT_OWN + 2)
            nfm_ev = [0] * (NT_OWN + 2)
            qraw_ev = [0] * (NT_OWN + 2)
            spill = [0] * (NT_OWN + 2)
            ntr_l = [0] * (NT_OWN + 2)
            qmm_l = [0] * (NT_OWN + 2)
            cos_in = [0] * (NT_OWN + 2)

            for s in range(NT_OWN + 1):
                if s < NT_OWN:
                    p = s % 2
                    t0 = s * 128
                    x_in = self.dma("sync", xin[p][:, :], dr["x0own"][t0:t0 + 128, :],
                                    waits=[("sSC", lnapply[s - 2])])
                    cview = dr["costab_own"][:, :].rearrange("p (c t) -> p c t", t=2048)[:, :, t0:t0 + 128]
                    sview = dr["sintab_own"][:, :].rearrange("p (c t) -> p c t", t=2048)[:, :, t0:t0 + 128]
                    self.dma("gpsimd", cosb[p][:, :].rearrange("p (c t) -> p c t", t=128), cview,
                             waits=[("sVE", ropeQ[s - 2])])
                    self.dma("gpsimd", sinb[p][:, :].rearrange("p (c t) -> p c t", t=128), sview)
                    cos_in[s] = self.ckpt("gpsimd")
                    lnapply[s] = self.ln_stats_apply(xin[p], nbf[p], ("sIN", x_in), p,
                                                     extra_waits=[("sMM", ntr_l[s - 2])])
                    ntr_l[s] = ntr = S.op("tensor", lambda e, i=nbf[p], b=pTn[p]: [
                        e.transpose(b[:, c * 128:(c + 1) * 128], i[:, c * 128:(c + 1) * 128], sb["ident"][:, :])
                        for c in range(8)],
                        waits=[("sSC", lnapply[s]), ("sVE", nfm_ev[s - 2])], inc="sMM")
                    nfm_ev[s] = S.op("vector", lambda e, o=nfm[p], b=pTn[p]: e.tensor_copy(o[:, :], b[:, :]),
                                     waits=[("sMM", ntr)], inc="sVE")
                    rQ = pp[s % 2]

                    def mmQ(e, n=nfm[p], rq=rQ):
                        r = []
                        for jc in range(8):
                            st_, sp = jc == 0, jc == 7
                            lhs = n[:, jc * 128:(jc + 1) * 128]
                            r.append(e.matmul(rq[:, 0:512], lhs, sb["wqT"][:, jc * D:jc * D + 512], start=st_, stop=sp))
                            r.append(e.matmul(rq[:, 512:1024], lhs, sb["wqT"][:, jc * D + 512:(jc + 1) * D], start=st_, stop=sp))
                        return r
                    qmm_l[s] = qmm = S.op("tensor", mmQ, waits=[("sVE", nfm_ev[s]), ("sSC", evQ[s - 2]),
                                                                ("sGP", self.wq_loaded)], inc="sMM")
                    evQ[s] = S.op("scalar", lambda e, o=qtmb[p], rq=rQ, c_=s: [
                        e.activation(o[:, 0:512], rq[:, 0:512], AF.Copy),
                        e.activation(o[:, 512:1024], rq[:, 512:1024], AF.Copy),
                        e.activation(sb["sq"][:, :], o[:, :], AF.Square, accum_out=sb["qnorm"][:, c_:c_ + 1]),
                    ], waits=[("sMM", qmm)], inc="sSC")
                if s - 1 >= 0:
                    bs = s - 1
                    pb = bs % 2
                    tb0 = bs * 128
                    qtr = S.op("tensor", lambda e, i=qtmb[pb]: [
                        e.transpose(pT2[:, c * 128:(c + 1) * 128], i[:, c * 128:(c + 1) * 128], sb["ident"][:, :])
                        for c in range(8)],
                        waits=[("sSC", evQ[bs]), ("sVE", qraw_ev[bs - 1])], inc="sMM")
                    qraw_ev[bs] = S.op("vector", lambda e, o=qraw[pb]: e.tensor_copy(o[:, :], pT2[:, :]),
                                       waits=[("sMM", qtr)], inc="sVE")
                    qv = qfmt[pb][:, :].rearrange("p (c t) -> p c t", t=128)
                    ropeQ[bs] = self.rope(qraw[pb], cosb[pb], sinb[pb], qv[:, 0:4, :], qv[:, 4:8, :],
                                          waits=[("sVE", qraw_ev[bs]), ("sGD", cos_in[bs]),
                                                 ("sGP", spill[bs - 2])])
                    qview = dr["qfm_s"][:, :].rearrange("p (c t) -> p c t", t=2048)[:, :, tb0:tb0 + 128]
                    spill[bs] = self.dma("gpsimd", qview,
                                         qfmt[pb][:, :].rearrange("p (c t) -> p c t", t=128),
                                         waits=[("sVE", ropeQ[bs])])
            vmax = S.op("vector", lambda e: [
                e.reduce_max(sb["tiny"][:, 0:1], sb["knorm"][:, :], axis=AX.X),
                e.reduce_max(sb["tiny"][:, 1:2], sb["qnorm"][:, :], axis=AX.X),
            ], waits=[("sSC", evQ[NT_OWN - 1])], inc="sVE")
            scast = S.op("scalar", lambda e: e.activation(sb["rsc"][:, 0:2], sb["tiny"][:, 0:2], AF.Copy),
                         waits=[("sVE", vmax)], inc="sSC")
            ttr = S.op("tensor", lambda e: e.transpose(pT[0:2, 0:128], sb["rsc"][:, 0:2], sb["ident"][:, :]),
                       waits=[("sSC", scast), ("sVE", qraw_ev[NT_OWN - 1])], inc="sMM")
            vred = S.op("vector", lambda e: e.reduce_max(sb["tiny"][0:2, 3:4], pT[0:2, 0:128], axis=AX.X),
                        waits=[("sMM", ttr)], inc="sVE")
            scast2 = S.op("scalar", lambda e: e.activation(sb["rsc"][0:2, 2:3], sb["tiny"][0:2, 3:4], AF.Copy),
                          waits=[("sVE", vred)], inc="sSC")
            ttr2 = S.op("tensor", lambda e: e.transpose(pT[0:1, 128:130], sb["rsc"][0:2, 2:3], sb["ident"][0:2, 0:2]),
                        waits=[("sSC", scast2)], inc="sMM")
            vcp = S.op("vector", lambda e: e.tensor_copy(sb["tiny"][0:1, 9:11], pT[0:1, 128:130]),
                       waits=[("sMM", ttr2)], inc="sVE")
            vprod = S.op("vector", lambda e: e.tensor_mul(
                sb["tiny"][0:1, 3:4], sb["tiny"][0:1, 9:10], sb["tiny"][0:1, 10:11]),
                waits=[("sVE", vcp)], inc="sVE")
            sneg = S.op("scalar", lambda e: [
                e.activation(sb["tiny"][0:1, 4:5], sb["tiny"][0:1, 3:4], AF.Ln),
                e.activation(sb["tiny"][0:1, 5:6], sb["tiny"][0:1, 4:5], AF.Exp, scale=0.5),
            ], waits=[("sVE", vprod)], inc="sSC")
            vneg = S.op("vector", lambda e: e.tensor_scalar_mul(
                sb["tiny"][0:1, 6:7], sb["tiny"][0:1, 5:6], -1.05), waits=[("sSC", sneg)], inc="sVE")
            S.op("scalar", lambda e: e.activation(sb["negm_row"][:, :], sb["ones_rf"][0:1, 0:256],
                                                  AF.Identity, scale=sb["tiny"][0:1, 6:7]),
                 waits=[("sVE", vneg)], inc="sSC")
            if self.dbg == "p0":
                for c in range(8):
                    self.dma("gpsimd", dr["dbg_kfm"][:, c * NSEQ:(c + 1) * NSEQ],
                             sb["K_fm"][:, c * NSEQ:(c + 1) * NSEQ],
                             waits=[("sVE", self.C["sVE"]), ("sSC", self.C["sSC"])])
                    self.dma("gpsimd", dr["dbg_vtm"][:, c * NSEQ:(c + 1) * NSEQ],
                             sb["V_tm"][:, c * NSEQ:(c + 1) * NSEQ])
                self.dma("gpsimd", dr["dbg_negm"][:, :], sb["negm_row"][:, :])
                self.dma("gpsimd", dr["dbg_tiny"][:, :], sb["tiny"][:, :])
                self.dma("gpsimd", dr["dbg_kn"][:, :], sb["knorm"][:, :])
                self.dma("gpsimd", dr["dbg_qn"][:, :], sb["qnorm"][:, :])
            self.end()

    def p1(self):
        nc, sb, dr, st = self.nc, self.sb, self.dram, self.stack
        with ExitStack() as ph:
            T = lambda n, s, dt: ph.enter_context(nc.sbuf_tensor(n, s, dt))
            qg = [T(f"c_qg{p}", [128, 2048], BF16) for p in range(3)]
            maskb = [T(f"c_mask{p}", [128, 1024], F32) for p in range(3)]
            x0t = [T(f"c_x0t{p}", [128, D], F32) for p in range(2)]
            x1t = [T(f"c_x1t{p}", [128, D], F32) for p in range(2)]
            PT = [T(f"c_PT{p}", [128, 256], BF16) for p in range(3)]
            rcp = T("c_rcp", [128, 2], F32)
            pS = [ph.enter_context(nc.psum_tensor(f"c_pS{i}", [128, 512], F32)) for i in range(3)]
            av = [ph.enter_context(nc.psum_tensor(f"c_av{i}", [128, 512], F32)) for i in range(4)]
            pden = ph.enter_context(nc.psum_tensor("c_pden", [128, 512], F32))
            self.begin()
            S = self.S
            Kv = sb["K_fm"]
            n = 0
            expd = {}
            avmm = {}
            qg_lastuse = [0, 0, 0]
            mk_lastuse = [0, 0, 0]
            x0_in = [0, 0]
            epi_v = 0
            epi_prev = 0
            x1_store = [0] * (NT_OWN + 2)
            recip_done = 0
            prefetched = {}
            pend_epi = [None]

            def emit_group_dmas(g_):
                gp_ = g_ % 3
                qview_ = dr["qfm_s"][:, :].rearrange("p (c t) -> p c t", t=2048)[:, :, g_ * 256:(g_ + 1) * 256]
                self.dma("sync", qg[gp_][:, :].rearrange("p (c t) -> p c t", t=256),
                         qview_, waits=[("sMM", qg_lastuse[gp_])])
                self.dma("sync", maskb[gp_][:, :], dr["p1mask"][g_, :, :],
                         waits=[("sVE", mk_lastuse[gp_])])
                prefetched[g_] = self.ckpt("sync")

            for g in range(8):
                gp = g % 3
                L = 4 * g + 4
                if g not in prefetched:
                    emit_group_dmas(g)
                grp_rdy = prefetched[g]
                for kt in range(L):
                    p = n % 3
                    def mmS(e, g_=g, kt_=kt, p_=p, qb=qg[gp]):
                        r = []
                        for dch in range(8):
                            r.append(e.matmul(
                                pS[p_][:, 0:256],
                                Kv[:, dch * NSEQ + kt_ * 128: dch * NSEQ + (kt_ + 1) * 128],
                                qb[:, dch * 256:(dch + 1) * 256],
                                start=(dch == 0), stop=False))
                        r.append(e.matmul(pS[p_][:, 0:256], sb["ones_rb"][0:1, 0:128],
                                          sb["negm_row"][0:1, 0:256], start=False, stop=True))
                        return r
                    smm = S.op("tensor", mmS,
                               waits=[("sRD", grp_rdy), ("sSC", expd.get(n - 3, 0))],
                               inc="sMM")
                    qg_lastuse[gp] = smm
                    m = kt - 4 * g
                    if m >= 0:
                        maskd = S.op("vector", lambda e, p_=p, m_=m, gp_=gp: e.tensor_add(
                            pS[p_][:, 0:256], pS[p_][:, 0:256], maskb[gp_][:, m_ * 256:(m_ + 1) * 256]),
                            waits=[("sMM", smm), ("sRD", grp_rdy)], inc="sVE")
                        mk_lastuse[gp] = maskd
                        ew = [("sMM", smm), ("sVE", maskd), ("sMM", avmm.get(n - 3, 0))]
                    else:
                        ew = [("sMM", smm), ("sMM", avmm.get(n - 3, 0))]
                    expd[n] = S.op("scalar", lambda e, p_=p: e.activation(
                        PT[p_][:, :], pS[p_][:, 0:256], AF.Exp, scale=1.0 / 32.0), waits=ew, inc="sSC")
                    if kt == 0 and pend_epi[0] is not None:
                        pend_epi[0]()
                        pend_epi[0] = None
                    if kt == 1:
                        if g + 1 < 8:
                            emit_group_dmas(g + 1)
                        for qc in range(2):
                            slot = 2 * g + qc
                            self.dma("sync", x0t[qc][:, :],
                                     dr["x0own"][slot * 128:(slot + 1) * 128, :],
                                     waits=[("sVE", epi_prev)])
                        x0_in[0] = x0_in[1] = self.ckpt("sync")
                    def mmAV(e, kt_=kt, p_=p, st_=(kt == 0), sp_=(kt == L - 1)):
                        r = []
                        for qc in range(2):
                            for ec in range(2):
                                r.append(e.matmul(
                                    av[qc * 2 + ec][:, 0:512],
                                    PT[p_][:, qc * 128:(qc + 1) * 128],
                                    sb["V_tm"][:, kt_ * D + ec * 512: kt_ * D + (ec + 1) * 512],
                                    start=st_, stop=sp_))
                        for qc in range(2):
                            r.append(e.matmul(pden[:, qc:qc + 1], PT[p_][:, qc * 128:(qc + 1) * 128],
                                              sb["ones_cb"][:, :],
                                              start=(st_ and qc == 0), stop=(sp_ and qc == 1)))
                        return r
                    w = [("sSC", expd[n])]
                    if kt == 0:
                        w += [("sVE", epi_v), ("sVE", recip_done)]
                    avmm[n] = S.op("tensor", mmAV, waits=w, inc="sMM")
                    n += 1
                def mk_epi(g_=g, last_av=avmm[n - 1], x0r=x0_in[0]):
                    nonlocal epi_v, epi_prev, recip_done
                    recip_done = S.op("vector", lambda e: e.reciprocal(rcp[:, 0:2], pden[:, 0:2]),
                                      waits=[("sMM", last_av)], inc="sVE")
                    for qc in range(2):
                        slot = 2 * g_ + qc
                        epi = S.op("vector", lambda e, qc_=qc: [
                            e.scalar_tensor_tensor(x1t[qc_][:, ec * 512:(ec + 1) * 512],
                                                   av[qc_ * 2 + ec][:, 0:512], rcp[:, qc_:qc_ + 1],
                                                   x0t[qc_][:, ec * 512:(ec + 1) * 512],
                                                   ALU.mult, ALU.add)
                            for ec in range(2)],
                            waits=[("sVE", recip_done), ("sRD", x0r),
                                   ("sGP", x1_store[slot - 2])], inc="sVE")
                        x1_store[slot] = self.dma("gpsimd", dr["x1s"][slot * 128:(slot + 1) * 128, :],
                                                  x1t[qc][:, :], waits=[("sVE", epi)])
                        epi_v = epi
                    epi_prev = epi_v
                pend_epi[0] = mk_epi
            pend_epi[0]()
            self.end()

    def p2w(self):
        nc, sb, dr = self.nc, self.sb, self.dram
        self.alloc("wdqT", [128, 8 * D], BF16, side="right")
        self.alloc("mxb_bf", [1, D], BF16)
        with ExitStack() as ph:
            T = lambda n, s, dt: ph.enter_context(nc.sbuf_tensor(n, s, dt))
            wbuf = [T(f"w_wbuf{p}", [128, D], F32) for p in range(2)]
            wq_bf = [T(f"w_wq{p}", [128, D], BF16) for p in range(2)]
            z = [T(f"w_z{i}", [128, D], F32) for i in range(2)]
            zsq = [T(f"w_zsq{i}", [128, D], F32) for i in range(2)]
            mg = [T(f"w_mg{i}", [128, D], F32) for i in range(2)]
            mxbf = T("w_mxbf", [1, D], F32)
            pT = ph.enter_context(nc.psum_tensor("w_pT", [128, 1024], BF16))
            self.begin()
            S = self.S
            self.dma("sync", mxbf[:, :], dr["mxb"][:, :])
            rdy0 = self.ckpt("sync")
            S.op("scalar", lambda e: e.activation(sb["mxb_bf"][:, :], mxbf[:, :], AF.Copy),
                 waits=[("sRD", rdy0)], inc="sSC")
            vq_l = [0] * (8 + 2)
            tr_l = [0] * (8 + 2)
            ev_prev = 0
            wv = sb["wdqT"][:, :].rearrange("p (c t) -> p c t", t=D)
            for wt in range(8):
                p = wt % 2
                self.dma("sync", wbuf[p][:, :], dr["mxw"][wt * 128:(wt + 1) * 128, :],
                         waits=[("sVE", vq_l[wt - 2])])
                rdy = self.ckpt("sync")
                vq_l[wt] = self.quant(wbuf[p], wq_bf[p], z[p], zsq[p], mg[p], ("sRD", rdy), p,
                                      vq_waits=[("sMM", tr_l[wt - 2])])
                tr_l[wt], ev_prev = self.tr8(wq_bf[p], pT, wv[:, :, wt * 128:(wt + 1) * 128],
                                             waits=[("sVE", vq_l[wt]), ("sVE", ev_prev)])
            self.end()

    def p2(self):
        nc, sb, dr = self.nc, self.sb, self.dram
        with ExitStack() as ph:
            T = lambda n, s, dt: ph.enter_context(nc.sbuf_tensor(n, s, dt))
            x1t = [T(f"f_x1{p}", [128, D], F32) for p in range(3)]
            ht = [T(f"f_h{p}", [128, D], F32) for p in range(3)]
            xd = [T(f"f_xd{p}", [128, D], BF16) for p in range(2)]
            xdfm = [T(f"f_xdfm{p}", [128, D], BF16) for p in range(2)]
            x2t = [T(f"f_x2{p}", [128, D], F32) for p in range(2)]
            z = [T(f"f_z{i}", [128, D], F32) for i in range(2)]
            zsq = [T(f"f_zsq{i}", [128, D], F32) for i in range(2)]
            mg = [T(f"f_mg{i}", [128, D], F32) for i in range(2)]
            pT = ph.enter_context(nc.psum_tensor("f_pT", [128, 1024], BF16))
            pp = [ph.enter_context(nc.psum_tensor(f"f_pp{i}", [128, 1024], F32)) for i in range(2)]
            self.begin()
            S = self.S
            lnap = [0] * (NT_OWN + 2)
            vq_l = [0] * (NT_OWN + 2)
            trx = [0] * (NT_OWN + 2)
            fmev = [0] * (NT_OWN + 2)
            mmv = [0] * (NT_OWN + 2)
            epi = [0] * (NT_OWN + 2)
            st_l = [0] * (NT_OWN + 2)
            def front(s):
                p3 = s % 3
                p = s % 2
                p4 = s % 4
                self.dma("sync", x1t[p3][:, :], dr["x1s"][s * 128:(s + 1) * 128, :],
                         waits=[("sVE", epi[s - 3])])
                rdy = self.ckpt("sync")
                lnap[s] = self.ln_stats_apply(x1t[p3], ht[p3], ("sRD", rdy), p4,
                                              extra_waits=[("sVE", vq_l[s - 3])],
                                              sum_on_scalar=True)
                vq_l[s] = self.quant(ht[p3], xd[p], z[p], zsq[p], mg[p], ("sSC", lnap[s]), p4,
                                     vq_waits=[("sMM", trx[s - 2])])
                trx[s], fmev[s] = self.tr8(xd[p], pT, xdfm[p][:, :],
                                           waits=[("sVE", vq_l[s]), ("sVE", fmev[s - 1])],
                                           evac_waits=[("sMM", mmv[s - 1])])
                mmv[s] = S.op("tensor", lambda e, xf=xdfm[p], rp=pp[s % 2]: self.proj_mm(
                    e, rp, xf, sb["wdqT"], sb["mxb_bf"]),
                    waits=[("sVE", fmev[s]), ("sVE", epi[s - 2])], inc="sMM")

            def back(s):
                if s < 0:
                    return
                p = s % 2
                p3 = s % 3
                epi[s] = S.op("vector", lambda e, p_=p, p3_=p3, rp=pp[s % 2]: [
                    e.tensor_add(x2t[p_][:, 0:512], rp[:, 0:512], x1t[p3_][:, 0:512]),
                    e.tensor_add(x2t[p_][:, 512:1024], rp[:, 512:1024], x1t[p3_][:, 512:1024]),
                ], waits=[("sMM", mmv[s]), ("sGP", st_l[s - 2])], inc="sVE")
                st_l[s] = self.dma("gpsimd", dr["x2s"][s * 128:(s + 1) * 128, :], x2t[p][:, :],
                                   waits=[("sVE", epi[s])])

            front(0)
            for s in range(NT_OWN):
                if s + 1 < NT_OWN:
                    front(s + 1)
                back(s)
            self.end()

    def p3w(self):
        nc, sb, dr = self.nc, self.sb, self.dram
        self.alloc("bo_bf", [1, D], BF16)
        self.alloc("bf2_bf", [1, D], BF16)
        self.alloc("bd_bf", [1, D], BF16)
        self.alloc("Wff1", [128, 8 * 4 * D], BF16)
        self.alloc("negmk_row", [1, D], BF16)
        self.alloc("bq_r", [1, D], BF16, side="right")
        self.alloc("bk_r", [1, D], BF16, side="right")
        self.alloc("bv_r", [1, D], BF16, side="right")
        self.alloc("wo_bf", [128, 8 * D], BF16, side="right")
        self.alloc("Wq_b", [128, 8 * D], BF16, side="right")
        self.alloc("Wk_b", [128, 8 * D], BF16, side="right")
        self.alloc("Wv_b", [128, 8 * D], BF16, side="right")
        self.alloc("b1row", [1, 4 * D], BF16, side="right")
        with ExitStack() as ph:
            T = lambda n, s, dt: ph.enter_context(nc.sbuf_tensor(n, s, dt))
            stage = [T(f"g_st{p}", [128, D], F32) for p in range(2)]
            raw_bf = [T(f"g_raw{p}", [128, D], BF16) for p in range(2)]
            g1col = T("g_g1", [128, 8], F32)
            g2col = T("g_g2", [128, 8], F32)
            b1bf = T("g_b1bf", [128, 8], BF16)
            b2bf = T("g_b2bf", [128, 8], BF16)
            sqb = T("g_sqb", [128, D], BF16)
            rowf = T("g_rowf", [1, D], F32)
            rowf2 = T("g_rowf2", [1, D], F32)
            self.begin()
            S = self.S
            self.dma("sync", g1col[:, :], dr["g1c"][:, :])
            self.dma("sync", g2col[:, :], dr["g2c"][:, :])
            self.dma("gpsimd", sb["bo_bf"][:, :], dr["bo_r"][:, :])
            self.dma("gpsimd", sb["bf2_bf"][:, :], dr["bf2_r"][:, :])
            self.dma("gpsimd", sb["bd_bf"][:, :], dr["bd_r"][:, :])
            self.dma("gpsimd", b1bf[:, :], dr["b1c"][:, :])
            self.dma("gpsimd", b2bf[:, :], dr["b2c"][:, :])
            bcast_rdy = self.ckpt("gpsimd")
            for jc in range(8):
                self.dma("gpsimd", sb["wo_bf"][:, jc * D:(jc + 1) * D],
                         dr["bwo"][jc * 128:(jc + 1) * 128, :])

            with ExitStack() as pA:
                pRow = [pA.enter_context(nc.psum_tensor(f"g_pR{i}", [1, 512], F32)) for i in range(2)]
                pCS = [pA.enter_context(nc.psum_tensor(f"g_pC{i}", [1, 512], F32)) for i in range(2)]

                def wprep(wname, dest, brow, colsq):
                    last_mm = 0
                    last_raw = [0, 0]
                    cs = 0
                    for jc in range(8):
                        p = jc % 2
                        self.dma("sync", stage[p][:, :], dr[wname][jc * 128:(jc + 1) * 128, :],
                                 waits=[("sSC", last_raw[p])])
                        rd = self.ckpt("sync")
                        rc = S.op("scalar", lambda e, st_=stage[p], rb=raw_bf[p]: e.activation(
                            rb[:, :], st_[:, :], AF.Copy),
                            waits=[("sRD", rd), ("sMM", last_mm)], inc="sSC")
                        last_mm = S.op("tensor", lambda e, jc_=jc, rb=raw_bf[p]: [
                            e.matmul(pRow[h][0:1, 0:512], b1bf[:, jc_:jc_ + 1],
                                     rb[:, h * 512:(h + 1) * 512],
                                     start=(jc_ == 0), stop=(jc_ == 7))
                            for h in range(2)],
                            waits=[("sSC", rc), ("sGD", bcast_rdy)], inc="sMM")
                        gsc = S.op("scalar", lambda e, st_=stage[p], jc_=jc: e.activation(
                            dest[:, jc_ * D:(jc_ + 1) * D], st_[:, :], AF.Identity,
                            scale=g1col[:, jc_:jc_ + 1]),
                            waits=[("sRD", rd)], inc="sSC")
                        last_raw[p] = gsc
                        if colsq:
                            sq2 = S.op("scalar", lambda e, jc_=jc: e.activation(
                                sqb[:, :], dest[:, jc_ * D:(jc_ + 1) * D], AF.Square),
                                waits=[("sSC", gsc)], inc="sSC")
                            cs = S.op("tensor", lambda e, jc_=jc: [
                                e.matmul(pCS[h][0:1, 0:512], sb["ones_cb"][:, :],
                                         sqb[:, h * 512:(h + 1) * 512],
                                         start=(jc_ == 0), stop=(jc_ == 7))
                                for h in range(2)],
                                waits=[("sSC", sq2)], inc="sMM")
                    ev = S.op("vector", lambda e: [
                        e.tensor_copy(brow[0:1, 0:512], pRow[0][0:1, 0:512]),
                        e.tensor_copy(brow[0:1, 512:1024], pRow[1][0:1, 0:512]),
                    ], waits=[("sMM", last_mm)], inc="sVE")
                    return ev, cs

                evq, _ = wprep("bwq", sb["Wq_b"], sb["bq_r"], False)
                evk, cs_done = wprep("bwk", sb["Wk_b"], sb["bk_r"], True)
                evv, _ = wprep("bwv", sb["Wv_b"], sb["bv_r"], False)
                vcs = S.op("vector", lambda e: [
                    e.tensor_copy(rowf2[0:1, 0:512], pCS[0][0:1, 0:512]),
                    e.tensor_copy(rowf2[0:1, 512:1024], pCS[1][0:1, 0:512]),
                ], waits=[("sMM", cs_done)], inc="sVE")
                vm1 = S.op("vector", lambda e: e.reduce_max(sb["tiny"][0:1, 0:1], rowf2[0:1, 0:1024],
                                                            axis=AX.X), waits=[("sVE", vcs)], inc="sVE")
                vm2 = S.op("vector", lambda e: e.reduce_max(sb["tiny"][0:1, 1:2], sb["bk_r"][0:1, :],
                                                            axis=AX.X, apply_absolute_value=True),
                           waits=[("sVE", evk)], inc="sVE")
                sm1 = S.op("scalar", lambda e: [
                    e.activation(sb["tiny"][0:1, 2:3], sb["tiny"][0:1, 0:1], AF.Ln),
                    e.activation(sb["tiny"][0:1, 3:4], sb["tiny"][0:1, 2:3], AF.Exp, scale=0.5),
                ], waits=[("sVE", vm1)], inc="sSC")
                vm3 = S.op("vector", lambda e: e.tensor_scalar_mul(
                    sb["tiny"][0:1, 4:5], sb["tiny"][0:1, 3:4], 32.1),
                    waits=[("sSC", sm1)], inc="sVE")
                vm4 = S.op("vector", lambda e: e.tensor_add(
                    sb["tiny"][0:1, 5:6], sb["tiny"][0:1, 4:5], sb["tiny"][0:1, 1:2]),
                    waits=[("sVE", vm3), ("sVE", vm2)], inc="sVE")
                vm5 = S.op("vector", lambda e: e.tensor_scalar_mul(
                    sb["tiny"][0:1, 6:7], sb["tiny"][0:1, 5:6], -1.02),
                    waits=[("sVE", vm4)], inc="sVE")
                S.op("scalar", lambda e: e.activation(sb["negmk_row"][:, :], sb["ones_rf"][0:1, :],
                                                      AF.Identity, scale=sb["tiny"][0:1, 6:7]),
                     waits=[("sVE", vm5)], inc="sSC")
            with ExitStack() as pB:
                pFF = [pB.enter_context(nc.psum_tensor(f"g_pF{i}", [1, 512], F32)) for i in range(8)]
                last_mm = 0
                last_raw = [0, 0]
                for jc in range(8):
                    for qtr in range(4):
                        p = (jc * 4 + qtr) % 2
                        self.dma("sync", stage[p][:, :],
                                 dr["bff1"][jc * 128:(jc + 1) * 128, qtr * D:(qtr + 1) * D],
                                 waits=[("sSC", last_raw[p])])
                        rd = self.ckpt("sync")
                        rc = S.op("scalar", lambda e, st_=stage[p], rb=raw_bf[p]: e.activation(
                            rb[:, :], st_[:, :], AF.Copy),
                            waits=[("sRD", rd), ("sMM", last_mm)], inc="sSC")
                        last_mm = S.op("tensor", lambda e, jc_=jc, q_=qtr, rb=raw_bf[p]: [
                            e.matmul(pFF[q_ * 2 + h][0:1, 0:512], b2bf[:, jc_:jc_ + 1],
                                     rb[:, h * 512:(h + 1) * 512],
                                     start=(jc_ == 0), stop=(jc_ == 7))
                            for h in range(2)],
                            waits=[("sSC", rc)], inc="sMM")
                        last_raw[p] = S.op("scalar", lambda e, st_=stage[p], jc_=jc, q_=qtr: e.activation(
                            sb["Wff1"][:, jc_ * 4 * D + q_ * D: jc_ * 4 * D + (q_ + 1) * D],
                            st_[:, :], AF.Identity, scale=g2col[:, jc_:jc_ + 1]),
                            waits=[("sRD", rd)], inc="sSC")
                prev_cast = 0
                for i in range(4):
                    self.dma("sync", rowf[0:1, :], dr["bf1_r"][0:1, i * D:(i + 1) * D],
                             waits=[("sSC", prev_cast)])
                    rdf = self.ckpt("sync")
                    vr = S.op("vector", lambda e, i_=i: [
                        e.tensor_copy(rowf2[0:1, 0:512], pFF[2 * i_][0:1, 0:512]),
                        e.tensor_copy(rowf2[0:1, 512:1024], pFF[2 * i_ + 1][0:1, 0:512]),
                    ], waits=[("sMM", last_mm), ("sSC", prev_cast)], inc="sVE")
                    vr2 = S.op("vector", lambda e: e.tensor_add(rowf2[0:1, :], rowf2[0:1, :],
                                                                rowf[0:1, :]),
                               waits=[("sVE", vr), ("sRD", rdf)], inc="sVE")
                    prev_cast = S.op("scalar", lambda e, i_=i: e.activation(
                        sb["b1row"][0:1, i_ * D:(i_ + 1) * D], rowf2[0:1, :], AF.Copy),
                        waits=[("sVE", vr2)], inc="sSC")
                self.dma("gpsimd", dr["b1row_s"][:, :], sb["b1row"][0:1, :],
                         waits=[("sSC", prev_cast)])
            self.end()

    def p3a(self):
        nc, sb, dr = self.nc, self.sb, self.dram
        with ExitStack() as ph:
            T = lambda n, s, dt: ph.enter_context(nc.sbuf_tensor(n, s, dt))
            x2t = [T(f"h_x2{p}", [128, D], F32) for p in range(2)]
            n2b = [T(f"h_n2{p}", [128, D], BF16) for p in range(2)]
            n2fm = [T(f"h_n2fm{p}", [128, D], BF16) for p in range(2)]
            qet = [T(f"h_qe{p}", [128, D], BF16) for p in range(2)]
            qefm = [T(f"h_qefm{p}", [128, D], BF16) for p in range(2)]
            ekt = [T(f"h_ek{p}", [128, D], BF16) for p in range(2)]
            v2b = [T(f"h_v2{p}", [128, D], BF16) for p in range(2)]
            U_sb = T("h_Usb", [128, 2056], F32)
            pT = ph.enter_context(nc.psum_tensor("h_pT", [128, 1024], BF16))
            pP = ph.enter_context(nc.psum_tensor("h_pP", [128, 1024], F32))
            Ub = [ph.enter_context(nc.psum_tensor(f"h_U{i}", [128, 512], F32)) for i in range(4)]
            Ucol = ph.enter_context(nc.psum_tensor("h_Uc", [128, 512], F32))
            self.begin()
            S = self.S
            lnap = [0] * (NT_OWN + 2)
            trn = [0] * (NT_OWN + 2)
            fmev = [0] * (NT_OWN + 2)
            qev = [0] * (NT_OWN + 2)
            qtr_l = [0] * (NT_OWN + 2)
            qfm_ev = [0] * (NT_OWN + 2)
            spl = [0] * (NT_OWN + 2)
            ekev = [0] * (NT_OWN + 2)
            vev = [0] * (NT_OWN + 2)
            umm = [0] * (NT_OWN + 2)
            vrq_l = [0] * (NT_OWN + 2)
            def front(s):
                p = s % 2
                self.dma("sync", x2t[p][:, :], dr["x2s"][s * 128:(s + 1) * 128, :],
                         waits=[("sSC", lnap[s - 2])])
                rdy = self.ckpt("sync")
                lnap[s] = self.ln_stats_apply(x2t[p], n2b[p], ("sRD", rdy), s % 4,
                                              extra_waits=[("sMM", trn[s - 2])],
                                              fused2=True,
                                              save=(sb["ln1_r"], sb["ln1_b"], s))

            def trn_emit(s):
                p = s % 2
                trn[s], fmev[s] = self.tr8(n2b[p], pT, n2fm[p][:, :],
                                           waits=[("sSC", lnap[s]), ("sVE", qfm_ev[s - 1])],
                                           evac_waits=[("sMM", umm[s - 1])])

            def back(s):
                p = s % 2
                mmq = S.op("tensor", lambda e, xf=n2fm[p]: self.proj_mm(
                    e, pP, xf, sb["Wq_b"], sb["bq_r"]),
                    waits=[("sVE", fmev[s]), ("sSC", vev[s - 1]), ("sVE", vrq_l[s - 1])], inc="sMM")
                if s + 1 < NT_OWN:
                    front(s + 1)
                vq1 = S.op("vector", lambda e: [
                    e.reduce_max(sb["qt"][:, hh:hh + 1], pP[:, hh * 256:(hh + 1) * 256], axis=AX.X)
                    for hh in range(4)], waits=[("sMM", mmq)], inc="sVE")
                vq2 = S.op("vector", lambda e: e.tensor_scalar_mul(sb["qt"][:, 4:8], sb["qt"][:, 0:4],
                                                                   -0.25),
                           waits=[("sVE", vq1)], inc="sVE")
                qev[s] = S.op("scalar", lambda e, o=qet[p]: [
                    e.activation(o[:, hh * 256:(hh + 1) * 256], pP[:, hh * 256:(hh + 1) * 256],
                                 AF.Exp, scale=0.25, bias=sb["qt"][:, 4 + hh:5 + hh],
                                 accum_out=sb["qt"][:, 8 + hh:9 + hh])
                    for hh in range(4)], waits=[("sVE", vq2), ("sMM", qtr_l[s - 2])], inc="sSC")
                vrq_l[s] = S.op("vector", lambda e, s_=s: e.reciprocal(
                    sb["rq"][:, 4 * s_:4 * s_ + 4], sb["qt"][:, 8:12]),
                    waits=[("sSC", qev[s])], inc="sVE")
                qtr_l[s], qfm_ev[s] = self.tr8(qet[p], pT, qefm[p][:, :],
                                               waits=[("sSC", qev[s]), ("sVE", fmev[s])],
                                               evac_waits=[("sGP", spl[s - 2])])
                qv = dr["qefm_s"][:, :].rearrange("p (c t) -> p c t", t=2048)[:, :, s * 128:(s + 1) * 128]
                spl[s] = self.dma("gpsimd", qv,
                                  qefm[p][:, :].rearrange("p (c t) -> p c t", t=128),
                                  waits=[("sVE", qfm_ev[s])])
                mmk = S.op("tensor", lambda e, xf=n2fm[p]: self.proj_mm(
                    e, pP, xf, sb["Wk_b"], sb["bk_r"], extra_rows=(sb["negmk_row"],)),
                    waits=[("sSC", qev[s]), ("sVE", vq1)], inc="sMM")
                ekev[s] = S.op("scalar", lambda e, o=ekt[p]: [
                    e.activation(o[:, 0:512], pP[:, 0:512], AF.Exp, scale=0.25),
                    e.activation(o[:, 512:1024], pP[:, 512:1024], AF.Exp, scale=0.25),
                ], waits=[("sMM", mmk), ("sMM", umm[s - 1])], inc="sSC")
                mmv = S.op("tensor", lambda e, xf=n2fm[p]: self.proj_mm(
                    e, pP, xf, sb["Wv_b"], sb["bv_r"]),
                    waits=[("sSC", ekev[s])], inc="sMM")
                vev[s] = S.op("scalar", lambda e, o=v2b[p]: [
                    e.activation(o[:, 0:512], pP[:, 0:512], AF.Copy),
                    e.activation(o[:, 512:1024], pP[:, 512:1024], AF.Copy),
                ], waits=[("sMM", mmv)], inc="sSC")
                def mmU(e, s_=s, ek=ekt[p], v_=v2b[p]):
                    r = []
                    for dc in range(8):
                        hh = dc >> 1
                        r.append(e.matmul(Ub[hh][:, (dc & 1) * 256:((dc & 1) + 1) * 256],
                                          ek[:, dc * 128:(dc + 1) * 128],
                                          v_[:, hh * 256:(hh + 1) * 256],
                                          start=(s_ == 0 and (dc & 1) == 0),
                                          stop=(s_ == NT_OWN - 1 and (dc & 1) == 1)))
                    for dc in range(8):
                        r.append(e.matmul(Ucol[:, dc:dc + 1], ek[:, dc * 128:(dc + 1) * 128],
                                          sb["ones_cb"][:, :],
                                          start=(s_ == 0 and dc == 0),
                                          stop=(s_ == NT_OWN - 1 and dc == 7)))
                    return r
                umm[s] = S.op("tensor", mmU, waits=[("sSC", ekev[s]), ("sSC", vev[s])], inc="sMM")

            front(0)
            trn_emit(0)
            for s in range(NT_OWN):
                back(s)
                if s + 1 < NT_OWN:
                    trn_emit(s + 1)
            uev = S.op("scalar", lambda e: [
                e.activation(U_sb[:, bh * 512:(bh + 1) * 512], Ub[bh][:, 0:512], AF.Copy)
                for bh in range(4)], waits=[("sMM", umm[NT_OWN - 1])], inc="sSC")
            ucv = S.op("vector", lambda e: e.tensor_copy(U_sb[:, 2048:2056], Ucol[:, 0:8]),
                       waits=[("sMM", umm[NT_OWN - 1])], inc="sVE")
            self.dma("gpsimd", dr["cc_in"][:, :], U_sb[:, :],
                     waits=[("sSC", uev), ("sVE", ucv)])
            self.end()

    def p3x(self):
        nc, sb, dr = self.nc, self.sb, self.dram
        self.alloc("C_fm", [128, 2 * D], BF16)
        with ExitStack() as ph:
            T = lambda n, s, dt: ph.enter_context(nc.sbuf_tensor(n, s, dt))
            Ug = T("x_Ug", [128, 2056], F32)
            rcb = T("x_rc", [128, 8], F32)
            self.begin()
            S = self.S
            cc = S.op("gpsimd", lambda e: e.collective_compute(
                "AllReduce", ALU.add, replica_groups=[[0, 1], [2, 3], [4, 5], [6, 7]],
                ins=[dr["cc_in"][:, :]], outs=[dr["cc_out"][:, :]]), inc="sCC")
            self.dma("sync", Ug[:, :], dr["cc_out"][:, :], waits=[("sCC", cc)])
            rdy = self.ckpt("sync")
            vr = S.op("vector", lambda e: e.reciprocal(rcb[:, 0:8], Ug[:, 2048:2056]),
                      waits=[("sRD", rdy)], inc="sVE")
            S.op("scalar", lambda e: [
                e.activation(sb["C_fm"][:, dc * 256:(dc + 1) * 256], Ug[:, dc * 256:(dc + 1) * 256],
                             AF.Identity, scale=rcb[:, dc:dc + 1])
                for dc in range(8)], waits=[("sVE", vr)], inc="sSC")
            self.end()

    def p3e1(self):
        nc, sb, dr = self.nc, self.sb, self.dram
        with ExitStack() as ph:
            T = lambda n, s, dt: ph.enter_context(nc.sbuf_tensor(n, s, dt))
            x2t = [T(f"i_x2{p}", [128, D], F32) for p in range(2)]
            qet = [T(f"i_qe{p}", [128, D], BF16) for p in range(2)]
            atb = [T(f"i_at{p}", [128, D], BF16) for p in range(2)]
            atfm = [T(f"i_atfm{p}", [128, D], BF16) for p in range(2)]
            inpt = [T(f"i_inp{p}", [128, D], F32) for p in range(2)]
            xxt = [T(f"i_xx{p}", [128, D], F32) for p in range(2)]
            pT = ph.enter_context(nc.psum_tensor("i_pT", [128, 1024], BF16))
            pP = [ph.enter_context(nc.psum_tensor(f"i_pP{i}", [128, 1024], F32)) for i in range(2)]
            pW = ph.enter_context(nc.psum_tensor("i_pW", [128, 1024], F32))
            self.begin()
            S = self.S
            atev = [0] * (NT_OWN + 2)
            trat = [0] * (NT_OWN + 2)
            fmev = [0] * (NT_OWN + 2)
            mmw_l = [0] * (NT_OWN + 2)
            mma_l = [0] * (NT_OWN + 2)
            xxv = [0] * (NT_OWN + 2)
            inps = [0] * (NT_OWN + 2)
            stl = [0] * (NT_OWN + 2)
            sq_l = [0] * (NT_OWN + 2)
            for s in range(NT_OWN):
                p = s % 2
                qv = dr["qefm_s"][:, :].rearrange("p (c t) -> p c t", t=2048)[:, :, s * 128:(s + 1) * 128]
                self.dma("sync", qet[p][:, :].rearrange("p (c t) -> p c t", t=128), qv,
                         waits=[("sMM", mma_l[s - 2])])
                self.dma("sync", x2t[p][:, :], dr["x2s"][s * 128:(s + 1) * 128, :],
                         waits=[("sSC", inps[s - 2]), ("sSC", sq_l[s - 2])])
                rdy = self.ckpt("sync")
                pPb = pP[s % 2]
                def mmA(e, q_=qet[p], pp_=pPb):
                    r = []
                    for hh in range(4):
                        for j, dc in enumerate((2 * hh, 2 * hh + 1)):
                            bank_first = (hh % 2 == 0) and j == 0
                            r.append(e.matmul(pp_[:, hh * 256:(hh + 1) * 256],
                                              q_[:, dc * 128:(dc + 1) * 128],
                                              sb["C_fm"][:, dc * 256:(dc + 1) * 256],
                                              start=bank_first, stop=(j == 1 and hh % 2 == 1)))
                    return r
                mma_l[s] = S.op("tensor", mmA, waits=[("sRD", rdy), ("sSC", atev[s - 2])], inc="sMM")
                atev[s] = S.op("scalar", lambda e, o=atb[p], s_=s, pp_=pPb: [
                    e.activation(o[:, hh * 256:(hh + 1) * 256], pp_[:, hh * 256:(hh + 1) * 256],
                                 AF.Identity, scale=sb["rq"][:, 4 * s_ + hh:4 * s_ + hh + 1])
                    for hh in range(4)], waits=[("sMM", mma_l[s]), ("sMM", trat[s - 2])], inc="sSC")
                trat[s], fmev[s] = self.tr8(atb[p], pT, atfm[p][:, :],
                                            waits=[("sSC", atev[s]), ("sVE", fmev[s - 1])],
                                            evac_waits=[("sMM", mmw_l[s - 1])])
                mmw_l[s] = S.op("tensor", lambda e, xf=atfm[p]: self.proj_mm(
                    e, pW, xf, sb["wo_bf"], sb["bo_bf"]),
                    waits=[("sVE", fmev[s]), ("sVE", xxv[s - 1])], inc="sMM")
                inps[s] = S.op("scalar", lambda e, a=x2t[p], o=inpt[p], s_=s: e.activation(
                    o[:, :], a[:, :], AF.Identity, scale=sb["ln1_r"][:, s_:s_ + 1],
                    bias=sb["ln1_b"][:, s_:s_ + 1]),
                    waits=[("sRD", rdy), ("sVE", xxv[s - 2])], inc="sSC")
                xxv[s] = S.op("vector", lambda e, p_=p: [
                    e.tensor_add(xxt[p_][:, 0:512], pW[:, 0:512], inpt[p_][:, 0:512]),
                    e.tensor_add(xxt[p_][:, 512:1024], pW[:, 512:1024], inpt[p_][:, 512:1024]),
                ], waits=[("sMM", mmw_l[s]), ("sSC", inps[s]), ("sGP", stl[s - 2])], inc="sVE")
                sq_l[s] = self.ln_stats_apply(xxt[p], None, ("sVE", xxv[s]), p,
                                              save=(sb["ln2_r"], sb["ln2_b"], s))
                stl[s] = self.dma("gpsimd", dr["xxs"][s * 128:(s + 1) * 128, :], xxt[p][:, :],
                                  waits=[("sVE", xxv[s])])
            self.end()

    def p3e2(self):
        nc, sb, dr = self.nc, self.sb, self.dram
        self.alloc("ff2_bf", [128, 32 * D], BF16)
        with ExitStack() as ph:
            T = lambda n, s, dt: ph.enter_context(nc.sbuf_tensor(n, s, dt))
            xxt = [sb["sq"]] + [T(f"j_xx{i}", [128, D], F32) for i in range(3)]
            n3b = sb["rsc"]
            n3g = T("j_n3g", [128, 4 * D], BF16)
            u_g = T("j_ug", [128, 16 * D], BF16)
            b1v = T("j_b1v", [32, 128], BF16)
            b1col = T("j_b1col", [128, 32], BF16)
            pT = ph.enter_context(nc.psum_tensor("j_pT", [128, 1024], BF16))
            pF1 = [ph.enter_context(nc.psum_tensor(f"j_f1{i}", [128, 512], F32)) for i in range(3)]
            pF2 = [ph.enter_context(nc.psum_tensor(f"j_f2{i}", [128, 1024], F32)) for i in range(2)]
            self.begin()
            S = self.S
            self.dma("gpsimd", b1v[:, :],
                     dr["b1row_s"][:, :].rearrange("o (a b) -> (o a) b", b=128))
            for c in range(32):
                self.dma("gpsimd", sb["ff2_bf"][:, c * D:(c + 1) * D],
                         dr["bff2"][c * 128:(c + 1) * 128, :])
            ff2_rdy = self.ckpt("gpsimd")
            b1tr = S.op("tensor", lambda e: e.transpose(pT[:, 0:32], b1v[:, :],
                                                        sb["ident"][0:32, 0:32]),
                        waits=[("sGD", ff2_rdy)], inc="sMM")
            b1ev = S.op("vector", lambda e: e.tensor_copy(b1col[:, :], pT[:, 0:32]),
                        waits=[("sMM", b1tr)], inc="sVE")
            lnap = [0] * (NT_OWN + 4)
            trn = [0] * (NT_OWN + 4)
            fme = [0] * (NT_OWN + 4)
            epi = [0] * (NT_OWN + 4)
            stl = [0] * (NT_OWN + 4)
            mmf1_last = [0] * 6
            mmf2_l = [0] * (NT_OWN + 4)
            gel_d = {}
            gel_last = [0] * 6

            def front(t):
                self.dma("sync", xxt[t % 4][:, :], dr["xxs"][t * 128:(t + 1) * 128, :],
                         waits=[("sGP", stl[t - 4])])
                rdy = self.ckpt("sync")
                lnap[t] = S.op("scalar", lambda e, a=xxt[t % 4], t_=t: e.activation(
                    n3b[:, :], a[:, :], AF.Identity, scale=sb["ln2_r"][:, t_:t_ + 1],
                    bias=sb["ln2_b"][:, t_:t_ + 1]),
                    waits=[("sRD", rdy), ("sMM", trn[t - 1])], inc="sSC")
                trn[t] = S.op("tensor", lambda e: [
                    e.transpose(pT[:, c * 128:(c + 1) * 128], n3b[:, c * 128:(c + 1) * 128],
                                sb["ident"][:, :]) for c in range(8)],
                    waits=[("sSC", lnap[t]), ("sVE", fme[t - 1] if t else b1ev)], inc="sMM")
                dst = n3g[:, :].rearrange("p (c b t) -> p c b t", b=4, t=128)[:, :, t % 4, :]
                fme[t] = S.op("vector", lambda e, d=dst: e.tensor_copy(
                    d, pT[:, :].rearrange("p (c t) -> p c t", t=128)),
                    waits=[("sMM", trn[t]), ("sMM", mmf1_last[t // 4 - 1])], inc="sVE")

            for t in range(4):
                front(t)
            for g in range(4):
                for oc in range(32):
                    gidx = g * 32 + oc
                    q = gidx % 3
                    def mmF1(e, oc_=oc, q_=q):
                        r = []
                        for ic in range(8):
                            r.append(e.matmul(
                                pF1[q_][:, 0:512],
                                sb["Wff1"][:, ic * 4 * D + oc_ * 128: ic * 4 * D + (oc_ + 1) * 128],
                                n3g[:, ic * 512:(ic + 1) * 512],
                                start=(ic == 0), stop=(ic == 7)))
                        return r
                    w = [("sSC", gel_d.get(gidx - 3, 0))]
                    if oc == 0:
                        w += [("sVE", fme[4 * g + 3])]
                    mmf1 = S.op("tensor", mmF1, waits=w, inc="sMM")
                    gel_d[gidx] = S.op("scalar", lambda e, oc_=oc, q_=q: e.activation(
                        u_g[:, oc_ * 512:(oc_ + 1) * 512], pF1[q_][:, 0:512],
                        AF.Gelu_apprx_tanh, bias=b1col[:, oc_:oc_ + 1]),
                        waits=[("sMM", mmf1), ("sVE", b1ev)], inc="sSC")
                mmf1_last[g] = mmf1
                for tb in range(4):
                    t = 4 * g + tb
                    pf = pF2[tb % 2]
                    def mmF2(e, tb_=tb, pf_=pf):
                        r = []
                        for hx in range(2):
                            r.append(e.matmul(pf_[:, hx * 512:(hx + 1) * 512],
                                              sb["ones_rb"][0:1, 0:128],
                                              sb["bf2_bf"][0:1, hx * 512:(hx + 1) * 512],
                                              start=True, stop=False))
                        for oc in range(32):
                            for hx in range(2):
                                r.append(e.matmul(
                                    pf_[:, hx * 512:(hx + 1) * 512],
                                    u_g[:, oc * 512 + tb_ * 128: oc * 512 + tb_ * 128 + 128],
                                    sb["ff2_bf"][:, oc * D + hx * 512: oc * D + (hx + 1) * 512],
                                    start=False, stop=(oc == 31)))
                        return r
                    mmf2_l[t] = S.op("tensor", mmF2,
                                     waits=[("sSC", gel_d[g * 32 + 31]), ("sGD", ff2_rdy),
                                            ("sVE", epi[t - 2])], inc="sMM")
                    epi[t] = S.op("vector", lambda e, pf_=pf, xs=xxt[t % 4]: [
                        e.tensor_add(xs[:, 0:512], pf_[:, 0:512], xs[:, 0:512]),
                        e.tensor_add(xs[:, 512:1024], pf_[:, 512:1024], xs[:, 512:1024]),
                    ], waits=[("sMM", mmf2_l[t])], inc="sVE")
                    stl[t] = self.dma("gpsimd", dr["xx2s"][t * 128:(t + 1) * 128, :],
                                      xxt[t % 4][:, :], waits=[("sVE", epi[t])])
                    if t + 4 < NT_OWN:
                        front(t + 4)
            self.end()

    def p3e3(self):
        nc, sb, dr = self.nc, self.sb, self.dram
        self.alloc("wd_bf", [128, 8 * D], BF16, side="right")
        self.alloc("wdqT", [128, 8 * D], BF16)
        with ExitStack() as ph:
            T = lambda n, s, dt: ph.enter_context(nc.sbuf_tensor(n, s, dt))
            xx2 = [T(f"k_xx2{p}", [128, D], F32) for p in range(2)]
            x2t = [T(f"k_x2{p}", [128, D], F32) for p in range(2)]
            xxb = [T(f"k_xxb{p}", [128, D], BF16) for p in range(2)]
            xxfm = [T(f"k_xxfm{p}", [128, D], BF16) for p in range(2)]
            x3t = [T(f"k_x3{p}", [128, D], F32) for p in range(3)]
            ht = [T(f"k_h{p}", [128, D], F32) for p in range(3)]
            xd = [T(f"k_xd{p}", [128, D], BF16) for p in range(2)]
            xdfm = [T(f"k_xdfm{p}", [128, D], BF16) for p in range(2)]
            x4t = [T(f"k_x4{p}", [128, D], F32) for p in range(2)]
            z = [T(f"k_z{i}", [128, D], F32) for i in range(2)]
            zsq = [T(f"k_zsq{i}", [128, D], F32) for i in range(2)]
            mg = [T(f"k_mg{i}", [128, D], F32) for i in range(2)]
            pT = ph.enter_context(nc.psum_tensor("k_pT", [128, 1024], BF16))
            pD = [ph.enter_context(nc.psum_tensor(f"k_pD{i}", [128, 1024], F32)) for i in range(2)]
            pM = ph.enter_context(nc.psum_tensor("k_pM", [128, 1024], F32))
            self.begin()
            S = self.S
            self.dma("gpsimd", sb["wdqT"][:, :], dr["wdq_s"][:, :])
            for jc in range(8):
                self.dma("gpsimd", sb["wd_bf"][:, jc * D:(jc + 1) * D],
                         dr["bwd"][jc * 128:(jc + 1) * 128, :])
            wd_rdy = self.ckpt("gpsimd")
            cast_l = [0] * (NT_OWN + 2)
            trx = [0] * (NT_OWN + 2)
            fmev = [0] * (NT_OWN + 2)
            mmd = [0] * (NT_OWN + 2)
            x3v = [0] * (NT_OWN + 2)
            lnap = [0] * (NT_OWN + 2)
            vq_l = [0] * (NT_OWN + 2)
            trq = [0] * (NT_OWN + 2)
            fmq = [0] * (NT_OWN + 2)
            mmx = [0] * (NT_OWN + 2)
            x4v = [0] * (NT_OWN + 2)
            stl = [0] * (NT_OWN + 2)
            def stA(s):
                p = s % 2
                self.dma("sync", xx2[p][:, :], dr["xx2s"][s * 128:(s + 1) * 128, :],
                         waits=[("sSC", cast_l[s - 2])])
                self.dma("sync", x2t[p][:, :], dr["x2s"][s * 128:(s + 1) * 128, :],
                         waits=[("sVE", x3v[s - 2])])
                rdy = self.ckpt("sync")
                cast_l[s] = S.op("scalar", lambda e, a=xx2[p], o=xxb[p]: e.activation(
                    o[:, :], a[:, :], AF.Copy),
                    waits=[("sRD", rdy), ("sMM", trx[s - 2])], inc="sSC")
                trx[s], fmev[s] = self.tr8(xxb[p], pT, xxfm[p][:, :],
                                           waits=[("sSC", cast_l[s]), ("sVE", fmq[s - 3])],
                                           evac_waits=[("sMM", mmd[s - 1])])
                mmd[s] = S.op("tensor", lambda e, xf=xxfm[p], rp=pD[s % 2]: self.proj_mm(
                    e, rp, xf, sb["wd_bf"], sb["bd_bf"]),
                    waits=[("sVE", fmev[s]), ("sGD", wd_rdy), ("sVE", x3v[s - 2])], inc="sMM")

            def stB(s):
                if s < 0 or s >= NT_OWN:
                    return
                p = s % 2
                p3 = s % 3
                p4 = s % 4
                x3v[s] = S.op("vector", lambda e, p_=p3, p2_=p, rp=pD[s % 2]: [
                    e.tensor_add(x3t[p_][:, 0:512], rp[:, 0:512], x2t[p2_][:, 0:512]),
                    e.tensor_add(x3t[p_][:, 512:1024], rp[:, 512:1024], x2t[p2_][:, 512:1024]),
                ], waits=[("sMM", mmd[s]), ("sSC", lnap[s - 3]), ("sVE", x4v[s - 3])], inc="sVE")
                lnap[s] = self.ln_stats_apply(x3t[p3], ht[p3], ("sVE", x3v[s]), p4,
                                              extra_waits=[("sVE", vq_l[s - 3])],
                                              sum_on_scalar=True)
                vq_l[s] = self.quant(ht[p3], xd[p], z[p], zsq[p], mg[p], ("sSC", lnap[s]), p4,
                                     vq_waits=[("sMM", trq[s - 2])])

            def stC(s):
                if s < 0 or s >= NT_OWN:
                    return
                p = s % 2
                p3 = s % 3
                trq[s], fmq[s] = self.tr8(xd[p], pT, xdfm[p][:, :],
                                          waits=[("sVE", vq_l[s]),
                                                 ("sVE", fmev[s + 2] if s + 2 < NT_OWN else fmev[NT_OWN - 1])],
                                          evac_waits=[("sMM", mmx[s - 1])])
                mmx[s] = S.op("tensor", lambda e, xf=xdfm[p]: self.proj_mm(
                    e, pM, xf, sb["wdqT"], sb["mxb_bf"]),
                    waits=[("sVE", fmq[s]), ("sVE", x4v[s - 1]), ("sGD", wd_rdy)], inc="sMM")
                x4v[s] = S.op("vector", lambda e, p_=p, p3_=p3: [
                    e.tensor_add(x4t[p_][:, 0:512], pM[:, 0:512], x3t[p3_][:, 0:512]),
                    e.tensor_add(x4t[p_][:, 512:1024], pM[:, 512:1024], x3t[p3_][:, 512:1024]),
                ], waits=[("sMM", mmx[s]), ("sGP", stl[s - 2])], inc="sVE")
                stl[s] = self.dma("gpsimd", dr["out"][s * 128:(s + 1) * 128, :], x4t[p][:, :],
                                  waits=[("sVE", x4v[s])])

            for i in range(NT_OWN + 2):
                if i < NT_OWN:
                    stA(i)
                stB(i - 1)
                stC(i - 2)
            self.end()

    def build(self):
        ph = self.phases
        if "prep" in ph:
            self.prep()
        if "p0a" in ph:
            self.p0a()
            self.free("wkT", "wvT")
        if "p0b" in ph:
            self.alloc("wqT", [128, 8 * D], BF16, side="right")
            self.begin()
            for jc in range(8):
                self.wq_loaded = self.dma("gpsimd", self.sb["wqT"][:, jc * D:(jc + 1) * D],
                                          self.dram["wqT"][jc * 128:(jc + 1) * 128, :])
            self.end()
            self.p0b()
            self.free("wqT")
        if "p1" in ph:
            self.p1()
            self.free("V_tm", "K_fm")
        if "p2w" in ph:
            self.p2w()
        if "p2" in ph:
            self.p2()
        if "p2w" in ph:
            self.begin()
            self.dma("gpsimd", self.dram["wdq_s"][:, :], self.sb["wdqT"][:, :])
            self.end()
            self.free("wdqT")
        if "p3w" in ph:
            self.p3w()
            self.free("b1row")
        if "p3a" in ph:
            self.p3a()
            self.free("Wv_b", "Wk_b", "Wq_b", "negmk_row")
        if "p3x" in ph:
            self.p3x()
        if "p3e1" in ph:
            self.p3e1()
            self.free("wo_bf", "C_fm", "bv_r", "bk_r", "bq_r")
        if "p3e2" in ph:
            self.p3e2()
            self.free("ff2_bf", "Wff1")
        if "p3e3" in ph:
            self.p3e3()
        if self._cms:
            self.free(*list(self._cms.keys()))
        self.stack.close()
        return self.nc


def _host_tables():
    i = np.arange(512, dtype=np.float64)
    phi = 10000.0 ** (-2.0 * (i - 1.0) / D)
    pos = np.arange(NSEQ, dtype=np.float64)
    ang = phi[:, None] * pos[None, :]
    cos = np.cos(ang).astype(np.float32)
    sin = np.sin(ang).astype(np.float32)
    cos_fm = np.ascontiguousarray(cos.reshape(4, 128, NSEQ).transpose(1, 0, 2).reshape(128, 4 * NSEQ))
    sin_fm = np.ascontiguousarray(sin.reshape(4, 128, NSEQ).transpose(1, 0, 2).reshape(128, 4 * NSEQ))
    return cos_fm, sin_fm


def _own_slices(tab_fm, h):
    t = tab_fm.reshape(128, 4, NT_ALL, 128)
    own = t[:, :, h::2, :]
    return np.ascontiguousarray(own.reshape(128, 4 * 2048))


def _p1_masks(h):
    M = np.zeros((8, 128, 4, 256), np.float32)
    for g in range(8):
        for m in range(4):
            kt = 4 * g + m
            kg = kt * 128 + np.arange(128)
            tA = (4 * g + h) * 128
            tB = (4 * g + 2 + h) * 128
            qg = np.concatenate([tA + np.arange(128), tB + np.arange(128)])
            M[g, :, m, :] = np.where(qg[None, :] >= kg[:, None], 0.0, NEG)
    return np.ascontiguousarray(M.reshape(8, 128, 1024))


def prepare_inputs(inputs):
    x = np.ascontiguousarray(inputs["L0_attn_a_x"], dtype=np.float32)
    perm = np.concatenate([np.arange(0, D, 2), np.arange(1, D, 2)])
    wqT = np.ascontiguousarray(np.asarray(inputs["L0_attn_a_wq"], np.float32).T[:, perm])
    wkT = np.ascontiguousarray(np.asarray(inputs["L0_attn_a_wk"], np.float32).T[:, perm])
    wvT = np.ascontiguousarray(np.asarray(inputs["L0_attn_a_wv"], np.float32).T)
    cos_fm, sin_fm = _host_tables()
    colsify = lambda v: np.ascontiguousarray(np.asarray(v, np.float32).reshape(8, 128).T)
    common = dict(
        wqT=wqT, wkT=wkT, wvT=wvT, costab=cos_fm, sintab=sin_fm,
        mxw=np.ascontiguousarray(np.asarray(inputs["L0_ffn_weight"], np.float32)),
        mxb=np.asarray(inputs["L0_ffn_bias"], np.float32).reshape(1, D),
        bwq=np.ascontiguousarray(np.asarray(inputs["L0_attn_b_wq"], np.float32)),
        bwk=np.ascontiguousarray(np.asarray(inputs["L0_attn_b_wk"], np.float32)),
        bwv=np.ascontiguousarray(np.asarray(inputs["L0_attn_b_wv"], np.float32)),
        bwo=np.ascontiguousarray(np.asarray(inputs["L0_attn_b_wo"], np.float32)),
        bwd=np.ascontiguousarray(np.asarray(inputs["L0_attn_b_w_d"], np.float32)),
        bff1=np.ascontiguousarray(np.asarray(inputs["L0_attn_b_w_ff1"], np.float32)),
        bff2=np.ascontiguousarray(np.asarray(inputs["L0_attn_b_w_ff2"], np.float32)),
        g1c=colsify(inputs["L0_attn_b_ln1_g"]), b1c=colsify(inputs["L0_attn_b_ln1_b"]),
        g2c=colsify(inputs["L0_attn_b_ln2_g"]), b2c=colsify(inputs["L0_attn_b_ln2_b"]),
        bo_r=np.asarray(inputs["L0_attn_b_bo"], np.float32).reshape(1, D),
        bf1_r=np.asarray(inputs["L0_attn_b_b_ff1"], np.float32).reshape(1, 4 * D),
        bf2_r=np.asarray(inputs["L0_attn_b_b_ff2"], np.float32).reshape(1, D),
        bd_r=np.asarray(inputs["L0_attn_b_b_d"], np.float32).reshape(1, D),
        ident=np.eye(128, dtype=np.float32),
        ones_r=np.ones((1, 1024), np.float32),
        ones_c=np.ones((128, 1), np.float32),
    )
    masks = [_p1_masks(0), _p1_masks(1)]
    cos_own = [_own_slices(cos_fm, 0), _own_slices(cos_fm, 1)]
    sin_own = [_own_slices(sin_fm, 0), _own_slices(sin_fm, 1)]
    in_maps = []
    for c in range(8):
        b, h = c >> 1, c & 1
        m = dict(common)
        m["x0"] = np.ascontiguousarray(x[b])
        m["x0own"] = np.ascontiguousarray(
            x[b].reshape(NT_ALL, 128, D)[h::2].reshape(NT_OWN * 128, D))
        m["p1mask"] = masks[h]
        m["costab_own"] = cos_own[h]
        m["sintab_own"] = sin_own[h]
        in_maps.append(m)

    def assemble(results):
        out = np.empty((4, NSEQ, D), np.float32)
        for c in range(8):
            b, h = c >> 1, c & 1
            o = np.asarray(results[c]["out"], np.float32)
            for s in range(NT_OWN):
                t = 2 * s + h
                out[b, t * 128:(t + 1) * 128, :] = o[s * 128:(s + 1) * 128, :]
        return out
    return in_maps, assemble


def build_program():
    return Builder(PHASES, DBG).build()


def kernel(**inputs):
    in_maps, assemble = prepare_inputs(inputs)
    nc = build_program()
    res = run_bass_kernel_spmd(nc, in_maps, core_ids=list(range(8)))
    return assemble(res.results)
